# version 01
# baseline (speedup 1.0000x reference)
import numpy as np
import ml_dtypes
import concourse.bass as bass
import concourse.mybir as mybir
from concourse.bass_utils import run_bass_kernel_spmd

F32 = mybir.dt.float32
BF16 = mybir.dt.bfloat16
AF = mybir.ActivationFunctionType
ALU = mybir.AluOpType
AX = mybir.AxisListType

B, S, D, H, DH = 4, 2048, 4096, 32, 128
NCORE = 8
RTOK = 1024
NM = 8
NK = 32
CLIP = 8.0
EPS = 1e-5
SCALE = DH ** -0.5
PAIRS = [[2 * i, 2 * i + 1] for i in range(4)]

SCHED = []
for g in range(4):
    SCHED += [("k", 2 * g), ("k", 2 * g + 1), ("v", 2 * g), ("v", 2 * g + 1)]
SCHED += [("q", a) for a in range(8)]
QK_ORD, V_ORD, QK_SCHED, V_SCHED = {}, {}, [], []
for _i, (_kind, _a) in enumerate(SCHED):
    if _kind == "v":
        V_ORD[_i] = len(V_SCHED); V_SCHED.append(_i)
    else:
        QK_ORD[_i] = len(QK_SCHED); QK_SCHED.append(_i)

LS = [2 * s + 2 for s in range(8)]
TSTART = [sum(LS[:s]) for s in range(8)]
NDTAB = sum(LS)
NCH = 24 * NM
P0_GRPS = NM * 8
P1_GRPS = len(QK_SCHED) * NM


import math as _math
CHW = []
for _s in range(8):
    w = LS[_s] * 128
    CHW.append([min(512, w - c * 512) for c in range((w + 511) // 512)])
preC = [0]
for _s in range(8):
    preC.append(preC[-1] + len(CHW[_s]))
CPH = preC[-1]


def GROUPS(s):
    return [(t0, min(4, LS[s] - t0)) for t0 in range(0, LS[s], 4)]


GPH = sum(len(GROUPS(_s)) for _s in range(8))
preG = [0]
for _s in range(8):
    preG.append(preG[-1] + len(GROUPS(_s)))


def GG_END(sg):
    h, s = divmod(sg, 8)
    return GPH * h + preG[s + 1]


SLOPES = [2.0 ** (-0.25 * (hh + 1)) for hh in range(H)]
MMB = 12


class SelfSem:
    def __init__(self, eng, semh):
        self.eng, self.sem, self.n = eng, semh, 0

    def edge(self, instr):
        instr.then_inc(self.sem, 1)
        self.n += 1
        self.eng.wait_ge(self.sem, self.n)


def build(debug=True):
    nc = bass.Bass("TRN2", target_bir_lowering=False, num_devices=NCORE)

    x_in = nc.dram_tensor("x", [RTOK, D], F32, kind="ExternalInput").ap()
    wqkv = nc.dram_tensor("wqkv", [24, NK, 128, 512], F32, kind="ExternalInput").ap()
    lnp = nc.dram_tensor("lnp", [128, 4 * 32], F32, kind="ExternalInput").ap()
    ident_in = nc.dram_tensor("ident", [128, 128], BF16, kind="ExternalInput").ap()
    identf_in = nc.dram_tensor("identf", [128, 128], F32, kind="ExternalInput").ap()
    out = nc.dram_tensor("out", [RTOK, D], F32, kind="ExternalOutput").ap()

    cin = [nc.dram_tensor(f"cin{g}", [2048, 1024], BF16, kind="Internal").ap() for g in range(4)]
    cout = [nc.dram_tensor(f"cout{g}", [4096, 1024], BF16, kind="Internal").ap() for g in range(4)]
    sin = nc.dram_tensor("sin", [16, 128], F32, kind="Internal").ap()
    sout = nc.dram_tensor("sout", [32, 128], F32, kind="Internal").ap()
    qT_d = nc.dram_tensor("qT_d", [D, RTOK], BF16, kind="Internal").ap()
    qs_d = nc.dram_tensor("qs_d", [32, 128], F32, kind="Internal").ap()
    aT_d = nc.dram_tensor("aT_d", [D, RTOK], BF16, kind="Internal").ap()
    zb_d = nc.dram_tensor("zb_d", [8, 128], BF16, kind="Internal").ap()
    dtab_in = nc.dram_tensor("dtab", [NDTAB, 128, 128], F32, kind="ExternalInput").ap()
    wout_in = nc.dram_tensor("wout", [16, NK, 128, 256], F32, kind="ExternalInput").ap()
    wbase_in = nc.dram_tensor("wbase", [16, NK, 128, 256], F32, kind="ExternalInput").ap()
    wa_in = nc.dram_tensor("wa", [NK, 128, 128], F32, kind="ExternalInput").ap()
    wb_in = nc.dram_tensor("wb", [16, 128, 256], F32, kind="ExternalInput").ap()
    wr_in = nc.dram_tensor("wr", [NK, 128, 8], F32, kind="ExternalInput").ap()
    bb_in = nc.dram_tensor("bb", [1, D], F32, kind="ExternalInput").ap()

    dbg = {}
    if debug:
        dbg["d_qT"] = nc.dram_tensor("d_qT", [D, RTOK], BF16, kind="ExternalOutput").ap()
        dbg["d_kv"] = nc.dram_tensor("d_kv", [4, 4096, 1024], BF16, kind="ExternalOutput").ap()
        dbg["d_st"] = nc.dram_tensor("d_st", [32, 128], F32, kind="ExternalOutput").ap()
        dbg["d_xT"] = nc.dram_tensor("d_xT", [128, NK * RTOK], BF16, kind="ExternalOutput").ap()
        dbg["d_aT"] = nc.dram_tensor("d_aT", [D, RTOK], BF16, kind="ExternalOutput").ap()

    ctx = []

    def sb(name, shape, dtype):
        g = nc.sbuf_tensor(name, shape, dtype); t = g.__enter__(); ctx.append(g); return t

    def psum(name, shape, dtype):
        g = nc.psum_tensor(name, shape, dtype); t = g.__enter__(); ctx.append(g); return t

    def sem(name):
        g = nc.semaphore(name); s = g.__enter__(); ctx.append(g); return s

    id_bf = sb("id_bf", [128, 128], BF16)
    id_f = sb("id_f", [128, 128], F32)
    lnp_sb = sb("lnp_sb", [128, 4 * 32], F32)
    eps_sb = sb("eps_sb", [128, 1], F32)

    ctx_p1_base = len(ctx)
    xhatT = sb("xhatT", [128, NK * RTOK], BF16)
    wbuf = [sb(f"wbuf{j}", [128, NK * 512], BF16) for j in range(2)]
    neg8 = sb("neg8", [128, 512], F32)
    xld = [sb("xld0", [128, D], F32)] * 2
    xhat = [sb("xhat0", [128, D], BF16)] * 2
    sqj = sb("sqj", [128, D], BF16)
    qring = [sb(f"qr{j}", [128, 512], BF16) for j in range(4)]
    tring = [sb(f"tr{j}", [128, 4096], BF16) for j in range(2)]
    vring = [sb(f"vr{j}", [128, 4096], BF16) for j in range(2)]
    psums = sb("psums", [128, 128], F32)
    psumsq = sb("psumsq", [128, 128], F32)
    stA = sb("stA", [128, 8], F32)
    stB = sb("stB", [128, 8], F32)
    stM = sb("stM", [128, 8], F32)
    stV = sb("stV", [128, 8], F32)
    allst = sb("allst", [128, 32], F32)
    ln1st = sb("ln1st", [128, 2], F32)
    st32 = sb("st32", [32, 128], F32)

    pacc = [psum(f"pacc{j}", [128, 512], F32) for j in range(3)]
    ptr = [psum(f"ptr{j}", [128, 1024], BF16) for j in range(2)]
    pst = psum("pst", [128, 128], F32)
    ctx_p1_end = len(ctx)

    s_x = sem("s_x")
    s_x2 = sem("s_x2")
    s_w = [sem("s_w0"), sem("s_w1")]
    s_mm = sem("s_mm")
    s_bk = sem("s_bk")
    s_tp = sem("s_tp")
    s_ev = sem("s_ev")
    s_tsp = [sem("s_tsp0"), sem("s_tsp1")]
    s_vsp = [sem("s_vsp0"), sem("s_vsp1")]
    s_ssp = sem("s_ssp")
    s_ccg = [sem(f"s_ccg{g}") for g in range(4)]
    s_ccs = sem("s_ccs")
    s_vd = sem("s_vd")
    s_ad = sem("s_ad")
    s_st = sem("s_st")
    s_dbg = sem("s_dbg")
    s_vself = sem("s_vself")
    s_aself = sem("s_aself")

    STATS_TP = P0_GRPS + P1_GRPS

    with nc.Block() as block:

        @block.sync
        def _(sync):
            sync.dma_start(out=id_bf[:], in_=ident_in[:, :]).then_inc(s_x, 16)
            sync.dma_start(out=id_f[:], in_=identf_in[:, :]).then_inc(s_x, 16)
            sync.dma_start(out=lnp_sb[:], in_=lnp[:, :]).then_inc(s_x, 16)
            for m in range(NM):
                if m >= 1:
                    sync.wait_ge(s_ad, 2 * (m - 1) + 2)
                sync.dma_start(out=xld[m % 2][:], in_=x_in[m * 128:(m + 1) * 128, :]).then_inc(s_x2, 16)
            for i, (kind, a) in enumerate(SCHED):
                if kind == "v":
                    sync.wait_ge(s_bk, i * NM + NM)
                    par = V_ORD[i] % 2
                    g = a // 2
                    hl0 = (a % 2) * 4
                    flat = cin[g].rearrange("r c -> (r c)")
                    vsrc = vring[par].ap().rearrange("p (m hl d) -> p m hl d", m=NM, hl=4, d=128)
                    for hl in range(4):
                        base = 1024 * 1024 + (hl0 + hl) * 131072
                        dst3 = flat[base: base + 131072].rearrange("(m p d) -> p m d", m=NM, p=128, d=128)
                        sync.dma_start(out=dst3, in_=vsrc[:, :, hl, :]).then_inc(s_vsp[V_ORD[i] % 2], 16)
                else:
                    sync.wait_ge(s_ev, P0_GRPS + NM * (QK_ORD[i] + 1))
                    par = QK_ORD[i] % 2
                    if kind == "k":
                        g = a // 2
                        hl0 = (a % 2) * 4
                        dst = cin[g][hl0 * 128:(hl0 + 4) * 128, :]
                    else:
                        dst = qT_d[a * 512:(a + 1) * 512, :]
                    dst3 = dst.rearrange("(hl d) t -> d hl t", hl=4, d=128)
                    src3 = tring[par].ap().rearrange("p (hl t) -> p hl t", hl=4, t=1024)
                    sync.dma_start(out=dst3, in_=src3).then_inc(s_tsp[QK_ORD[i] % 2], 16)
            sync.wait_ge(s_vd, 23)
            sync.dma_start(out=qs_d[:, :], in_=st32[:, :]).then_inc(s_ssp, 16)
            sync.dma_start(out=sin[:, :], in_=st32[16:32, :]).then_inc(s_ssp, 16)

            if debug:
                for g in range(4):
                    sync.wait_ge(s_ccg[g], 1)
                sync.wait_ge(s_ccs, 1)
                sync.wait_ge(s_tsp[0], 16 * 8)
                sync.wait_ge(s_tsp[1], 16 * 8)
                sync.dma_start(out=dbg["d_qT"][:, :], in_=qT_d[:, :]).then_inc(s_dbg, 16)
                for g in range(4):
                    sync.dma_start(out=dbg["d_kv"][g], in_=cout[g][:, :]).then_inc(s_dbg, 16)
                sync.dma_start(out=dbg["d_st"][:, :], in_=sout[:, :]).then_inc(s_dbg, 16)
                sync.dma_start(out=dbg["d_xT"][:, :], in_=xhatT[:]).then_inc(s_dbg, 16)
                sync.wait_ge(s_dbg, 16 * 7)

        @block.gpsimd
        def _(gpsimd):
            for nn in range(24):
                if nn >= 2:
                    gpsimd.wait_ge(s_mm, (nn - 1) * NM)
                gpsimd.dma_start(
                    out=wbuf[nn % 2].ap().rearrange("p (k n) -> p k n", k=NK, n=512),
                    in_=wqkv[nn].rearrange("k p n -> p k n"),
                ).then_inc(s_w[nn % 2], 16)
            for g in range(4):
                gpsimd.wait_ge(s_tsp[0], 16 * (g + 1))
                gpsimd.wait_ge(s_tsp[1], 16 * (g + 1))
                gpsimd.wait_ge(s_vsp[0], 64 * (g + 1))
                gpsimd.wait_ge(s_vsp[1], 64 * (g + 1))
                gpsimd.collective_compute(
                    "AllGather", ALU.bypass, replica_groups=PAIRS,
                    ins=[cin[g][:, :]], outs=[cout[g][:, :]],
                ).then_inc(s_ccg[g], 1)
            gpsimd.wait_ge(s_ssp, 32)
            gpsimd.collective_compute(
                "AllGather", ALU.bypass, replica_groups=PAIRS,
                ins=[sin[:, :]], outs=[sout[:, :]],
            ).then_inc(s_ccs, 1)

        @block.tensor
        def _(tensor):
            tensor.wait_ge(s_x, 48)
            for m in range(NM):
                tensor.wait_ge(s_ad, 2 * m + 2)
                for kg in range(8):
                    gi = m * 8 + kg
                    if gi >= 2:
                        tensor.wait_ge(s_ev, gi - 1)
                    tb = ptr[gi % 2]
                    for j in range(4):
                        k = kg * 4 + j
                        ins = tensor.transpose(tb[:, j * 128:(j + 1) * 128],
                                               xhat[m % 2][:, k * 128:(k + 1) * 128],
                                               id_bf[:, :])
                    ins.then_inc(s_tp, 1)

            def emit_qk_transposes(j):
                jnn, jm = divmod(j, NM)
                if SCHED[jnn][0] == "v":
                    return
                gidx = P0_GRPS + QK_ORD[jnn] * NM + jm
                tensor.wait_ge(s_bk, j + 1)
                if gidx >= 2:
                    tensor.wait_ge(s_ev, gidx - 1)
                tb = ptr[gidx % 2]
                for t in range(4):
                    ins = tensor.transpose(tb[:, t * 128:(t + 1) * 128],
                                           qring[j % 4][:, t * 128:(t + 1) * 128],
                                           id_bf[:, :])
                ins.then_inc(s_tp, 1)

            for i in range(NCH):
                nn, m = divmod(i, NM)
                tensor.wait_ge(s_w[nn % 2], 16 * (nn // 2 + 1))
                tensor.wait_ge(s_ev, 8 * (m + 1))
                if i >= 3:
                    tensor.wait_ge(s_bk, i - 2)
                bk = pacc[i % 3]
                for k in range(NK):
                    ins = tensor.matmul(bk[:, :],
                                        xhatT[:, k * RTOK + m * 128: k * RTOK + m * 128 + 128],
                                        wbuf[nn % 2][:, k * 512:(k + 1) * 512],
                                        start=(k == 0), stop=(k == NK - 1))
                ins.then_inc(s_mm, 1)
                if i >= 1:
                    emit_qk_transposes(i - 1)
            emit_qk_transposes(NCH - 1)

            tensor.wait_ge(s_ad, 22)
            tensor.transpose(pst[0:32, :], allst[:, 0:32], id_f[:, :]).then_inc(s_tp, 1)

        @block.vector
        def _(vector):
            vs = SelfSem(vector, s_vself)
            ins = vector.memset(neg8[:], -CLIP)
            vs.edge(ins)
            ins = vector.memset(eps_sb[:], EPS)
            vs.edge(ins)
            for m in range(NM):
                vector.wait_ge(s_x2, 16 * (m + 1))
                xm = xld[m % 2]
                vector.reduce_sum(stA[:, m:m + 1], xm[:], axis=AX.X)
                vector.scalar_tensor_tensor(sqj[:], xm[:], 1.0, xm[:], ALU.bypass,
                                            ALU.mult, accum_out=stB[:, m:m + 1]).then_inc(s_vd, 1)
                vector.wait_ge(s_ad, 2 * m + 1)
                vector.scalar_tensor_tensor(ln1st[:, 0:1], stB[:, m:m + 1], 1.0 / D,
                                            stV[:, m:m + 1], ALU.mult, ALU.subtract).then_inc(s_vd, 1)
            ins = vector.nop()
            vs.edge(ins)
            for i in range(NCH):
                nn, m = divmod(i, NM)
                kind, a = SCHED[nn]
                vector.wait_ge(s_mm, i + 1)
                if kind == "v":
                    par = V_ORD[nn] % 2
                    if V_ORD[nn] >= 2 and m == 0:
                        vector.wait_ge(s_vsp[V_ORD[nn] % 2], 64 * (V_ORD[nn] // 2))
                    vector.scalar_tensor_tensor(vring[par][:, m * 512:(m + 1) * 512],
                                                pacc[i % 3][:, :], CLIP, neg8[:],
                                                ALU.min, ALU.max).then_inc(s_bk, 1)
                else:
                    if i >= 4:
                        j = i - 4
                        jnn = j // NM
                        if SCHED[jnn][0] != "v":
                            vector.wait_ge(s_tp, P0_GRPS + QK_ORD[jnn] * NM + (j % NM) + 1)
                    col = (0 if kind == "q" else 64) + a * 8 + m
                    vector.scalar_tensor_tensor(qring[i % 4][:, :], pacc[i % 3][:, :],
                                                CLIP, neg8[:], ALU.min, ALU.max,
                                                accum_out=psums[:, col:col + 1]).then_inc(s_bk, 1)
                j = i - 1
                if j >= 0 and SCHED[j // NM][0] != "v":
                    jnn, jm = divmod(j, NM)
                    jcol = (0 if SCHED[jnn][0] == "q" else 64) + SCHED[jnn][1] * 8 + jm
                    vector.wait_ge(s_bk, j + 1)
                    vector.scalar_tensor_tensor(sqj[:, (jm % 8) * 512:(jm % 8) * 512 + 512], qring[j % 4][:, :], 1.0,
                                                qring[j % 4][:, :], ALU.bypass, ALU.mult,
                                                accum_out=psumsq[:, jcol:jcol + 1])
            j = NCH - 1
            if SCHED[j // NM][0] != "v":
                jnn, jm = divmod(j, NM)
                jcol = (0 if SCHED[jnn][0] == "q" else 64) + SCHED[jnn][1] * 8 + jm
                vector.wait_ge(s_bk, j + 1)
                ins = vector.scalar_tensor_tensor(sqj[:, (jm % 8) * 512:(jm % 8) * 512 + 512], qring[j % 4][:, :], 1.0,
                                                  qring[j % 4][:, :], ALU.bypass, ALU.mult,
                                                  accum_out=psumsq[:, jcol:jcol + 1])
            else:
                ins = vector.nop()
            vs.edge(ins)

            for qi, off in ((0, 0), (1, 64)):
                if qi == 1:
                    vector.wait_ge(s_ad, 16 + 3)
                for m in range(NM):
                    vector.reduce_sum(stA[:, m:m + 1], psums[:, off + m: off + 64: 8], axis=AX.X)
                    ins = vector.reduce_sum(stB[:, m:m + 1], psumsq[:, off + m: off + 64: 8], axis=AX.X)
                ins.then_inc(s_vd, 1)
                vector.wait_ge(s_ad, 16 + 3 * qi + 1)
                vector.scalar_tensor_tensor(stV[:, 0:8], stB[:, 0:8], 1.0 / D,
                                            stV[:, 0:8], ALU.mult, ALU.subtract).then_inc(s_vd, 1)
                vector.wait_ge(s_ad, 16 + 3 * qi + 2)
                rc = slice(0, 8) if qi == 0 else slice(16, 24)
                vector.tensor_tensor(stB[:, 0:8], stM[:, 0:8], allst[:, rc],
                                     ALU.mult).then_inc(s_vd, 1)
            vector.wait_ge(s_tp, STATS_TP + 1)
            vector.tensor_copy(st32[:, :], pst[0:32, :]).then_inc(s_vd, 23 - 22)

        @block.scalar
        def _(scalar):
            asf = SelfSem(scalar, s_aself)
            for m in range(NM):
                scalar.wait_ge(s_vd, 2 * m + 1)
                ins = scalar.mul(stM[:, m:m + 1], stA[:, m:m + 1], 1.0 / D)
                asf.edge(ins)
                scalar.activation(stV[:, m:m + 1], stM[:, m:m + 1], AF.Square).then_inc(s_ad, 1)
                scalar.wait_ge(s_vd, 2 * m + 2)
                ins = scalar.activation(ln1st[:, 1:2], ln1st[:, 0:1], AF.Ln, bias=eps_sb[:, 0:1])
                asf.edge(ins)
                ins = scalar.activation(ln1st[:, 1:2], ln1st[:, 1:2], AF.Exp, scale=-0.5)
                asf.edge(ins)
                ins = scalar.activation(ln1st[:, 0:1], stM[:, m:m + 1], AF.Identity, scale=ln1st[:, 1:2])
                asf.edge(ins)
                ins = scalar.mul(ln1st[:, 0:1], ln1st[:, 0:1], -1.0)
                asf.edge(ins)
                if m >= 1:
                    for kg in range(8):
                        gi = (m - 1) * 8 + kg
                        scalar.wait_ge(s_tp, gi + 1)
                        k0 = kg * 4
                        dst = xhatT.ap().rearrange("p (k t) -> p k t", k=NK, t=RTOK)[
                            :, k0:k0 + 4, (m - 1) * 128:m * 128]
                        src = ptr[gi % 2][:, 0:512].rearrange("p (k t) -> p k t", k=4, t=128)
                        scalar.copy(dst, src).then_inc(s_ev, 1)
                    scalar.wait_ge(s_tp, 8 * m)
                scalar.activation(xhat[m % 2][:], xld[m % 2][:], AF.Identity,
                                  scale=ln1st[:, 1:2], bias=ln1st[:, 0:1]).then_inc(s_ad, 1)
                if m == NM - 1:
                    for kg in range(8):
                        gi = m * 8 + kg
                        scalar.wait_ge(s_tp, gi + 1)
                        k0 = kg * 4
                        dst = xhatT.ap().rearrange("p (k t) -> p k t", k=NK, t=RTOK)[
                            :, k0:k0 + 4, m * 128:(m + 1) * 128]
                        src = ptr[gi % 2][:, 0:512].rearrange("p (k t) -> p k t", k=4, t=128)
                        scalar.copy(dst, src).then_inc(s_ev, 1)
            for i in range(NCH):
                nn, m = divmod(i, NM)
                kind, a = SCHED[nn]
                if kind == "v":
                    continue
                qko = QK_ORD[nn]
                gidx = P0_GRPS + qko * NM + m
                scalar.wait_ge(s_tp, gidx + 1)
                par = qko % 2
                if qko >= 2 and m == 0:
                    scalar.wait_ge(s_tsp[qko % 2], 16 * (qko // 2))
                hl_all = tring[par].ap().rearrange("p (hl t) -> p hl t", hl=4, t=1024)
                dst = hl_all[:, :, m * 128:(m + 1) * 128]
                src = ptr[gidx % 2][:, 0:512].rearrange("p (hl t) -> p hl t", hl=4, t=128)
                scalar.copy(dst, src).then_inc(s_ev, 1)
            for qi in range(2):
                scalar.wait_ge(s_vd, 16 + 3 * qi + 1)
                ins = scalar.mul(stM[:, 0:8], stA[:, 0:8], 1.0 / D)
                asf.edge(ins)
                scalar.activation(stV[:, 0:8], stM[:, 0:8], AF.Square).then_inc(s_ad, 1)
                scalar.wait_ge(s_vd, 16 + 3 * qi + 2)
                rc = slice(0, 8) if qi == 0 else slice(16, 24)
                nbc = slice(8, 16) if qi == 0 else slice(24, 32)
                ins = scalar.activation(allst[:, rc], stV[:, 0:8], AF.Ln, bias=eps_sb[:, 0:1])
                asf.edge(ins)
                scalar.activation(allst[:, rc], allst[:, rc], AF.Exp,
                                  scale=-0.5).then_inc(s_ad, 1)
                scalar.wait_ge(s_vd, 16 + 3 * qi + 3)
                scalar.mul(allst[:, nbc], stB[:, 0:8], -1.0).then_inc(s_ad, 1)


        for g_ in reversed(ctx[ctx_p1_base:ctx_p1_end]):
            g_.__exit__(None, None, None)
        del ctx[ctx_p1_base:ctx_p1_end]

        ctx_p3_base = len(ctx)
        dtab_sb = sb("dtab_sb", [128, NDTAB * 128], F32)
        rrep_k = sb("rrep_k", [128, 2048], F32)
        nbrep_k = sb("nbrep_k", [128, 2048], F32)
        rrep_q = sb("rrep_q", [128, 1024], F32)
        nbrep_q = sb("nbrep_q", [128, 1024], F32)
        krow = sb("krow", [1, 2048], F32)
        knbrow = sb("knbrow", [1, 2048], F32)
        qrow = sb("qrow", [1, 1024], F32)
        qnbrow = sb("qnbrow", [1, 1024], F32)
        ones_f = sb("ones_f", [1, 128], F32)
        ones_b = sb("ones_b", [1, 128], BF16)
        qraw = [sb(f"qraw{j}", [128, 1024], BF16) for j in range(2)]
        kraw = [sb(f"kraw{j}", [128, 2048], BF16) for j in range(2)]
        v_sb = [sb(f"v_sb{j}", [128, 2048], BF16) for j in range(2)]
        scr = [sb(f"scr{j}", [128, 3072], F32) for j in range(2)]
        qhat = [sb(f"qhat{j}", [128, 1024], BF16) for j in range(2)]
        khat = [sb(f"khat{j}", [128, 2048], BF16) for j in range(2)]
        scores = [sb(f"scores{j}", [128, 2048], F32) for j in range(2)]
        probs = [sb(f"probs{j}", [128, 2048], BF16) for j in range(2)]
        probsT = [sb(f"probsT{j}", [128, 2048], BF16) for j in range(2)]
        negmax = [sb(f"negmax{j}", [128, 8], F32) for j in range(2)]
        Zcols = [sb(f"Zcols{j}", [128, 8], F32) for j in range(2)]
        recipZ = sb("recipZ", [128, 8], F32)
        recipZb = sb("recipZb", [128, 8], BF16)
        zb_sb = sb("zb_sb", [8, 128], BF16)
        zrow1 = [sb(f"zrow1{j}", [1, 1024], BF16) for j in range(2)]
        attnraw = [sb(f"attnraw{j}", [128, 1024], F32) for j in range(2)]
        reciprep = [sb(f"reciprep{j}", [128, 1024], F32) for j in range(2)]
        attnT_sb = [sb(f"attnT{j}", [128, 1024], BF16) for j in range(2)]

        psc = [psum(f"psc{j}", [128, 512], F32) for j in range(2)]
        ptp = [psum(f"ptp{j}", [128, 1024], BF16) for j in range(2)]
        av_ps = psum("av_ps", [128, 1024], F32)
        rp_ps = psum("rp_ps", [128, 1024], F32)
        ctx_p3_end = len(ctx)

        s3_pl = sem("s3_pl")
        s3_ld = [sem("s3_ld0"), sem("s3_ld1")]
        s3_mm3 = sem("s3_mm3")
        s3_bk3 = sem("s3_bk3")
        s3_tp3 = sem("s3_tp3")
        s3_te = sem("s3_te")
        s3_ex = sem("s3_ex")
        s3_mx = sem("s3_mx")
        s3_avs = sem("s3_avs")
        s3_ave = sem("s3_ave")
        s3_vd3 = sem("s3_vd3")
        s3_pd3 = sem("s3_pd3")
        s3_ak3 = sem("s3_ak3")
        s3_zr = sem("s3_zr")
        s3_zc = sem("s3_zc")
        s3_zt = sem("s3_zt")
        s3_ze = sem("s3_ze")
        s3_zd = sem("s3_zd")
        s3_zm = sem("s3_zm")
        s3_rp = sem("s3_rp")
        s3_nrm = sem("s3_nrm")
        s3_sp3 = [sem("s3_sp3a"), sem("s3_sp3b")]
        s_vs3 = sem("s_vs3")
        s_as3 = sem("s_as3")

        @block.sync
        def _(sync):
            sync.wait_ge(s_mm, NCH)
            sync.wait_ge(s_vd, 23)
            sync.wait_ge(s_ad, 22)
            sync.dma_start(out=dtab_sb.ap().rearrange("p (i j) -> p i j", i=NDTAB, j=128),
                           in_=dtab_in.rearrange("i p j -> p i j")).then_inc(s3_pl, 16)
            sync.wait_ge(s_ccs, 1)
            srows = sout.rearrange("(r t u) p -> t u r p", r=2, t=2, u=8)
            for ri in range(2):
                sync.dma_start(out=krow.ap().rearrange("o (u r p) -> o u r p", u=8, r=2, p=128)[:, :, ri, :],
                               in_=srows[0:1, :, ri, :]).then_inc(s3_pl, 16)
                sync.dma_start(out=knbrow.ap().rearrange("o (u r p) -> o u r p", u=8, r=2, p=128)[:, :, ri, :],
                               in_=srows[1:2, :, ri, :]).then_inc(s3_pl, 16)
            qsf = qs_d.rearrange("r p -> (r p)")
            sync.dma_start(out=qrow[0:1, :], in_=qsf[0:1024].rearrange("(o c) -> o c", o=1)).then_inc(s3_pl, 16)
            sync.dma_start(out=qnbrow[0:1, :], in_=qsf[1024:2048].rearrange("(o c) -> o c", o=1)).then_inc(s3_pl, 16)
            sync.wait_ge(s_tsp[0], 16 * 8)
            sync.wait_ge(s_tsp[1], 16 * 8)
            for h in range(H):
                g = h // 8
                hl = h % 8
                if h >= 2:
                    sync.wait_ge(s3_vd3, 2 * (h - 2) + 2)
                    sync.wait_ge(s3_avs, 8 * (h - 1))
                sync.wait_ge(s_ccg[g], 1)
                sync.dma_start(out=qraw[h % 2][:], in_=qT_d[h * 128:(h + 1) * 128, :]).then_inc(s3_ld[h % 2], 16)
                kdst = kraw[h % 2].ap().rearrange("p (u r j) -> p u r j", u=8, r=2, j=128)
                vdst = v_sb[h % 2].ap().rearrange("p (u r d) -> p u r d", u=8, r=2, d=128)
                cfl = cout[g].rearrange("r c -> (r c)")
                for r in range(2):
                    ksrc = cout[g][r * 2048 + hl * 128: r * 2048 + (hl + 1) * 128, :]
                    sync.dma_start(out=kdst[:, :, r, :],
                                   in_=ksrc.rearrange("d (u j) -> d u j", u=8, j=128)).then_inc(s3_ld[h % 2], 16)
                    vbase = r * 2097152 + 1048576 + hl * 131072
                    vsrc = cfl[vbase: vbase + 131072].rearrange("(u p d) -> p u d", u=8, p=128, d=128)
                    sync.dma_start(out=vdst[:, :, r, :], in_=vsrc).then_inc(s3_ld[h % 2], 16)
                if h >= 1:
                    j = h - 1
                    sync.wait_ge(s3_ze, j + 1)
                    if j >= 1:
                        sync.wait_ge(s3_zd, 32 * j)
                    sync.dma_start(out=zb_d[:, :], in_=zb_sb[:, :]).then_inc(s3_zd, 16)
                    sync.wait_ge(s3_zd, 32 * j + 16)
                    if j >= 2:
                        sync.wait_ge(s3_zm, j - 1)
                    sync.dma_start(out=zrow1[j % 2][0:1, :],
                                   in_=zb_d.rearrange("r p -> (r p)").rearrange("(o c) -> o c", o=1)).then_inc(s3_zd, 16)
                if h >= 2:
                    j = h - 2
                    sync.wait_ge(s3_nrm, j + 1)
                    sync.dma_start(out=aT_d[j * 128:(j + 1) * 128, :], in_=attnT_sb[j % 2][:]).then_inc(s3_sp3[j % 2], 16)
            j = H - 1
            sync.wait_ge(s3_ze, j + 1)
            sync.wait_ge(s3_zd, 32 * j)
            sync.dma_start(out=zb_d[:, :], in_=zb_sb[:, :]).then_inc(s3_zd, 16)
            sync.wait_ge(s3_zd, 32 * j + 16)
            sync.wait_ge(s3_zm, j - 1)
            sync.dma_start(out=zrow1[j % 2][0:1, :],
                           in_=zb_d.rearrange("r p -> (r p)").rearrange("(o c) -> o c", o=1)).then_inc(s3_zd, 16)
            for j in (H - 2, H - 1):
                sync.wait_ge(s3_nrm, j + 1)
                sync.dma_start(out=aT_d[j * 128:(j + 1) * 128, :], in_=attnT_sb[j % 2][:]).then_inc(s3_sp3[j % 2], 16)
            if debug:
                sync.wait_ge(s3_sp3[0], 16 * (H // 2))
                sync.wait_ge(s3_sp3[1], 16 * (H // 2))
                sync.dma_start(out=dbg["d_aT"][:, :], in_=aT_d[:, :]).then_inc(s_dbg, 16)
                sync.wait_ge(s_dbg, 16 * 8)

        @block.tensor
        def _(tensor):
            tensor.wait_ge(s3_pl, 112)
            tensor.wait_ge(s_vd, 24)
            mmi = 0
            for row_sb, width in ((krow, 2048), (knbrow, 2048), (qrow, 1024), (qnbrow, 1024)):
                for c in range(width // 512):
                    if mmi >= 2:
                        tensor.wait_ge(s3_bk3, mmi - 1)
                    tensor.matmul(psc[mmi % 2][:, 0:512], ones_f[0:1, :],
                                  row_sb[0:1, c * 512:(c + 1) * 512],
                                  start=True, stop=True).then_inc(s3_mm3, 1)
                    mmi += 1
            assert mmi == MMB

            for h in range(H):
                if h >= 1:
                    jz = h - 1
                    tensor.wait_ge(s3_zc, jz + 1)
                    tensor.wait_ge(s3_te, GPH * h - 1)
                    tensor.wait_ge(s3_ze, jz)
                    tensor.transpose(ptp[0][0:8, 0:128], recipZb[:, 0:8], id_bf[:, :]).then_inc(s3_zt, 1)
                tensor.wait_ge(s3_ak3, 2 * h + 2)
                if h >= 1:
                    tensor.wait_ge(s3_ave, h)

                def emit_scores(s):
                    sg = 8 * h + s
                    off = 0
                    for c, w in enumerate(CHW[s]):
                        pidx = MMB + CPH * h + preC[s] + c
                        if pidx >= 2:
                            tensor.wait_ge(s3_bk3, pidx - 1)
                        tensor.matmul(psc[pidx % 2][:, 0:w],
                                      qhat[h % 2][:, s * 128:(s + 1) * 128],
                                      khat[h % 2][:, off:off + w],
                                      start=True, stop=True).then_inc(s3_mm3, 1)
                        off += w

                def emit_tpav(s):
                    sg = 8 * h + s
                    tensor.wait_ge(s3_ex, sg + 1)
                    if s == 0:
                        tensor.wait_ge(s3_ze, h)
                    for gi, (t0, nt) in enumerate(GROUPS(s)):
                        gg = GPH * h + preG[s] + gi
                        if gg >= 2:
                            tensor.wait_ge(s3_te, gg - 1)
                        tb = ptp[gg % 2]
                        for t in range(t0, t0 + nt):
                            ins = tensor.transpose(tb[:, (t - t0) * 128:(t - t0 + 1) * 128],
                                                   probs[sg % 2][:, t * 128:(t + 1) * 128],
                                                   id_bf[:, :])
                        ins.then_inc(s3_tp3, 1)
                        tensor.wait_ge(s3_te, gg + 1)
                        for t in range(t0, t0 + nt):
                            ins = tensor.matmul(av_ps[:, s * 128:(s + 1) * 128],
                                                v_sb[h % 2][:, t * 128:(t + 1) * 128],
                                                probsT[sg % 2][:, t * 128:(t + 1) * 128],
                                                start=(s % 4 == 0 and t == 0),
                                                stop=(s % 4 == 3 and t == LS[s] - 1))
                        if t0 + nt == LS[s]:
                            ins.then_inc(s3_avs, 1)

                emit_scores(0)
                emit_scores(1)
                if h >= 2:
                    jz = h - 2
                    tensor.wait_ge(s3_zd, 32 * jz + 32)
                    tensor.wait_ge(s3_rp, jz)
                    for s8 in range(8):
                        ins = tensor.matmul(rp_ps[:, s8 * 128:(s8 + 1) * 128], ones_b[0:1, :],
                                            zrow1[jz % 2][0:1, s8 * 128:(s8 + 1) * 128],
                                            start=(s8 % 4 == 0), stop=(s8 % 4 == 3))
                    ins.then_inc(s3_zm, 1)
                for s in range(8):
                    if s + 2 < 8:
                        emit_scores(s + 2)
                    emit_tpav(s)
            jz = H - 1
            tensor.wait_ge(s3_zc, jz + 1)
            tensor.wait_ge(s3_te, GPH * H - 1)
            tensor.wait_ge(s3_ze, jz)
            tensor.transpose(ptp[0][0:8, 0:128], recipZb[:, 0:8], id_bf[:, :]).then_inc(s3_zt, 1)
            for jz in (H - 2, H - 1):
                tensor.wait_ge(s3_zd, 32 * jz + 32)
                tensor.wait_ge(s3_rp, jz)
                for s8 in range(8):
                    ins = tensor.matmul(rp_ps[:, s8 * 128:(s8 + 1) * 128], ones_b[0:1, :],
                                        zrow1[jz % 2][0:1, s8 * 128:(s8 + 1) * 128],
                                        start=(s8 % 4 == 0), stop=(s8 % 4 == 3))
                ins.then_inc(s3_zm, 1)

        @block.vector
        def _(vector):
            vs3 = SelfSem(vector, s_vs3)
            vector.memset(ones_f[:, :], 1.0)
            vector.memset(ones_b[:, :], 1.0).then_inc(s_vd, 1)
            dsts = ([(rrep_k, c) for c in range(4)] + [(nbrep_k, c) for c in range(4)]
                    + [(rrep_q, c) for c in range(2)] + [(nbrep_q, c) for c in range(2)])
            for mmi, (dst, c) in enumerate(dsts):
                vector.wait_ge(s3_mm3, mmi + 1)
                vector.tensor_copy(dst.ap()[:, c * 512:(c + 1) * 512], psc[mmi % 2][:, 0:512]).then_inc(s3_bk3, 1)

            def emit_tt(hh):
                vector.wait_ge(s3_ld[hh % 2], 80 * (hh // 2 + 1))
                if hh >= 2:
                    vector.wait_ge(s3_ak3, 2 * (hh - 2) + 2)
                vector.tensor_tensor(scr[hh % 2][:, 0:1024], qraw[hh % 2][:], rrep_q[:],
                                     ALU.mult).then_inc(s3_vd3, 1)
                vector.tensor_tensor(scr[hh % 2][:, 1024:3072], kraw[hh % 2][:], rrep_k[:],
                                     ALU.mult).then_inc(s3_vd3, 1)

            emit_tt(0)
            for h in range(H):
                if h >= 1:
                    j = h - 1
                    vector.wait_ge(s3_avs, 8 * (j + 1))
                    if j >= 2:
                        vector.wait_ge(s3_nrm, j - 1)
                    vector.tensor_copy(attnraw[j % 2][:], av_ps[:, 0:1024]).then_inc(s3_ave, 1)
                    vector.wait_ge(s3_zt, j + 1)
                    if j >= 1:
                        vector.wait_ge(s3_zd, 32 * (j - 1) + 16)
                    vector.tensor_copy(zb_sb[0:8, :], ptp[0][0:8, 0:128]).then_inc(s3_ze, 1)
                def emit_chunks_max(s):
                    sg = 8 * h + s
                    if sg >= 2:
                        vector.wait_ge(s3_ex, sg - 1)
                    off = 0
                    for c, w in enumerate(CHW[s]):
                        pidx = MMB + CPH * h + preC[s] + c
                        vector.wait_ge(s3_mm3, pidx + 1)
                        vector.scalar_tensor_tensor(scores[sg % 2][:, off:off + w],
                                                    dtab_sb[:, TSTART[s] * 128 + off: TSTART[s] * 128 + off + w],
                                                    SLOPES[h], psc[pidx % 2][:, 0:w],
                                                    ALU.mult, ALU.add).then_inc(s3_bk3, 1)
                        off += w
                    if h >= 2:
                        vector.wait_ge(s3_ex, 8 * (h - 2) + s + 1)
                    vector.wait_ge(s3_bk3, MMB + CPH * h + preC[s + 1])
                    vector.reduce_max(negmax[h % 2][:, s:s + 1], scores[sg % 2][:, 0:LS[s] * 128],
                                      axis=AX.X, negate=True).then_inc(s3_mx, 1)

                def emit_psT(s):
                    sg = 8 * h + s
                    for gi, (t0, nt) in enumerate(GROUPS(s)):
                        gg = GPH * h + preG[s] + gi
                        vector.wait_ge(s3_tp3, gg + 1)
                        if sg >= 2:
                            vector.wait_ge(s3_avs, sg - 1)
                        vector.tensor_copy(probsT[sg % 2][:, t0 * 128:(t0 + nt) * 128],
                                           ptp[gg % 2][:, 0:nt * 128]).then_inc(s3_te, 1)

                emit_chunks_max(0)
                emit_chunks_max(1)
                for s in range(8):
                    if s + 2 < 8:
                        emit_chunks_max(s + 2)
                    emit_psT(s)
                    if s == 0 and h >= 2:
                        jz = h - 2
                        vector.wait_ge(s3_zm, jz + 1)
                        if jz >= 2:
                            vector.wait_ge(s3_nrm, jz - 1)
                        vector.tensor_copy(reciprep[jz % 2][:], rp_ps[:, 0:1024]).then_inc(s3_rp, 1)
                vector.wait_ge(s3_ex, 8 * h + 8)
                if h >= 1:
                    vector.wait_ge(s3_zc, h)
                vector.reciprocal(recipZ[:, 0:8], Zcols[h % 2][:, 0:8]).then_inc(s3_zr, 1)
                if h + 1 < H:
                    emit_tt(h + 1)
            j = H - 1
            vector.wait_ge(s3_avs, 8 * (j + 1))
            vector.wait_ge(s3_nrm, j - 1)
            vector.tensor_copy(attnraw[j % 2][:], av_ps[:, 0:1024]).then_inc(s3_ave, 1)
            vector.wait_ge(s3_zt, j + 1)
            vector.wait_ge(s3_zd, 32 * (j - 1) + 16)
            vector.tensor_copy(zb_sb[0:8, :], ptp[0][0:8, 0:128]).then_inc(s3_ze, 1)
            for jz in (H - 2, H - 1):
                vector.wait_ge(s3_zm, jz + 1)
                vector.wait_ge(s3_nrm, jz - 1)
                vector.tensor_copy(reciprep[jz % 2][:], rp_ps[:, 0:1024]).then_inc(s3_rp, 1)

        @block.gpsimd
        def _(gpsimd):
            for h in range(H):
                gpsimd.wait_ge(s3_vd3, 2 * h + 1)
                gpsimd.tensor_tensor(scr[h % 2][:, 0:1024], scr[h % 2][:, 0:1024],
                                     nbrep_q[:], ALU.add).then_inc(s3_pd3, 1)
                gpsimd.wait_ge(s3_vd3, 2 * h + 2)
                gpsimd.tensor_tensor(scr[h % 2][:, 1024:3072], scr[h % 2][:, 1024:3072],
                                     nbrep_k[:], ALU.add).then_inc(s3_pd3, 1)
                if h >= 2:
                    j = h - 2
                    gpsimd.wait_ge(s3_rp, j + 1)
                    gpsimd.wait_ge(s3_ave, j + 1)
                    if j >= 2:
                        gpsimd.wait_ge(s3_sp3[j % 2], 16 * (j // 2))
                    gpsimd.tensor_tensor(attnT_sb[j % 2][:], attnraw[j % 2][:],
                                         reciprep[j % 2][:], ALU.mult).then_inc(s3_nrm, 1)
            for j in (H - 2, H - 1):
                gpsimd.wait_ge(s3_rp, j + 1)
                gpsimd.wait_ge(s3_ave, j + 1)
                gpsimd.wait_ge(s3_sp3[j % 2], 16 * (j // 2))
                gpsimd.tensor_tensor(attnT_sb[j % 2][:], attnraw[j % 2][:],
                                     reciprep[j % 2][:], ALU.mult).then_inc(s3_nrm, 1)

        @block.scalar
        def _(scalar):
            as3 = SelfSem(scalar, s_as3)
            for h in range(H):
                scalar.wait_ge(s3_pd3, 2 * h + 1)
                if h >= 2:
                    scalar.wait_ge(s3_mm3, MMB + CPH * (h - 1))
                scalar.activation(qhat[h % 2][:], scr[h % 2][:, 0:1024], AF.Identity,
                                  scale=lnp_sb[:, h:h + 1],
                                  bias=lnp_sb[:, 32 + h:33 + h]).then_inc(s3_ak3, 1)
                scalar.wait_ge(s3_pd3, 2 * h + 2)
                scalar.activation(khat[h % 2][:], scr[h % 2][:, 1024:3072], AF.Identity,
                                  scale=lnp_sb[:, 64 + h:65 + h],
                                  bias=lnp_sb[:, 96 + h:97 + h]).then_inc(s3_ak3, 1)
                for s in range(8):
                    sg = 8 * h + s
                    scalar.wait_ge(s3_mx, sg + 1)
                    if sg >= 2:
                        scalar.wait_ge(s3_tp3, GG_END(sg - 2))
                    if s == 0 and h >= 1:
                        scalar.wait_ge(s3_zr, h)
                    scalar.activation(probs[sg % 2][:, 0:LS[s] * 128],
                                      scores[sg % 2][:, 0:LS[s] * 128], AF.Exp,
                                      bias=negmax[h % 2][:, s:s + 1],
                                      accum_out=Zcols[h % 2][:, s:s + 1]).then_inc(s3_ex, 1)
                scalar.wait_ge(s3_zr, h + 1)
                if h >= 1:
                    scalar.wait_ge(s3_zt, h)
                scalar.copy(recipZb[:, 0:8], recipZ[:, 0:8]).then_inc(s3_zc, 1)


        for g_ in reversed(ctx[ctx_p3_base:ctx_p3_end]):
            g_.__exit__(None, None, None)
        del ctx[ctx_p3_base:ctx_p3_end]

        NG4 = 4
        proj4 = sb("proj4", [128, 2 * D], F32)
        x2_4 = sb("x2_4", [128, 2 * D], F32)
        bigT = sb("bigT", [128, NK * 256], BF16)
        w4buf = [sb(f"w4buf{j}", [128, NK * 256], BF16) for j in range(2)]
        xld4 = sb("xld4", [128, D], F32)
        x3h4 = sb("x3h4", [128, D], BF16)
        outrow = sb("outrow", [128, D], F32)
        at_sb = sb("at_sb", [128, NK * 128], BF16)
        bt_sb = sb("bt_sb", [128, 16 * 256], BF16)
        rt_sb = sb("rt_sb", [128, NK * 8], BF16)
        brow = sb("brow", [1, D], BF16)
        ones4 = sb("ones4", [1, 128], BF16)
        hw4 = sb("hw4", [128, 128], BF16)
        hwT4 = sb("hwT4", [128, 2 * 128], BF16)
        h4f = sb("h4f", [128, 128], F32)
        r8 = sb("r8", [128, 8], F32)
        e8 = sb("e8", [128, 8], F32)
        nm8 = sb("nm8", [128, 1], F32)
        t8 = sb("t8", [128, 8], F32)
        g8 = sb("g8", [128, 8], F32)
        m1c = sb("m1c", [128, 1], F32)
        m2c = sb("m2c", [128, 1], F32)
        rc8 = sb("rc8", [128, 1], F32)
        comb = sb("comb", [128, 8], F32)
        st4A = sb("st4A", [128, 2], F32)
        st4B = sb("st4B", [128, 2], F32)
        st4M = sb("st4M", [128, 2], F32)
        st4V = sb("st4V", [128, 2], F32)
        ln4st = sb("ln4st", [128, 4], F32)
        sqj4 = sb("sqj4", [128, D], BF16)

        pac4 = [psum(f"pac4{j}", [128, 512], F32) for j in range(4)]
        ptr4 = [psum(f"ptr4{j}", [128, 1024], BF16) for j in range(2)]
        prt4 = psum("prt4", [128, 512], F32)
        ph4 = psum("ph4", [128, 512], F32)

        s4_w = [sem("s4_w0"), sem("s4_w1")]
        s4_wp = sem("s4_wp")
        s4_at = sem("s4_at")
        s4_x = sem("s4_x")
        s4_mo = sem("s4_mo")
        s4_bo = sem("s4_bo")
        s4_mf = sem("s4_mf")
        s4_bf = sem("s4_bf")
        s4_mr = sem("s4_mr")
        s4_br = sem("s4_br")
        s4_mh = sem("s4_mh")
        s4_bh = sem("s4_bh")
        s4_tx = sem("s4_tx")
        s4_te = sem("s4_te")
        s4_vd = sem("s4_vd")
        s4_ad = sem("s4_ad")
        s4_vr = sem("s4_vr")
        s4_ar = sem("s4_ar")
        s4_cb = sem("s4_cb")
        s4_hw = sem("s4_hw")
        s_out = sem("s_out")
        s_vs4 = sem("s_vs4")
        s_as4 = sem("s_as4")

        CH4 = []
        for G4 in range(NG4):
            for typ in (2, 3, 4):
                for mloc in range(2):
                    CH4.append((typ, G4, mloc))

        def ch4_base(idx):
            vd = sum(3 if CH4[i][0] != 3 else 2 for i in range(idx))
            ad = 2 * idx
            return vd, ad

        def tx_ord(G4, kind, mloc, grp=0):
            base = 18 * G4
            if kind == "x3":
                return base + mloc * 8 + grp
            return base + 16 + mloc

        @block.sync
        def _(sync):
            sync.wait_ge(s3_zm, H)
            sync.wait_ge(s3_rp, H)
            sync.wait_ge(s3_zc, H)
            sync.wait_ge(s3_nrm, H)
            sync.wait_ge(s3_sp3[0], 16 * (H // 2))
            sync.wait_ge(s3_sp3[1], 16 * (H // 2))
            for G4 in range(NG4):
                if G4 >= 1:
                    sync.wait_ge(s4_mf, 32 * G4)
                sync.dma_start(
                    out=bigT.ap().rearrange("p (k j) -> p k j", k=NK, j=256),
                    in_=aT_d.rearrange("(k d) t -> d k t", k=NK, d=128)[:, :, G4 * 256:(G4 + 1) * 256],
                ).then_inc(s4_at, 16)
                for mloc in range(2):
                    m = 2 * G4 + mloc
                    ci = CH4.index((2, G4, mloc))
                    vb, ab = ch4_base(ci)
                    if m >= 1:
                        pci = CH4.index((2, G4, mloc)) - 1
                        pvb, _ = ch4_base(pci)
                        sync.wait_ge(s4_vd, pvb + 3)
                    sync.dma_start(out=xld4[:], in_=x_in[m * 128:(m + 1) * 128, :]).then_inc(s4_x, 16)
                for mloc in range(2):
                    m = 2 * G4 + mloc
                    ci = CH4.index((4, G4, mloc))
                    vb, ab = ch4_base(ci)
                    sync.wait_ge(s4_vd, vb + 3)
                    sync.dma_start(out=out[m * 128:(m + 1) * 128, :], in_=outrow[:]).then_inc(s_out, 16)
            sync.wait_ge(s_out, 16 * NM)

        @block.gpsimd
        def _(gpsimd):
            gpsimd.wait_ge(s3_nrm, H)
            gpsimd.dma_start(out=at_sb.ap().rearrange("p (k n) -> p k n", k=NK, n=128),
                             in_=wa_in.rearrange("k p n -> p k n")).then_inc(s4_wp, 16)
            gpsimd.dma_start(out=bt_sb.ap().rearrange("p (k n) -> p k n", k=16, n=256),
                             in_=wb_in.rearrange("k p n -> p k n")).then_inc(s4_wp, 16)
            gpsimd.dma_start(out=rt_sb.ap().rearrange("p (k n) -> p k n", k=NK, n=8),
                             in_=wr_in.rearrange("k p n -> p k n")).then_inc(s4_wp, 16)
            gpsimd.dma_start(out=brow[:, :], in_=bb_in[:, :]).then_inc(s4_wp, 16)
            for G4 in range(NG4):
                for c4 in range(32):
                    widx = 32 * G4 + c4
                    src = wout_in[c4] if c4 < 16 else wbase_in[c4 - 16]
                    if widx >= 2:
                        prev = widx - 2
                        if (prev % 32) < 16:
                            gpsimd.wait_ge(s4_mo, 2 * ((prev % 32) + 1) + 32 * (prev // 32))
                        else:
                            gpsimd.wait_ge(s4_mf, 2 * ((prev % 32) - 16 + 1) + 32 * (prev // 32))
                    gpsimd.dma_start(
                        out=w4buf[widx % 2].ap().rearrange("p (k n) -> p k n", k=NK, n=256),
                        in_=src.rearrange("k p n -> p k n"),
                    ).then_inc(s4_w[widx % 2], 16)

        @block.tensor
        def _(tensor):
            tensor.wait_ge(s4_wp, 64)
            for G4 in range(NG4):
                tensor.wait_ge(s4_at, 16 * (G4 + 1))
                for nn in range(16):
                    widx = 32 * G4 + nn
                    tensor.wait_ge(s4_w[widx % 2], 16 * (widx // 2 + 1))
                    for mloc in range(2):
                        oi = 32 * G4 + 2 * nn + mloc
                        if oi >= 4:
                            tensor.wait_ge(s4_bo, oi - 3)
                        bk = pac4[oi % 4]
                        for k in range(NK):
                            ins = tensor.matmul(bk[:, 0:256],
                                                bigT[:, k * 256 + mloc * 128: k * 256 + mloc * 128 + 128],
                                                w4buf[widx % 2][:, k * 256:(k + 1) * 256],
                                                start=(k == 0), stop=(k == NK - 1))
                        ins.then_inc(s4_mo, 1)
                for mloc in range(2):
                    ci = CH4.index((3, G4, mloc))
                    vb, ab = ch4_base(ci)
                    tensor.wait_ge(s4_ad, ab + 2)
                    for grp in range(8):
                        gi = tx_ord(G4, "x3", mloc, grp)
                        if gi >= 2:
                            tensor.wait_ge(s4_te, gi - 1)
                        tb = ptr4[gi % 2]
                        for j in range(4):
                            k = grp * 4 + j
                            ins = tensor.transpose(tb[:, j * 128:(j + 1) * 128],
                                                   x3h4[:, k * 128:(k + 1) * 128],
                                                   id_bf[:, :])
                        ins.then_inc(s4_tx, 1)
                for mloc in range(2):
                    if G4 + mloc >= 1:
                        tensor.wait_ge(s4_br, 2 * G4 + mloc)
                    tensor.wait_ge(s4_te, tx_ord(G4, "x3", mloc, 7) + 1 - 0)
                    for k in range(NK):
                        ins = tensor.matmul(prt4[:, 0:8],
                                            bigT[:, k * 256 + mloc * 128: k * 256 + mloc * 128 + 128],
                                            rt_sb[:, k * 8:(k + 1) * 8],
                                            start=(k == 0), stop=(k == NK - 1))
                    ins.then_inc(s4_mr, 1)
                    if G4 + mloc >= 1:
                        tensor.wait_ge(s4_bh, 2 * G4 + mloc)
                    for k in range(NK):
                        ins = tensor.matmul(ph4[:, 0:128],
                                            bigT[:, k * 256 + mloc * 128: k * 256 + mloc * 128 + 128],
                                            at_sb[:, k * 128:(k + 1) * 128],
                                            start=(k == 0), stop=(k == NK - 1))
                    ins.then_inc(s4_mh, 1)
                    gi = tx_ord(G4, "hw", mloc)
                    tensor.wait_ge(s4_hw, 2 * G4 + mloc + 1)
                    tensor.wait_ge(s4_te, gi - 1)
                    tensor.transpose(ptr4[gi % 2][:, 0:128], hw4[:, 0:128],
                                     id_bf[:, :]).then_inc(s4_tx, 1)
                for nn in range(16):
                    widx = 32 * G4 + 16 + nn
                    tensor.wait_ge(s4_w[widx % 2], 16 * (widx // 2 + 1))
                    for mloc in range(2):
                        oi = 32 * G4 + 2 * nn + mloc
                        if oi >= 4:
                            tensor.wait_ge(s4_bf, oi - 3)
                        tensor.wait_ge(s4_te, tx_ord(G4, "hw", mloc) + 1)
                        bk = pac4[oi % 4]
                        for k in range(NK):
                            tensor.matmul(bk[:, 0:256],
                                          bigT[:, k * 256 + mloc * 128: k * 256 + mloc * 128 + 128],
                                          w4buf[widx % 2][:, k * 256:(k + 1) * 256],
                                          start=(k == 0), stop=False)
                        tensor.matmul(bk[:, 0:256],
                                      hwT4[:, mloc * 128:(mloc + 1) * 128],
                                      bt_sb[:, nn * 256:(nn + 1) * 256],
                                      start=False, stop=False)
                        ins = tensor.matmul(bk[:, 0:256], ones4[0:1, :],
                                            brow[0:1, nn * 256:(nn + 1) * 256],
                                            start=False, stop=True)
                        ins.then_inc(s4_mf, 1)

        @block.vector
        def _(vector):
            vs4 = SelfSem(vector, s_vs4)
            ins = vector.memset(ones4[:, :], 1.0)
            vs4.edge(ins)

            def ln_chain_dve(typ, G4, mloc):
                ci = CH4.index((typ, G4, mloc))
                vb, ab = ch4_base(ci)
                if typ == 2:
                    rowf = proj4.ap()[:, mloc * D:(mloc + 1) * D]
                    vector.wait_ge(s4_bo, 32 * (G4 + 1))
                elif typ == 3:
                    rowf = x2_4.ap()[:, mloc * D:(mloc + 1) * D]
                    vb2, _ = ch4_base(CH4.index((2, G4, mloc)))
                    vector.wait_ge(s4_vd, vb2 + 3)
                else:
                    rowf = proj4.ap()[:, mloc * D:(mloc + 1) * D]
                    vector.wait_ge(s4_bf, 32 * (G4 + 1))
                pi = ci % 2
                vector.reduce_sum(st4A[:, pi:pi + 1], rowf, axis=AX.X)
                vector.scalar_tensor_tensor(sqj4[:], rowf, 1.0, rowf, ALU.bypass,
                                            ALU.mult, accum_out=st4B[:, pi:pi + 1]).then_inc(s4_vd, 1)
                vector.wait_ge(s4_ad, ab + 1)
                vector.scalar_tensor_tensor(ln4st[:, 2 * pi:2 * pi + 1], st4B[:, pi:pi + 1], 1.0 / D,
                                            st4V[:, pi:pi + 1], ALU.mult, ALU.subtract).then_inc(s4_vd, 1)
                if typ == 2:
                    vector.wait_ge(s4_ad, ab + 2)
                    vector.wait_ge(s4_x, 16 * (2 * G4 + mloc + 1))
                    vector.tensor_tensor(x2_4.ap()[:, mloc * D:(mloc + 1) * D], xld4[:],
                                         rowf, ALU.add).then_inc(s4_vd, 1)
                elif typ == 4:
                    vector.wait_ge(s4_ad, ab + 2)
                    if 2 * G4 + mloc >= 1:
                        vector.wait_ge(s_out, 16 * (2 * G4 + mloc))
                    vector.tensor_tensor(outrow[:], x2_4.ap()[:, mloc * D:(mloc + 1) * D],
                                         rowf, ALU.add).then_inc(s4_vd, 1)

            for G4 in range(NG4):
                for nn in range(16):
                    for mloc in range(2):
                        oi = 32 * G4 + 2 * nn + mloc
                        vector.wait_ge(s4_mo, oi + 1 - 32 * G4 + 32 * G4)
                        vector.tensor_copy(proj4.ap()[:, mloc * D + nn * 256: mloc * D + nn * 256 + 256],
                                           pac4[oi % 4][:, 0:256]).then_inc(s4_bo, 1)
                for mloc in range(2):
                    ln_chain_dve(2, G4, mloc)
                for mloc in range(2):
                    ln_chain_dve(3, G4, mloc)
                for mloc in range(2):
                    im = 2 * G4 + mloc
                    vector.wait_ge(s4_mr, im + 1)
                    if im >= 1:
                        vector.wait_ge(s4_ar, im)
                    vector.tensor_copy(r8[:, 0:8], prt4[:, 0:8]).then_inc(s4_br, 1)
                    vector.wait_ge(s4_br, im + 1)
                    ins = vector.reduce_max(nm8[:, 0:1], r8[:, 0:8], axis=AX.X, negate=True)
                    ins.then_inc(s4_vr, 1)
                    vector.wait_ge(s4_ar, im + 1)
                    ins = vector.reduce_max(m1c[:, 0:1], e8[:, 0:8], axis=AX.X)
                    vs4.edge(ins)
                    ins = vector.tensor_scalar(t8[:, 0:8], e8[:, 0:8], m1c[:, 0:1], None, ALU.is_ge)
                    vs4.edge(ins)
                    ins = vector.scalar_tensor_tensor(t8[:, 0:8], t8[:, 0:8], -1e30, e8[:, 0:8],
                                                      ALU.mult, ALU.add)
                    vs4.edge(ins)
                    ins = vector.reduce_max(m2c[:, 0:1], t8[:, 0:8], axis=AX.X)
                    vs4.edge(ins)
                    ins = vector.tensor_scalar(g8[:, 0:8], e8[:, 0:8], m2c[:, 0:1], None, ALU.is_ge)
                    vs4.edge(ins)
                    ins = vector.tensor_tensor(g8[:, 0:8], g8[:, 0:8], e8[:, 0:8], ALU.mult)
                    vs4.edge(ins)
                    ins = vector.tensor_tensor(m1c[:, 0:1], m1c[:, 0:1], m2c[:, 0:1], ALU.add)
                    vs4.edge(ins)
                    ins = vector.reciprocal(rc8[:, 0:1], m1c[:, 0:1])
                    vs4.edge(ins)
                    ins = vector.tensor_scalar(comb[:, 0:8], g8[:, 0:8], rc8[:, 0:1], 2.0,
                                               ALU.mult, ALU.mult)
                    vs4.edge(ins)
                    vector.wait_ge(s4_mh, im + 1)
                    vector.tensor_copy(h4f[:, 0:128], ph4[:, 0:128]).then_inc(s4_bh, 1)
                    vector.wait_ge(s4_bh, im + 1)
                    if im >= 1:
                        pg, pm = divmod(im - 1, 2)
                        vector.wait_ge(s4_tx, tx_ord(pg, "hw", pm) + 1)
                    for e in range(8):
                        ins = vector.tensor_single_scalar(hw4[:, e * 16:(e + 1) * 16],
                                                          h4f[:, e * 16:(e + 1) * 16],
                                                          comb[:, e:e + 1], ALU.mult)
                    ins.then_inc(s4_hw, 1)
                for nn in range(16):
                    for mloc in range(2):
                        oi = 32 * G4 + 2 * nn + mloc
                        vector.wait_ge(s4_mf, oi + 1)
                        ci = CH4.index((2, G4, mloc))
                        vb, _ = ch4_base(ci)
                        vector.wait_ge(s4_vd, vb + 3)
                        vector.tensor_copy(proj4.ap()[:, mloc * D + nn * 256: mloc * D + nn * 256 + 256],
                                           pac4[oi % 4][:, 0:256]).then_inc(s4_bf, 1)
                for mloc in range(2):
                    ln_chain_dve(4, G4, mloc)

        @block.scalar
        def _(scalar):
            as4 = SelfSem(scalar, s_as4)

            def ln_chain_act(typ, G4, mloc):
                ci = CH4.index((typ, G4, mloc))
                vb, ab = ch4_base(ci)
                if typ == 2:
                    rowf = proj4.ap()[:, mloc * D:(mloc + 1) * D]
                elif typ == 3:
                    rowf = x2_4.ap()[:, mloc * D:(mloc + 1) * D]
                else:
                    rowf = proj4.ap()[:, mloc * D:(mloc + 1) * D]
                pi = ci % 2
                scalar.wait_ge(s4_vd, vb + 1)
                ins = scalar.mul(st4M[:, pi:pi + 1], st4A[:, pi:pi + 1], 1.0 / D)
                as4.edge(ins)
                scalar.activation(st4V[:, pi:pi + 1], st4M[:, pi:pi + 1], AF.Square).then_inc(s4_ad, 1)
                scalar.wait_ge(s4_vd, vb + 2)
                ins = scalar.activation(ln4st[:, 2 * pi + 1:2 * pi + 2], ln4st[:, 2 * pi:2 * pi + 1], AF.Ln, bias=eps_sb[:, 0:1])
                as4.edge(ins)
                ins = scalar.activation(ln4st[:, 2 * pi + 1:2 * pi + 2], ln4st[:, 2 * pi + 1:2 * pi + 2], AF.Exp, scale=-0.5)
                as4.edge(ins)
                ins = scalar.activation(ln4st[:, 2 * pi:2 * pi + 1], st4M[:, pi:pi + 1], AF.Identity, scale=ln4st[:, 2 * pi + 1:2 * pi + 2])
                as4.edge(ins)
                ins = scalar.mul(ln4st[:, 2 * pi:2 * pi + 1], ln4st[:, 2 * pi:2 * pi + 1], -1.0)
                as4.edge(ins)
                if typ == 3:
                    pci = ci - 1
                    if CH4[pci][0] == 3:
                        scalar.wait_ge(s4_tx, tx_ord(G4, "x3", mloc - 1, 7) + 1)
                    elif G4 >= 1:
                        scalar.wait_ge(s4_tx, tx_ord(G4 - 1, "x3", 1, 7) + 1)
                    scalar.activation(x3h4[:], rowf, AF.Identity,
                                      scale=ln4st[:, 2 * pi + 1:2 * pi + 2], bias=ln4st[:, 2 * pi:2 * pi + 1]).then_inc(s4_ad, 1)
                else:
                    scalar.activation(rowf, rowf, AF.Identity,
                                      scale=ln4st[:, 2 * pi + 1:2 * pi + 2], bias=ln4st[:, 2 * pi:2 * pi + 1]).then_inc(s4_ad, 1)

            for G4 in range(NG4):
                for mloc in range(2):
                    ln_chain_act(2, G4, mloc)
                for mloc in range(2):
                    ln_chain_act(3, G4, mloc)
                    for grp in range(8):
                        gi = tx_ord(G4, "x3", mloc, grp)
                        scalar.wait_ge(s4_tx, gi + 1)
                        k0 = grp * 4
                        dst = bigT.ap().rearrange("p (k j) -> p k j", k=NK, j=256)[
                            :, k0:k0 + 4, mloc * 128:(mloc + 1) * 128]
                        srcp = ptr4[gi % 2][:, 0:512].rearrange("p (k j) -> p k j", k=4, j=128)
                        scalar.copy(dst, srcp).then_inc(s4_te, 1)
                for mloc in range(2):
                    im = 2 * G4 + mloc
                    scalar.wait_ge(s4_vr, im + 1)
                    scalar.activation(e8[:, 0:8], r8[:, 0:8], AF.Exp,
                                      bias=nm8[:, 0:1]).then_inc(s4_ar, 1)
                    gi = tx_ord(G4, "hw", mloc)
                    scalar.wait_ge(s4_tx, gi + 1)
                    if G4 >= 1:
                        scalar.wait_ge(s4_mf, 32 * G4)
                    scalar.copy(hwT4[:, mloc * 128:(mloc + 1) * 128],
                                ptr4[gi % 2][:, 0:128]).then_inc(s4_te, 1)
                for mloc in range(2):
                    ln_chain_act(4, G4, mloc)

    for g in reversed(ctx):
        g.__exit__(None, None, None)
    return nc, dbg



def prep_inputs(inputs):
    wq = np.ascontiguousarray(inputs["L0_attn_Wqkv"].T)
    tiles = wq.reshape(NK, 128, 24, 512).transpose(2, 0, 1, 3)
    order = []
    for kind, a in SCHED:
        base = {"q": 0, "k": 8, "v": 16}[kind]
        order.append(base + a)
    wqkv_t = np.ascontiguousarray(tiles[order])

    lnp_arr = np.zeros((128, 4 * 32), np.float32)
    lnp_arr[:, 0:32] = inputs["L0_attn_q_ln_w"].reshape(32, 128).T * SCALE
    lnp_arr[:, 32:64] = inputs["L0_attn_q_ln_b"].reshape(32, 128).T * SCALE
    lnp_arr[:, 64:96] = inputs["L0_attn_k_ln_w"].reshape(32, 128).T
    lnp_arr[:, 96:128] = inputs["L0_attn_k_ln_b"].reshape(32, 128).T

    ident = np.eye(128, dtype=ml_dtypes.bfloat16)
    identf = np.eye(128, dtype=np.float32)

    wout_t = np.ascontiguousarray(
        inputs["L0_attn_Wout"].T.reshape(NK, 128, 16, 256).transpose(2, 0, 1, 3))
    wbase_t = np.ascontiguousarray(
        inputs["L0_ffn_W_base"].T.reshape(NK, 128, 16, 256).transpose(2, 0, 1, 3))
    wa_t = np.ascontiguousarray(
        np.asarray(inputs["L0_ffn_A"]).reshape(128, D).T.reshape(NK, 128, 128))
    wb_t = np.ascontiguousarray(
        np.asarray(inputs["L0_ffn_B"]).transpose(0, 2, 1).reshape(128, D).reshape(128, 16, 256).transpose(1, 0, 2))
    wr_t = np.ascontiguousarray(
        np.asarray(inputs["L0_ffn_W_router"]).T.reshape(NK, 128, 8))
    bb = np.ascontiguousarray(np.asarray(inputs["L0_ffn_b_base"]).reshape(1, D))
    return dict(wqkv=wqkv_t, lnp=lnp_arr, ident=ident, identf=identf,
                wout=wout_t, wbase=wbase_t, wa=wa_t, wb=wb_t, wr=wr_t, bb=bb)


def core_x(inputs, c):
    x = np.asarray(inputs["L0_attn_hidden_states"])
    seq, par = c // 2, c % 2
    return np.ascontiguousarray(x[seq].reshape(16, 128, D)[par::2].reshape(RTOK, D))


def core_dtab(c):
    p = c % 2
    out = np.full((NDTAB, 128, 128), -1e33, np.float32)
    jj = np.arange(128)[None, :]
    ii = np.arange(128)[:, None]
    idx = 0
    for s in range(8):
        Q = 2 * s + p
        for t in range(LS[s]):
            Dv = (t * 128 + jj) - (Q * 128 + ii)
            out[idx] = np.where(Dv <= 0, Dv.astype(np.float32), np.float32(-1e33))
            idx += 1
    assert idx == NDTAB
    return out



_CACHE = {}


def kernel(**inputs):
    inputs = {k: np.asarray(v) for k, v in inputs.items()}
    if "nc" not in _CACHE:
        nc, _ = build(debug=False)
        _CACHE["nc"] = nc
    nc = _CACHE["nc"]
    shared = prep_inputs(inputs)
    in_maps = []
    for c in range(NCORE):
        m = dict(shared)
        m["x"] = core_x(inputs, c)
        m["dtab"] = core_dtab(c)
        in_maps.append(m)
    res = run_bass_kernel_spmd(nc, in_maps, core_ids=list(range(NCORE)))
    out = np.empty((B, S, D), np.float32)
    view = out.reshape(B, 16, 128, D)
    for c in range(NCORE):
        seq, par = c // 2, c % 2
        view[seq, par::2] = np.asarray(res.results[c]["out"]).reshape(NM, 128, D)
    return out


# version 02
# speedup vs baseline: 1.0324x; 1.0324x over previous
import numpy as np
import ml_dtypes
import concourse.bass as bass
import concourse.mybir as mybir
from concourse.bass_utils import run_bass_kernel_spmd

F32 = mybir.dt.float32
BF16 = mybir.dt.bfloat16
AF = mybir.ActivationFunctionType
ALU = mybir.AluOpType
AX = mybir.AxisListType

B, S, D, H, DH = 4, 2048, 4096, 32, 128
NCORE = 8
RTOK = 1024
NM = 8
NK = 32
CLIP = 8.0
EPS = 1e-5
SCALE = DH ** -0.5
PAIRS = [[2 * i, 2 * i + 1] for i in range(4)]

SCHED = []
for g in range(4):
    SCHED += [("k", 2 * g), ("k", 2 * g + 1), ("v", 2 * g), ("v", 2 * g + 1)]
SCHED += [("q", a) for a in range(8)]
QK_ORD, V_ORD, QK_SCHED, V_SCHED = {}, {}, [], []
for _i, (_kind, _a) in enumerate(SCHED):
    if _kind == "v":
        V_ORD[_i] = len(V_SCHED); V_SCHED.append(_i)
    else:
        QK_ORD[_i] = len(QK_SCHED); QK_SCHED.append(_i)

LS = [2 * s + 2 for s in range(8)]
TSTART = [sum(LS[:s]) for s in range(8)]
NDTAB = sum(LS)
NCH = 24 * NM
P0_GRPS = NM * 8
P1_GRPS = len(QK_SCHED) * NM


import math as _math
CHW = []
for _s in range(8):
    w = LS[_s] * 128
    CHW.append([min(512, w - c * 512) for c in range((w + 511) // 512)])
preC = [0]
for _s in range(8):
    preC.append(preC[-1] + len(CHW[_s]))
CPH = preC[-1]


def GROUPS(s):
    return [(t0, min(4, LS[s] - t0)) for t0 in range(0, LS[s], 4)]


GPH = sum(len(GROUPS(_s)) for _s in range(8))
preG = [0]
for _s in range(8):
    preG.append(preG[-1] + len(GROUPS(_s)))


def GG_END(sg):
    h, s = divmod(sg, 8)
    return GPH * h + preG[s + 1]


SLOPES = [2.0 ** (-0.25 * (hh + 1)) for hh in range(H)]
MMB = 12


class SelfSem:
    def __init__(self, eng, semh):
        self.eng, self.sem, self.n = eng, semh, 0

    def edge(self, instr):
        instr.then_inc(self.sem, 1)
        self.n += 1
        self.eng.wait_ge(self.sem, self.n)


def build(debug=True):
    nc = bass.Bass("TRN2", target_bir_lowering=False, num_devices=NCORE)

    x_in = nc.dram_tensor("x", [RTOK, D], F32, kind="ExternalInput").ap()
    wqkv = nc.dram_tensor("wqkv", [24, 128, NK, 512], F32, kind="ExternalInput").ap()
    lnp = nc.dram_tensor("lnp", [128, 4 * 32], F32, kind="ExternalInput").ap()
    ident_in = nc.dram_tensor("ident", [128, 128], BF16, kind="ExternalInput").ap()
    identf_in = nc.dram_tensor("identf", [128, 128], F32, kind="ExternalInput").ap()
    out = nc.dram_tensor("out", [RTOK, D], F32, kind="ExternalOutput").ap()

    cin = [nc.dram_tensor(f"cin{g}", [2048, 1024], BF16, kind="Internal").ap() for g in range(4)]
    cout = [nc.dram_tensor(f"cout{g}", [4096, 1024], BF16, kind="Internal").ap() for g in range(4)]
    sin = nc.dram_tensor("sin", [16, 128], F32, kind="Internal").ap()
    sout = nc.dram_tensor("sout", [32, 128], F32, kind="Internal").ap()
    qT_d = nc.dram_tensor("qT_d", [D, RTOK], BF16, kind="Internal").ap()
    qs_d = nc.dram_tensor("qs_d", [32, 128], F32, kind="Internal").ap()
    aT_d = nc.dram_tensor("aT_d", [D, RTOK], BF16, kind="Internal").ap()
    zb_d = nc.dram_tensor("zb_d", [8, 128], BF16, kind="Internal").ap()
    dtab_in = nc.dram_tensor("dtab", [NDTAB, 128, 128], F32, kind="ExternalInput").ap()
    wout_in = nc.dram_tensor("wout", [16, 128, NK, 256], F32, kind="ExternalInput").ap()
    wbase_in = nc.dram_tensor("wbase", [16, 128, NK, 256], F32, kind="ExternalInput").ap()
    wa_in = nc.dram_tensor("wa", [128, NK, 128], F32, kind="ExternalInput").ap()
    wb_in = nc.dram_tensor("wb", [128, 16, 256], F32, kind="ExternalInput").ap()
    wr_in = nc.dram_tensor("wr", [128, NK, 8], F32, kind="ExternalInput").ap()
    bb_in = nc.dram_tensor("bb", [1, D], F32, kind="ExternalInput").ap()

    dbg = {}
    if debug:
        dbg["d_qT"] = nc.dram_tensor("d_qT", [D, RTOK], BF16, kind="ExternalOutput").ap()
        dbg["d_kv"] = nc.dram_tensor("d_kv", [4, 4096, 1024], BF16, kind="ExternalOutput").ap()
        dbg["d_st"] = nc.dram_tensor("d_st", [32, 128], F32, kind="ExternalOutput").ap()
        dbg["d_xT"] = nc.dram_tensor("d_xT", [128, NK * RTOK], BF16, kind="ExternalOutput").ap()
        dbg["d_aT"] = nc.dram_tensor("d_aT", [D, RTOK], BF16, kind="ExternalOutput").ap()

    ctx = []

    def sb(name, shape, dtype):
        g = nc.sbuf_tensor(name, shape, dtype); t = g.__enter__(); ctx.append(g); return t

    def psum(name, shape, dtype):
        g = nc.psum_tensor(name, shape, dtype); t = g.__enter__(); ctx.append(g); return t

    def sem(name):
        g = nc.semaphore(name); s = g.__enter__(); ctx.append(g); return s

    id_bf = sb("id_bf", [128, 128], BF16)
    id_f = sb("id_f", [128, 128], F32)
    lnp_sb = sb("lnp_sb", [128, 4 * 32], F32)
    eps_sb = sb("eps_sb", [128, 1], F32)

    ctx_p1_base = len(ctx)
    xhatT = sb("xhatT", [128, NK * RTOK], BF16)
    wbuf = [sb(f"wbuf{j}", [128, NK * 512], BF16) for j in range(2)]
    neg8 = sb("neg8", [128, 512], F32)
    xld = [sb("xld0", [128, D], F32)] * 2
    xhat = [sb("xhat0", [128, D], BF16)] * 2
    sqj = sb("sqj", [128, D], BF16)
    qring = [sb(f"qr{j}", [128, 512], BF16) for j in range(4)]
    tring = [sb(f"tr{j}", [128, 4096], BF16) for j in range(2)]
    vring = [sb(f"vr{j}", [128, 4096], BF16) for j in range(2)]
    psums = sb("psums", [128, 128], F32)
    psumsq = sb("psumsq", [128, 128], F32)
    stA = sb("stA", [128, 8], F32)
    stB = sb("stB", [128, 8], F32)
    stM = sb("stM", [128, 8], F32)
    stV = sb("stV", [128, 8], F32)
    allst = sb("allst", [128, 32], F32)
    ln1st = sb("ln1st", [128, 2], F32)
    st32 = sb("st32", [32, 128], F32)

    pacc = [psum(f"pacc{j}", [128, 512], F32) for j in range(3)]
    ptr = [psum(f"ptr{j}", [128, 1024], BF16) for j in range(2)]
    pst = psum("pst", [128, 128], F32)
    ctx_p1_end = len(ctx)

    s_x = sem("s_x")
    s_x2 = sem("s_x2")
    s_w = [sem("s_w0"), sem("s_w1")]
    s_mm = sem("s_mm")
    s_bk = sem("s_bk")
    s_tp = sem("s_tp")
    s_ev = sem("s_ev")
    s_tsp = [sem("s_tsp0"), sem("s_tsp1")]
    s_vsp = [sem("s_vsp0"), sem("s_vsp1")]
    s_ssp = sem("s_ssp")
    s_ccg = [sem(f"s_ccg{g}") for g in range(4)]
    s_ccs = sem("s_ccs")
    s_vd = sem("s_vd")
    s_ad = sem("s_ad")
    s_st = sem("s_st")
    s_dbg = sem("s_dbg")
    s_vself = sem("s_vself")
    s_aself = sem("s_aself")

    STATS_TP = P0_GRPS + P1_GRPS

    with nc.Block() as block:

        @block.sync
        def _(sync):
            sync.dma_start(out=id_bf[:], in_=ident_in[:, :]).then_inc(s_x, 16)
            sync.dma_start(out=id_f[:], in_=identf_in[:, :]).then_inc(s_x, 16)
            sync.dma_start(out=lnp_sb[:], in_=lnp[:, :]).then_inc(s_x, 16)
            for m in range(NM):
                if m >= 1:
                    sync.wait_ge(s_ad, 2 * (m - 1) + 2)
                sync.dma_start(out=xld[m % 2][:], in_=x_in[m * 128:(m + 1) * 128, :]).then_inc(s_x2, 16)
            for i, (kind, a) in enumerate(SCHED):
                if kind == "v":
                    sync.wait_ge(s_bk, i * NM + NM)
                    par = V_ORD[i] % 2
                    g = a // 2
                    hl0 = (a % 2) * 4
                    flat = cin[g].rearrange("r c -> (r c)")
                    vsrc = vring[par].ap().rearrange("p (m hl d) -> p m hl d", m=NM, hl=4, d=128)
                    for hl in range(4):
                        base = 1024 * 1024 + (hl0 + hl) * 131072
                        dst3 = flat[base: base + 131072].rearrange("(m p d) -> p m d", m=NM, p=128, d=128)
                        sync.dma_start(out=dst3, in_=vsrc[:, :, hl, :]).then_inc(s_vsp[V_ORD[i] % 2], 16)
                else:
                    sync.wait_ge(s_ev, P0_GRPS + NM * (QK_ORD[i] + 1))
                    par = QK_ORD[i] % 2
                    if kind == "k":
                        g = a // 2
                        hl0 = (a % 2) * 4
                        dst = cin[g][hl0 * 128:(hl0 + 4) * 128, :]
                    else:
                        dst = qT_d[a * 512:(a + 1) * 512, :]
                    dst3 = dst.rearrange("(hl d) t -> d hl t", hl=4, d=128)
                    src3 = tring[par].ap().rearrange("p (hl t) -> p hl t", hl=4, t=1024)
                    sync.dma_start(out=dst3, in_=src3).then_inc(s_tsp[QK_ORD[i] % 2], 16)
            sync.wait_ge(s_vd, 23)
            sync.dma_start(out=qs_d[:, :], in_=st32[:, :]).then_inc(s_ssp, 16)
            sync.dma_start(out=sin[:, :], in_=st32[16:32, :]).then_inc(s_ssp, 16)

            if debug:
                for g in range(4):
                    sync.wait_ge(s_ccg[g], 1)
                sync.wait_ge(s_ccs, 1)
                sync.wait_ge(s_tsp[0], 16 * 8)
                sync.wait_ge(s_tsp[1], 16 * 8)
                sync.dma_start(out=dbg["d_qT"][:, :], in_=qT_d[:, :]).then_inc(s_dbg, 16)
                for g in range(4):
                    sync.dma_start(out=dbg["d_kv"][g], in_=cout[g][:, :]).then_inc(s_dbg, 16)
                sync.dma_start(out=dbg["d_st"][:, :], in_=sout[:, :]).then_inc(s_dbg, 16)
                sync.dma_start(out=dbg["d_xT"][:, :], in_=xhatT[:]).then_inc(s_dbg, 16)
                sync.wait_ge(s_dbg, 16 * 7)

        @block.gpsimd
        def _(gpsimd):
            for nn in range(24):
                if nn >= 2:
                    gpsimd.wait_ge(s_mm, (nn - 1) * NM)
                gpsimd.dma_start(
                    out=wbuf[nn % 2].ap().rearrange("p (k n) -> p k n", k=NK, n=512),
                    in_=wqkv[nn],
                ).then_inc(s_w[nn % 2], 16)
            for g in range(4):
                gpsimd.wait_ge(s_tsp[0], 16 * (g + 1))
                gpsimd.wait_ge(s_tsp[1], 16 * (g + 1))
                gpsimd.wait_ge(s_vsp[0], 64 * (g + 1))
                gpsimd.wait_ge(s_vsp[1], 64 * (g + 1))
                gpsimd.collective_compute(
                    "AllGather", ALU.bypass, replica_groups=PAIRS,
                    ins=[cin[g][:, :]], outs=[cout[g][:, :]],
                ).then_inc(s_ccg[g], 1)
            gpsimd.wait_ge(s_ssp, 32)
            gpsimd.collective_compute(
                "AllGather", ALU.bypass, replica_groups=PAIRS,
                ins=[sin[:, :]], outs=[sout[:, :]],
            ).then_inc(s_ccs, 1)

        @block.tensor
        def _(tensor):
            tensor.wait_ge(s_x, 48)
            for m in range(NM):
                tensor.wait_ge(s_ad, 2 * m + 2)
                for kg in range(8):
                    gi = m * 8 + kg
                    if gi >= 2:
                        tensor.wait_ge(s_ev, gi - 1)
                    tb = ptr[gi % 2]
                    for j in range(4):
                        k = kg * 4 + j
                        ins = tensor.transpose(tb[:, j * 128:(j + 1) * 128],
                                               xhat[m % 2][:, k * 128:(k + 1) * 128],
                                               id_bf[:, :])
                    ins.then_inc(s_tp, 1)

            def emit_qk_transposes(j):
                jnn, jm = divmod(j, NM)
                if SCHED[jnn][0] == "v":
                    return
                gidx = P0_GRPS + QK_ORD[jnn] * NM + jm
                tensor.wait_ge(s_bk, j + 1)
                if gidx >= 2:
                    tensor.wait_ge(s_ev, gidx - 1)
                tb = ptr[gidx % 2]
                for t in range(4):
                    ins = tensor.transpose(tb[:, t * 128:(t + 1) * 128],
                                           qring[j % 4][:, t * 128:(t + 1) * 128],
                                           id_bf[:, :])
                ins.then_inc(s_tp, 1)

            for i in range(NCH):
                nn, m = divmod(i, NM)
                tensor.wait_ge(s_w[nn % 2], 16 * (nn // 2 + 1))
                tensor.wait_ge(s_ev, 8 * (m + 1))
                if i >= 3:
                    tensor.wait_ge(s_bk, i - 2)
                bk = pacc[i % 3]
                for k in range(NK):
                    ins = tensor.matmul(bk[:, :],
                                        xhatT[:, k * RTOK + m * 128: k * RTOK + m * 128 + 128],
                                        wbuf[nn % 2][:, k * 512:(k + 1) * 512],
                                        start=(k == 0), stop=(k == NK - 1))
                ins.then_inc(s_mm, 1)
                if i >= 1:
                    emit_qk_transposes(i - 1)
            emit_qk_transposes(NCH - 1)

            tensor.wait_ge(s_ad, 22)
            tensor.transpose(pst[0:32, :], allst[:, 0:32], id_f[:, :]).then_inc(s_tp, 1)

        @block.vector
        def _(vector):
            vs = SelfSem(vector, s_vself)
            ins = vector.memset(neg8[:], -CLIP)
            vs.edge(ins)
            ins = vector.memset(eps_sb[:], EPS)
            vs.edge(ins)
            for m in range(NM):
                vector.wait_ge(s_x2, 16 * (m + 1))
                xm = xld[m % 2]
                vector.reduce_sum(stA[:, m:m + 1], xm[:], axis=AX.X)
                vector.scalar_tensor_tensor(sqj[:], xm[:], 1.0, xm[:], ALU.bypass,
                                            ALU.mult, accum_out=stB[:, m:m + 1]).then_inc(s_vd, 1)
                vector.wait_ge(s_ad, 2 * m + 1)
                vector.scalar_tensor_tensor(ln1st[:, 0:1], stB[:, m:m + 1], 1.0 / D,
                                            stV[:, m:m + 1], ALU.mult, ALU.subtract).then_inc(s_vd, 1)
            ins = vector.nop()
            vs.edge(ins)
            for i in range(NCH):
                nn, m = divmod(i, NM)
                kind, a = SCHED[nn]
                vector.wait_ge(s_mm, i + 1)
                if kind == "v":
                    par = V_ORD[nn] % 2
                    if V_ORD[nn] >= 2 and m == 0:
                        vector.wait_ge(s_vsp[V_ORD[nn] % 2], 64 * (V_ORD[nn] // 2))
                    vector.scalar_tensor_tensor(vring[par][:, m * 512:(m + 1) * 512],
                                                pacc[i % 3][:, :], CLIP, neg8[:],
                                                ALU.min, ALU.max).then_inc(s_bk, 1)
                else:
                    if i >= 4:
                        j = i - 4
                        jnn = j // NM
                        if SCHED[jnn][0] != "v":
                            vector.wait_ge(s_tp, P0_GRPS + QK_ORD[jnn] * NM + (j % NM) + 1)
                    col = (0 if kind == "q" else 64) + a * 8 + m
                    vector.scalar_tensor_tensor(qring[i % 4][:, :], pacc[i % 3][:, :],
                                                CLIP, neg8[:], ALU.min, ALU.max,
                                                accum_out=psums[:, col:col + 1]).then_inc(s_bk, 1)
                j = i - 1
                if j >= 0 and SCHED[j // NM][0] != "v":
                    jnn, jm = divmod(j, NM)
                    jcol = (0 if SCHED[jnn][0] == "q" else 64) + SCHED[jnn][1] * 8 + jm
                    vector.wait_ge(s_bk, j + 1)
                    vector.scalar_tensor_tensor(sqj[:, (jm % 8) * 512:(jm % 8) * 512 + 512], qring[j % 4][:, :], 1.0,
                                                qring[j % 4][:, :], ALU.bypass, ALU.mult,
                                                accum_out=psumsq[:, jcol:jcol + 1])
            j = NCH - 1
            if SCHED[j // NM][0] != "v":
                jnn, jm = divmod(j, NM)
                jcol = (0 if SCHED[jnn][0] == "q" else 64) + SCHED[jnn][1] * 8 + jm
                vector.wait_ge(s_bk, j + 1)
                ins = vector.scalar_tensor_tensor(sqj[:, (jm % 8) * 512:(jm % 8) * 512 + 512], qring[j % 4][:, :], 1.0,
                                                  qring[j % 4][:, :], ALU.bypass, ALU.mult,
                                                  accum_out=psumsq[:, jcol:jcol + 1])
            else:
                ins = vector.nop()
            vs.edge(ins)

            for qi, off in ((0, 0), (1, 64)):
                if qi == 1:
                    vector.wait_ge(s_ad, 16 + 3)
                for m in range(NM):
                    vector.reduce_sum(stA[:, m:m + 1], psums[:, off + m: off + 64: 8], axis=AX.X)
                    ins = vector.reduce_sum(stB[:, m:m + 1], psumsq[:, off + m: off + 64: 8], axis=AX.X)
                ins.then_inc(s_vd, 1)
                vector.wait_ge(s_ad, 16 + 3 * qi + 1)
                vector.scalar_tensor_tensor(stV[:, 0:8], stB[:, 0:8], 1.0 / D,
                                            stV[:, 0:8], ALU.mult, ALU.subtract).then_inc(s_vd, 1)
                vector.wait_ge(s_ad, 16 + 3 * qi + 2)
                rc = slice(0, 8) if qi == 0 else slice(16, 24)
                vector.tensor_tensor(stB[:, 0:8], stM[:, 0:8], allst[:, rc],
                                     ALU.mult).then_inc(s_vd, 1)
            vector.wait_ge(s_tp, STATS_TP + 1)
            vector.tensor_copy(st32[:, :], pst[0:32, :]).then_inc(s_vd, 23 - 22)

        @block.scalar
        def _(scalar):
            asf = SelfSem(scalar, s_aself)
            for m in range(NM):
                scalar.wait_ge(s_vd, 2 * m + 1)
                ins = scalar.mul(stM[:, m:m + 1], stA[:, m:m + 1], 1.0 / D)
                asf.edge(ins)
                scalar.activation(stV[:, m:m + 1], stM[:, m:m + 1], AF.Square).then_inc(s_ad, 1)
                scalar.wait_ge(s_vd, 2 * m + 2)
                ins = scalar.activation(ln1st[:, 1:2], ln1st[:, 0:1], AF.Ln, bias=eps_sb[:, 0:1])
                asf.edge(ins)
                ins = scalar.activation(ln1st[:, 1:2], ln1st[:, 1:2], AF.Exp, scale=-0.5)
                asf.edge(ins)
                ins = scalar.activation(ln1st[:, 0:1], stM[:, m:m + 1], AF.Identity, scale=ln1st[:, 1:2])
                asf.edge(ins)
                ins = scalar.mul(ln1st[:, 0:1], ln1st[:, 0:1], -1.0)
                asf.edge(ins)
                if m >= 1:
                    for kg in range(8):
                        gi = (m - 1) * 8 + kg
                        scalar.wait_ge(s_tp, gi + 1)
                        k0 = kg * 4
                        dst = xhatT.ap().rearrange("p (k t) -> p k t", k=NK, t=RTOK)[
                            :, k0:k0 + 4, (m - 1) * 128:m * 128]
                        src = ptr[gi % 2][:, 0:512].rearrange("p (k t) -> p k t", k=4, t=128)
                        scalar.copy(dst, src).then_inc(s_ev, 1)
                    scalar.wait_ge(s_tp, 8 * m)
                scalar.activation(xhat[m % 2][:], xld[m % 2][:], AF.Identity,
                                  scale=ln1st[:, 1:2], bias=ln1st[:, 0:1]).then_inc(s_ad, 1)
                if m == NM - 1:
                    for kg in range(8):
                        gi = m * 8 + kg
                        scalar.wait_ge(s_tp, gi + 1)
                        k0 = kg * 4
                        dst = xhatT.ap().rearrange("p (k t) -> p k t", k=NK, t=RTOK)[
                            :, k0:k0 + 4, m * 128:(m + 1) * 128]
                        src = ptr[gi % 2][:, 0:512].rearrange("p (k t) -> p k t", k=4, t=128)
                        scalar.copy(dst, src).then_inc(s_ev, 1)
            for i in range(NCH):
                nn, m = divmod(i, NM)
                kind, a = SCHED[nn]
                if kind == "v":
                    continue
                qko = QK_ORD[nn]
                gidx = P0_GRPS + qko * NM + m
                scalar.wait_ge(s_tp, gidx + 1)
                par = qko % 2
                if qko >= 2 and m == 0:
                    scalar.wait_ge(s_tsp[qko % 2], 16 * (qko // 2))
                hl_all = tring[par].ap().rearrange("p (hl t) -> p hl t", hl=4, t=1024)
                dst = hl_all[:, :, m * 128:(m + 1) * 128]
                src = ptr[gidx % 2][:, 0:512].rearrange("p (hl t) -> p hl t", hl=4, t=128)
                scalar.copy(dst, src).then_inc(s_ev, 1)
            for qi in range(2):
                scalar.wait_ge(s_vd, 16 + 3 * qi + 1)
                ins = scalar.mul(stM[:, 0:8], stA[:, 0:8], 1.0 / D)
                asf.edge(ins)
                scalar.activation(stV[:, 0:8], stM[:, 0:8], AF.Square).then_inc(s_ad, 1)
                scalar.wait_ge(s_vd, 16 + 3 * qi + 2)
                rc = slice(0, 8) if qi == 0 else slice(16, 24)
                nbc = slice(8, 16) if qi == 0 else slice(24, 32)
                ins = scalar.activation(allst[:, rc], stV[:, 0:8], AF.Ln, bias=eps_sb[:, 0:1])
                asf.edge(ins)
                scalar.activation(allst[:, rc], allst[:, rc], AF.Exp,
                                  scale=-0.5).then_inc(s_ad, 1)
                scalar.wait_ge(s_vd, 16 + 3 * qi + 3)
                scalar.mul(allst[:, nbc], stB[:, 0:8], -1.0).then_inc(s_ad, 1)


        for g_ in reversed(ctx[ctx_p1_base:ctx_p1_end]):
            g_.__exit__(None, None, None)
        del ctx[ctx_p1_base:ctx_p1_end]

        ctx_p3_base = len(ctx)
        dtab_sb = sb("dtab_sb", [128, NDTAB * 128], F32)
        rrep_k = sb("rrep_k", [128, 2048], F32)
        nbrep_k = sb("nbrep_k", [128, 2048], F32)
        rrep_q = sb("rrep_q", [128, 1024], F32)
        nbrep_q = sb("nbrep_q", [128, 1024], F32)
        krow = sb("krow", [1, 2048], F32)
        knbrow = sb("knbrow", [1, 2048], F32)
        qrow = sb("qrow", [1, 1024], F32)
        qnbrow = sb("qnbrow", [1, 1024], F32)
        ones_f = sb("ones_f", [1, 128], F32)
        ones_b = sb("ones_b", [1, 128], BF16)
        qraw = [sb(f"qraw{j}", [128, 1024], BF16) for j in range(2)]
        kraw = [sb(f"kraw{j}", [128, 2048], BF16) for j in range(2)]
        v_sb = [sb(f"v_sb{j}", [128, 2048], BF16) for j in range(2)]
        scr = [sb(f"scr{j}", [128, 3072], F32) for j in range(2)]
        qhat = [sb(f"qhat{j}", [128, 1024], BF16) for j in range(2)]
        khat = [sb(f"khat{j}", [128, 2048], BF16) for j in range(2)]
        scores = [sb(f"scores{j}", [128, 2048], F32) for j in range(2)]
        probs = [sb(f"probs{j}", [128, 2048], BF16) for j in range(2)]
        probsT = [sb(f"probsT{j}", [128, 2048], BF16) for j in range(2)]
        negmax = [sb(f"negmax{j}", [128, 8], F32) for j in range(2)]
        Zcols = [sb(f"Zcols{j}", [128, 8], F32) for j in range(2)]
        recipZ = sb("recipZ", [128, 8], F32)
        recipZb = sb("recipZb", [128, 8], BF16)
        zb_sb = sb("zb_sb", [8, 128], BF16)
        zrow1 = [sb(f"zrow1{j}", [1, 1024], BF16) for j in range(2)]
        attnraw = [sb(f"attnraw{j}", [128, 1024], F32) for j in range(2)]
        reciprep = [sb(f"reciprep{j}", [128, 1024], F32) for j in range(2)]
        attnT_sb = [sb(f"attnT{j}", [128, 1024], BF16) for j in range(2)]

        psc = [psum(f"psc{j}", [128, 512], F32) for j in range(2)]
        ptp = [psum(f"ptp{j}", [128, 1024], BF16) for j in range(2)]
        av_ps = psum("av_ps", [128, 1024], F32)
        rp_ps = psum("rp_ps", [128, 1024], F32)
        ctx_p3_end = len(ctx)

        s3_pl = sem("s3_pl")
        s3_ld = [sem("s3_ld0"), sem("s3_ld1")]
        s3_mm3 = sem("s3_mm3")
        s3_bk3 = sem("s3_bk3")
        s3_tp3 = sem("s3_tp3")
        s3_te = sem("s3_te")
        s3_ex = sem("s3_ex")
        s3_mx = sem("s3_mx")
        s3_avs = sem("s3_avs")
        s3_ave = sem("s3_ave")
        s3_vd3 = sem("s3_vd3")
        s3_pd3 = sem("s3_pd3")
        s3_ak3 = sem("s3_ak3")
        s3_zr = sem("s3_zr")
        s3_zc = sem("s3_zc")
        s3_zt = sem("s3_zt")
        s3_ze = sem("s3_ze")
        s3_zd = sem("s3_zd")
        s3_zm = sem("s3_zm")
        s3_rp = sem("s3_rp")
        s3_nrm = sem("s3_nrm")
        s3_sp3 = [sem("s3_sp3a"), sem("s3_sp3b")]
        s_vs3 = sem("s_vs3")
        s_as3 = sem("s_as3")

        @block.sync
        def _(sync):
            sync.wait_ge(s_mm, NCH)
            sync.wait_ge(s_vd, 23)
            sync.wait_ge(s_ad, 22)
            sync.dma_start(out=dtab_sb.ap().rearrange("p (i j) -> p i j", i=NDTAB, j=128),
                           in_=dtab_in.rearrange("i p j -> p i j")).then_inc(s3_pl, 16)
            sync.wait_ge(s_ccs, 1)
            srows = sout.rearrange("(r t u) p -> t u r p", r=2, t=2, u=8)
            for ri in range(2):
                sync.dma_start(out=krow.ap().rearrange("o (u r p) -> o u r p", u=8, r=2, p=128)[:, :, ri, :],
                               in_=srows[0:1, :, ri, :]).then_inc(s3_pl, 16)
                sync.dma_start(out=knbrow.ap().rearrange("o (u r p) -> o u r p", u=8, r=2, p=128)[:, :, ri, :],
                               in_=srows[1:2, :, ri, :]).then_inc(s3_pl, 16)
            qsf = qs_d.rearrange("r p -> (r p)")
            sync.dma_start(out=qrow[0:1, :], in_=qsf[0:1024].rearrange("(o c) -> o c", o=1)).then_inc(s3_pl, 16)
            sync.dma_start(out=qnbrow[0:1, :], in_=qsf[1024:2048].rearrange("(o c) -> o c", o=1)).then_inc(s3_pl, 16)
            sync.wait_ge(s_tsp[0], 16 * 8)
            sync.wait_ge(s_tsp[1], 16 * 8)
            for h in range(H):
                g = h // 8
                hl = h % 8
                if h >= 2:
                    sync.wait_ge(s3_vd3, 2 * (h - 2) + 2)
                    sync.wait_ge(s3_avs, 8 * (h - 1))
                sync.wait_ge(s_ccg[g], 1)
                sync.dma_start(out=qraw[h % 2][:], in_=qT_d[h * 128:(h + 1) * 128, :]).then_inc(s3_ld[h % 2], 16)
                kdst = kraw[h % 2].ap().rearrange("p (u r j) -> p u r j", u=8, r=2, j=128)
                vdst = v_sb[h % 2].ap().rearrange("p (u r d) -> p u r d", u=8, r=2, d=128)
                cfl = cout[g].rearrange("r c -> (r c)")
                for r in range(2):
                    ksrc = cout[g][r * 2048 + hl * 128: r * 2048 + (hl + 1) * 128, :]
                    sync.dma_start(out=kdst[:, :, r, :],
                                   in_=ksrc.rearrange("d (u j) -> d u j", u=8, j=128)).then_inc(s3_ld[h % 2], 16)
                    vbase = r * 2097152 + 1048576 + hl * 131072
                    vsrc = cfl[vbase: vbase + 131072].rearrange("(u p d) -> p u d", u=8, p=128, d=128)
                    sync.dma_start(out=vdst[:, :, r, :], in_=vsrc).then_inc(s3_ld[h % 2], 16)
                if h >= 1:
                    j = h - 1
                    sync.wait_ge(s3_ze, j + 1)
                    if j >= 1:
                        sync.wait_ge(s3_zd, 32 * j)
                    sync.dma_start(out=zb_d[:, :], in_=zb_sb[:, :]).then_inc(s3_zd, 16)
                    sync.wait_ge(s3_zd, 32 * j + 16)
                    if j >= 2:
                        sync.wait_ge(s3_zm, j - 1)
                    sync.dma_start(out=zrow1[j % 2][0:1, :],
                                   in_=zb_d.rearrange("r p -> (r p)").rearrange("(o c) -> o c", o=1)).then_inc(s3_zd, 16)
                if h >= 2:
                    j = h - 2
                    sync.wait_ge(s3_nrm, j + 1)
                    sync.dma_start(out=aT_d[j * 128:(j + 1) * 128, :], in_=attnT_sb[j % 2][:]).then_inc(s3_sp3[j % 2], 16)
            j = H - 1
            sync.wait_ge(s3_ze, j + 1)
            sync.wait_ge(s3_zd, 32 * j)
            sync.dma_start(out=zb_d[:, :], in_=zb_sb[:, :]).then_inc(s3_zd, 16)
            sync.wait_ge(s3_zd, 32 * j + 16)
            sync.wait_ge(s3_zm, j - 1)
            sync.dma_start(out=zrow1[j % 2][0:1, :],
                           in_=zb_d.rearrange("r p -> (r p)").rearrange("(o c) -> o c", o=1)).then_inc(s3_zd, 16)
            for j in (H - 2, H - 1):
                sync.wait_ge(s3_nrm, j + 1)
                sync.dma_start(out=aT_d[j * 128:(j + 1) * 128, :], in_=attnT_sb[j % 2][:]).then_inc(s3_sp3[j % 2], 16)
            if debug:
                sync.wait_ge(s3_sp3[0], 16 * (H // 2))
                sync.wait_ge(s3_sp3[1], 16 * (H // 2))
                sync.dma_start(out=dbg["d_aT"][:, :], in_=aT_d[:, :]).then_inc(s_dbg, 16)
                sync.wait_ge(s_dbg, 16 * 8)

        @block.tensor
        def _(tensor):
            tensor.wait_ge(s3_pl, 112)
            tensor.wait_ge(s_vd, 24)
            mmi = 0
            for row_sb, width in ((krow, 2048), (knbrow, 2048), (qrow, 1024), (qnbrow, 1024)):
                for c in range(width // 512):
                    if mmi >= 2:
                        tensor.wait_ge(s3_bk3, mmi - 1)
                    tensor.matmul(psc[mmi % 2][:, 0:512], ones_f[0:1, :],
                                  row_sb[0:1, c * 512:(c + 1) * 512],
                                  start=True, stop=True).then_inc(s3_mm3, 1)
                    mmi += 1
            assert mmi == MMB

            for h in range(H):
                if h >= 1:
                    jz = h - 1
                    tensor.wait_ge(s3_zc, jz + 1)
                    tensor.wait_ge(s3_te, GPH * h - 1)
                    tensor.wait_ge(s3_ze, jz)
                    tensor.transpose(ptp[0][0:8, 0:128], recipZb[:, 0:8], id_bf[:, :]).then_inc(s3_zt, 1)
                tensor.wait_ge(s3_ak3, 2 * h + 2)
                if h >= 1:
                    tensor.wait_ge(s3_ave, h)

                def emit_scores(s):
                    sg = 8 * h + s
                    off = 0
                    for c, w in enumerate(CHW[s]):
                        pidx = MMB + CPH * h + preC[s] + c
                        if pidx >= 2:
                            tensor.wait_ge(s3_bk3, pidx - 1)
                        tensor.matmul(psc[pidx % 2][:, 0:w],
                                      qhat[h % 2][:, s * 128:(s + 1) * 128],
                                      khat[h % 2][:, off:off + w],
                                      start=True, stop=True).then_inc(s3_mm3, 1)
                        off += w

                def emit_tpav(s):
                    sg = 8 * h + s
                    tensor.wait_ge(s3_ex, sg + 1)
                    if s == 0:
                        tensor.wait_ge(s3_ze, h)
                    for gi, (t0, nt) in enumerate(GROUPS(s)):
                        gg = GPH * h + preG[s] + gi
                        if gg >= 2:
                            tensor.wait_ge(s3_te, gg - 1)
                        tb = ptp[gg % 2]
                        for t in range(t0, t0 + nt):
                            ins = tensor.transpose(tb[:, (t - t0) * 128:(t - t0 + 1) * 128],
                                                   probs[sg % 2][:, t * 128:(t + 1) * 128],
                                                   id_bf[:, :])
                        ins.then_inc(s3_tp3, 1)
                        tensor.wait_ge(s3_te, gg + 1)
                        for t in range(t0, t0 + nt):
                            ins = tensor.matmul(av_ps[:, s * 128:(s + 1) * 128],
                                                v_sb[h % 2][:, t * 128:(t + 1) * 128],
                                                probsT[sg % 2][:, t * 128:(t + 1) * 128],
                                                start=(s % 4 == 0 and t == 0),
                                                stop=(s % 4 == 3 and t == LS[s] - 1))
                        if t0 + nt == LS[s]:
                            ins.then_inc(s3_avs, 1)

                emit_scores(0)
                emit_scores(1)
                if h >= 2:
                    jz = h - 2
                    tensor.wait_ge(s3_zd, 32 * jz + 32)
                    tensor.wait_ge(s3_rp, jz)
                    for s8 in range(8):
                        ins = tensor.matmul(rp_ps[:, s8 * 128:(s8 + 1) * 128], ones_b[0:1, :],
                                            zrow1[jz % 2][0:1, s8 * 128:(s8 + 1) * 128],
                                            start=(s8 % 4 == 0), stop=(s8 % 4 == 3))
                    ins.then_inc(s3_zm, 1)
                for s in range(8):
                    if s + 2 < 8:
                        emit_scores(s + 2)
                    emit_tpav(s)
            jz = H - 1
            tensor.wait_ge(s3_zc, jz + 1)
            tensor.wait_ge(s3_te, GPH * H - 1)
            tensor.wait_ge(s3_ze, jz)
            tensor.transpose(ptp[0][0:8, 0:128], recipZb[:, 0:8], id_bf[:, :]).then_inc(s3_zt, 1)
            for jz in (H - 2, H - 1):
                tensor.wait_ge(s3_zd, 32 * jz + 32)
                tensor.wait_ge(s3_rp, jz)
                for s8 in range(8):
                    ins = tensor.matmul(rp_ps[:, s8 * 128:(s8 + 1) * 128], ones_b[0:1, :],
                                        zrow1[jz % 2][0:1, s8 * 128:(s8 + 1) * 128],
                                        start=(s8 % 4 == 0), stop=(s8 % 4 == 3))
                ins.then_inc(s3_zm, 1)

        @block.vector
        def _(vector):
            vs3 = SelfSem(vector, s_vs3)
            vector.memset(ones_f[:, :], 1.0)
            vector.memset(ones_b[:, :], 1.0).then_inc(s_vd, 1)
            dsts = ([(rrep_k, c) for c in range(4)] + [(nbrep_k, c) for c in range(4)]
                    + [(rrep_q, c) for c in range(2)] + [(nbrep_q, c) for c in range(2)])
            for mmi, (dst, c) in enumerate(dsts):
                vector.wait_ge(s3_mm3, mmi + 1)
                vector.tensor_copy(dst.ap()[:, c * 512:(c + 1) * 512], psc[mmi % 2][:, 0:512]).then_inc(s3_bk3, 1)

            def emit_tt(hh):
                vector.wait_ge(s3_ld[hh % 2], 80 * (hh // 2 + 1))
                if hh >= 2:
                    vector.wait_ge(s3_ak3, 2 * (hh - 2) + 2)
                vector.tensor_tensor(scr[hh % 2][:, 0:1024], qraw[hh % 2][:], rrep_q[:],
                                     ALU.mult).then_inc(s3_vd3, 1)
                vector.tensor_tensor(scr[hh % 2][:, 1024:3072], kraw[hh % 2][:], rrep_k[:],
                                     ALU.mult).then_inc(s3_vd3, 1)

            emit_tt(0)
            for h in range(H):
                if h >= 1:
                    j = h - 1
                    vector.wait_ge(s3_avs, 8 * (j + 1))
                    if j >= 2:
                        vector.wait_ge(s3_nrm, j - 1)
                    vector.tensor_copy(attnraw[j % 2][:], av_ps[:, 0:1024]).then_inc(s3_ave, 1)
                    vector.wait_ge(s3_zt, j + 1)
                    if j >= 1:
                        vector.wait_ge(s3_zd, 32 * (j - 1) + 16)
                    vector.tensor_copy(zb_sb[0:8, :], ptp[0][0:8, 0:128]).then_inc(s3_ze, 1)
                def emit_chunks_max(s):
                    sg = 8 * h + s
                    if sg >= 2:
                        vector.wait_ge(s3_ex, sg - 1)
                    off = 0
                    for c, w in enumerate(CHW[s]):
                        pidx = MMB + CPH * h + preC[s] + c
                        vector.wait_ge(s3_mm3, pidx + 1)
                        vector.scalar_tensor_tensor(scores[sg % 2][:, off:off + w],
                                                    dtab_sb[:, TSTART[s] * 128 + off: TSTART[s] * 128 + off + w],
                                                    SLOPES[h], psc[pidx % 2][:, 0:w],
                                                    ALU.mult, ALU.add).then_inc(s3_bk3, 1)
                        off += w
                    if h >= 2:
                        vector.wait_ge(s3_ex, 8 * (h - 2) + s + 1)
                    vector.wait_ge(s3_bk3, MMB + CPH * h + preC[s + 1])
                    vector.reduce_max(negmax[h % 2][:, s:s + 1], scores[sg % 2][:, 0:LS[s] * 128],
                                      axis=AX.X, negate=True).then_inc(s3_mx, 1)

                def emit_psT(s):
                    sg = 8 * h + s
                    for gi, (t0, nt) in enumerate(GROUPS(s)):
                        gg = GPH * h + preG[s] + gi
                        vector.wait_ge(s3_tp3, gg + 1)
                        if sg >= 2:
                            vector.wait_ge(s3_avs, sg - 1)
                        vector.tensor_copy(probsT[sg % 2][:, t0 * 128:(t0 + nt) * 128],
                                           ptp[gg % 2][:, 0:nt * 128]).then_inc(s3_te, 1)

                emit_chunks_max(0)
                emit_chunks_max(1)
                for s in range(8):
                    if s + 2 < 8:
                        emit_chunks_max(s + 2)
                    emit_psT(s)
                    if s == 0 and h >= 2:
                        jz = h - 2
                        vector.wait_ge(s3_zm, jz + 1)
                        if jz >= 2:
                            vector.wait_ge(s3_nrm, jz - 1)
                        vector.tensor_copy(reciprep[jz % 2][:], rp_ps[:, 0:1024]).then_inc(s3_rp, 1)
                vector.wait_ge(s3_ex, 8 * h + 8)
                if h >= 1:
                    vector.wait_ge(s3_zc, h)
                vector.reciprocal(recipZ[:, 0:8], Zcols[h % 2][:, 0:8]).then_inc(s3_zr, 1)
                if h + 1 < H:
                    emit_tt(h + 1)
            j = H - 1
            vector.wait_ge(s3_avs, 8 * (j + 1))
            vector.wait_ge(s3_nrm, j - 1)
            vector.tensor_copy(attnraw[j % 2][:], av_ps[:, 0:1024]).then_inc(s3_ave, 1)
            vector.wait_ge(s3_zt, j + 1)
            vector.wait_ge(s3_zd, 32 * (j - 1) + 16)
            vector.tensor_copy(zb_sb[0:8, :], ptp[0][0:8, 0:128]).then_inc(s3_ze, 1)
            for jz in (H - 2, H - 1):
                vector.wait_ge(s3_zm, jz + 1)
                vector.wait_ge(s3_nrm, jz - 1)
                vector.tensor_copy(reciprep[jz % 2][:], rp_ps[:, 0:1024]).then_inc(s3_rp, 1)

        @block.gpsimd
        def _(gpsimd):
            for h in range(H):
                gpsimd.wait_ge(s3_vd3, 2 * h + 1)
                gpsimd.tensor_tensor(scr[h % 2][:, 0:1024], scr[h % 2][:, 0:1024],
                                     nbrep_q[:], ALU.add).then_inc(s3_pd3, 1)
                gpsimd.wait_ge(s3_vd3, 2 * h + 2)
                gpsimd.tensor_tensor(scr[h % 2][:, 1024:3072], scr[h % 2][:, 1024:3072],
                                     nbrep_k[:], ALU.add).then_inc(s3_pd3, 1)
                if h >= 2:
                    j = h - 2
                    gpsimd.wait_ge(s3_rp, j + 1)
                    gpsimd.wait_ge(s3_ave, j + 1)
                    if j >= 2:
                        gpsimd.wait_ge(s3_sp3[j % 2], 16 * (j // 2))
                    gpsimd.tensor_tensor(attnT_sb[j % 2][:], attnraw[j % 2][:],
                                         reciprep[j % 2][:], ALU.mult).then_inc(s3_nrm, 1)
            for j in (H - 2, H - 1):
                gpsimd.wait_ge(s3_rp, j + 1)
                gpsimd.wait_ge(s3_ave, j + 1)
                gpsimd.wait_ge(s3_sp3[j % 2], 16 * (j // 2))
                gpsimd.tensor_tensor(attnT_sb[j % 2][:], attnraw[j % 2][:],
                                     reciprep[j % 2][:], ALU.mult).then_inc(s3_nrm, 1)

        @block.scalar
        def _(scalar):
            as3 = SelfSem(scalar, s_as3)
            for h in range(H):
                scalar.wait_ge(s3_pd3, 2 * h + 1)
                if h >= 2:
                    scalar.wait_ge(s3_mm3, MMB + CPH * (h - 1))
                scalar.activation(qhat[h % 2][:], scr[h % 2][:, 0:1024], AF.Identity,
                                  scale=lnp_sb[:, h:h + 1],
                                  bias=lnp_sb[:, 32 + h:33 + h]).then_inc(s3_ak3, 1)
                scalar.wait_ge(s3_pd3, 2 * h + 2)
                scalar.activation(khat[h % 2][:], scr[h % 2][:, 1024:3072], AF.Identity,
                                  scale=lnp_sb[:, 64 + h:65 + h],
                                  bias=lnp_sb[:, 96 + h:97 + h]).then_inc(s3_ak3, 1)
                for s in range(8):
                    sg = 8 * h + s
                    scalar.wait_ge(s3_mx, sg + 1)
                    if sg >= 2:
                        scalar.wait_ge(s3_tp3, GG_END(sg - 2))
                    if s == 0 and h >= 1:
                        scalar.wait_ge(s3_zr, h)
                    scalar.activation(probs[sg % 2][:, 0:LS[s] * 128],
                                      scores[sg % 2][:, 0:LS[s] * 128], AF.Exp,
                                      bias=negmax[h % 2][:, s:s + 1],
                                      accum_out=Zcols[h % 2][:, s:s + 1]).then_inc(s3_ex, 1)
                scalar.wait_ge(s3_zr, h + 1)
                if h >= 1:
                    scalar.wait_ge(s3_zt, h)
                scalar.copy(recipZb[:, 0:8], recipZ[:, 0:8]).then_inc(s3_zc, 1)


        for g_ in reversed(ctx[ctx_p3_base:ctx_p3_end]):
            g_.__exit__(None, None, None)
        del ctx[ctx_p3_base:ctx_p3_end]

        NG4 = 4
        proj4 = sb("proj4", [128, 2 * D], F32)
        x2_4 = sb("x2_4", [128, 2 * D], F32)
        bigT = sb("bigT", [128, NK * 256], BF16)
        w4buf = [sb(f"w4buf{j}", [128, NK * 256], BF16) for j in range(2)]
        xld4 = sb("xld4", [128, D], F32)
        x3h4 = sb("x3h4", [128, D], BF16)
        outrow = sb("outrow", [128, D], F32)
        at_sb = sb("at_sb", [128, NK * 128], BF16)
        bt_sb = sb("bt_sb", [128, 16 * 256], BF16)
        rt_sb = sb("rt_sb", [128, NK * 8], BF16)
        brow = sb("brow", [1, D], BF16)
        ones4 = sb("ones4", [1, 128], BF16)
        hw4 = sb("hw4", [128, 128], BF16)
        hwT4 = sb("hwT4", [128, 2 * 128], BF16)
        h4f = sb("h4f", [128, 128], F32)
        r8 = sb("r8", [128, 8], F32)
        e8 = sb("e8", [128, 8], F32)
        nm8 = sb("nm8", [128, 1], F32)
        t8 = sb("t8", [128, 8], F32)
        g8 = sb("g8", [128, 8], F32)
        m1c = sb("m1c", [128, 1], F32)
        m2c = sb("m2c", [128, 1], F32)
        rc8 = sb("rc8", [128, 1], F32)
        comb = sb("comb", [128, 8], F32)
        st4A = sb("st4A", [128, 2], F32)
        st4B = sb("st4B", [128, 2], F32)
        st4M = sb("st4M", [128, 2], F32)
        st4V = sb("st4V", [128, 2], F32)
        ln4st = sb("ln4st", [128, 4], F32)
        sqj4 = sb("sqj4", [128, D], BF16)

        pac4 = [psum(f"pac4{j}", [128, 512], F32) for j in range(4)]
        ptr4 = [psum(f"ptr4{j}", [128, 1024], BF16) for j in range(2)]
        prt4 = psum("prt4", [128, 512], F32)
        ph4 = psum("ph4", [128, 512], F32)

        s4_w = [sem("s4_w0"), sem("s4_w1")]
        s4_wp = sem("s4_wp")
        s4_at = sem("s4_at")
        s4_x = sem("s4_x")
        s4_mo = sem("s4_mo")
        s4_bo = sem("s4_bo")
        s4_mf = sem("s4_mf")
        s4_bf = sem("s4_bf")
        s4_mr = sem("s4_mr")
        s4_br = sem("s4_br")
        s4_mh = sem("s4_mh")
        s4_bh = sem("s4_bh")
        s4_tx = sem("s4_tx")
        s4_te = sem("s4_te")
        s4_vd = sem("s4_vd")
        s4_ad = sem("s4_ad")
        s4_vr = sem("s4_vr")
        s4_ar = sem("s4_ar")
        s4_cb = sem("s4_cb")
        s4_hw = sem("s4_hw")
        s_out = sem("s_out")
        s_vs4 = sem("s_vs4")
        s_as4 = sem("s_as4")

        CH4 = []
        for G4 in range(NG4):
            for typ in (2, 3, 4):
                for mloc in range(2):
                    CH4.append((typ, G4, mloc))

        def ch4_base(idx):
            vd = sum(3 if CH4[i][0] != 3 else 2 for i in range(idx))
            ad = 2 * idx
            return vd, ad

        def tx_ord(G4, kind, mloc, grp=0):
            base = 18 * G4
            if kind == "x3":
                return base + mloc * 8 + grp
            return base + 16 + mloc

        @block.sync
        def _(sync):
            sync.wait_ge(s3_zm, H)
            sync.wait_ge(s3_rp, H)
            sync.wait_ge(s3_zc, H)
            sync.wait_ge(s3_nrm, H)
            sync.wait_ge(s3_sp3[0], 16 * (H // 2))
            sync.wait_ge(s3_sp3[1], 16 * (H // 2))
            for G4 in range(NG4):
                if G4 >= 1:
                    sync.wait_ge(s4_mf, 32 * G4)
                sync.dma_start(
                    out=bigT.ap().rearrange("p (k j) -> p k j", k=NK, j=256),
                    in_=aT_d.rearrange("(k d) t -> d k t", k=NK, d=128)[:, :, G4 * 256:(G4 + 1) * 256],
                ).then_inc(s4_at, 16)
                for mloc in range(2):
                    m = 2 * G4 + mloc
                    ci = CH4.index((2, G4, mloc))
                    vb, ab = ch4_base(ci)
                    if m >= 1:
                        pci = CH4.index((2, G4, mloc)) - 1
                        pvb, _ = ch4_base(pci)
                        sync.wait_ge(s4_vd, pvb + 3)
                    sync.dma_start(out=xld4[:], in_=x_in[m * 128:(m + 1) * 128, :]).then_inc(s4_x, 16)
                for mloc in range(2):
                    m = 2 * G4 + mloc
                    ci = CH4.index((4, G4, mloc))
                    vb, ab = ch4_base(ci)
                    sync.wait_ge(s4_vd, vb + 3)
                    sync.dma_start(out=out[m * 128:(m + 1) * 128, :], in_=outrow[:]).then_inc(s_out, 16)
            sync.wait_ge(s_out, 16 * NM)

        @block.gpsimd
        def _(gpsimd):
            gpsimd.wait_ge(s3_nrm, H)
            gpsimd.dma_start(out=at_sb.ap().rearrange("p (k n) -> p k n", k=NK, n=128),
                             in_=wa_in).then_inc(s4_wp, 16)
            gpsimd.dma_start(out=bt_sb.ap().rearrange("p (k n) -> p k n", k=16, n=256),
                             in_=wb_in).then_inc(s4_wp, 16)
            gpsimd.dma_start(out=rt_sb.ap().rearrange("p (k n) -> p k n", k=NK, n=8),
                             in_=wr_in).then_inc(s4_wp, 16)
            gpsimd.dma_start(out=brow[:, :], in_=bb_in[:, :]).then_inc(s4_wp, 16)
            for G4 in range(NG4):
                for c4 in range(32):
                    widx = 32 * G4 + c4
                    src = wout_in[c4] if c4 < 16 else wbase_in[c4 - 16]
                    if widx >= 2:
                        prev = widx - 2
                        if (prev % 32) < 16:
                            gpsimd.wait_ge(s4_mo, 2 * ((prev % 32) + 1) + 32 * (prev // 32))
                        else:
                            gpsimd.wait_ge(s4_mf, 2 * ((prev % 32) - 16 + 1) + 32 * (prev // 32))
                    gpsimd.dma_start(
                        out=w4buf[widx % 2].ap().rearrange("p (k n) -> p k n", k=NK, n=256),
                        in_=src,
                    ).then_inc(s4_w[widx % 2], 16)

        @block.tensor
        def _(tensor):
            tensor.wait_ge(s4_wp, 64)
            for G4 in range(NG4):
                tensor.wait_ge(s4_at, 16 * (G4 + 1))
                for nn in range(16):
                    widx = 32 * G4 + nn
                    tensor.wait_ge(s4_w[widx % 2], 16 * (widx // 2 + 1))
                    for mloc in range(2):
                        oi = 32 * G4 + 2 * nn + mloc
                        if oi >= 4:
                            tensor.wait_ge(s4_bo, oi - 3)
                        bk = pac4[oi % 4]
                        for k in range(NK):
                            ins = tensor.matmul(bk[:, 0:256],
                                                bigT[:, k * 256 + mloc * 128: k * 256 + mloc * 128 + 128],
                                                w4buf[widx % 2][:, k * 256:(k + 1) * 256],
                                                start=(k == 0), stop=(k == NK - 1))
                        ins.then_inc(s4_mo, 1)
                for mloc in range(2):
                    ci = CH4.index((3, G4, mloc))
                    vb, ab = ch4_base(ci)
                    tensor.wait_ge(s4_ad, ab + 2)
                    for grp in range(8):
                        gi = tx_ord(G4, "x3", mloc, grp)
                        if gi >= 2:
                            tensor.wait_ge(s4_te, gi - 1)
                        tb = ptr4[gi % 2]
                        for j in range(4):
                            k = grp * 4 + j
                            ins = tensor.transpose(tb[:, j * 128:(j + 1) * 128],
                                                   x3h4[:, k * 128:(k + 1) * 128],
                                                   id_bf[:, :])
                        ins.then_inc(s4_tx, 1)
                for mloc in range(2):
                    if G4 + mloc >= 1:
                        tensor.wait_ge(s4_br, 2 * G4 + mloc)
                    tensor.wait_ge(s4_te, tx_ord(G4, "x3", mloc, 7) + 1 - 0)
                    for k in range(NK):
                        ins = tensor.matmul(prt4[:, 0:8],
                                            bigT[:, k * 256 + mloc * 128: k * 256 + mloc * 128 + 128],
                                            rt_sb[:, k * 8:(k + 1) * 8],
                                            start=(k == 0), stop=(k == NK - 1))
                    ins.then_inc(s4_mr, 1)
                    if G4 + mloc >= 1:
                        tensor.wait_ge(s4_bh, 2 * G4 + mloc)
                    for k in range(NK):
                        ins = tensor.matmul(ph4[:, 0:128],
                                            bigT[:, k * 256 + mloc * 128: k * 256 + mloc * 128 + 128],
                                            at_sb[:, k * 128:(k + 1) * 128],
                                            start=(k == 0), stop=(k == NK - 1))
                    ins.then_inc(s4_mh, 1)
                    gi = tx_ord(G4, "hw", mloc)
                    tensor.wait_ge(s4_hw, 2 * G4 + mloc + 1)
                    tensor.wait_ge(s4_te, gi - 1)
                    tensor.transpose(ptr4[gi % 2][:, 0:128], hw4[:, 0:128],
                                     id_bf[:, :]).then_inc(s4_tx, 1)
                for nn in range(16):
                    widx = 32 * G4 + 16 + nn
                    tensor.wait_ge(s4_w[widx % 2], 16 * (widx // 2 + 1))
                    for mloc in range(2):
                        oi = 32 * G4 + 2 * nn + mloc
                        if oi >= 4:
                            tensor.wait_ge(s4_bf, oi - 3)
                        tensor.wait_ge(s4_te, tx_ord(G4, "hw", mloc) + 1)
                        bk = pac4[oi % 4]
                        for k in range(NK):
                            tensor.matmul(bk[:, 0:256],
                                          bigT[:, k * 256 + mloc * 128: k * 256 + mloc * 128 + 128],
                                          w4buf[widx % 2][:, k * 256:(k + 1) * 256],
                                          start=(k == 0), stop=False)
                        tensor.matmul(bk[:, 0:256],
                                      hwT4[:, mloc * 128:(mloc + 1) * 128],
                                      bt_sb[:, nn * 256:(nn + 1) * 256],
                                      start=False, stop=False)
                        ins = tensor.matmul(bk[:, 0:256], ones4[0:1, :],
                                            brow[0:1, nn * 256:(nn + 1) * 256],
                                            start=False, stop=True)
                        ins.then_inc(s4_mf, 1)

        @block.vector
        def _(vector):
            vs4 = SelfSem(vector, s_vs4)
            ins = vector.memset(ones4[:, :], 1.0)
            vs4.edge(ins)

            def ln_chain_dve(typ, G4, mloc):
                ci = CH4.index((typ, G4, mloc))
                vb, ab = ch4_base(ci)
                if typ == 2:
                    rowf = proj4.ap()[:, mloc * D:(mloc + 1) * D]
                    vector.wait_ge(s4_bo, 32 * (G4 + 1))
                elif typ == 3:
                    rowf = x2_4.ap()[:, mloc * D:(mloc + 1) * D]
                    vb2, _ = ch4_base(CH4.index((2, G4, mloc)))
                    vector.wait_ge(s4_vd, vb2 + 3)
                else:
                    rowf = proj4.ap()[:, mloc * D:(mloc + 1) * D]
                    vector.wait_ge(s4_bf, 32 * (G4 + 1))
                pi = ci % 2
                vector.reduce_sum(st4A[:, pi:pi + 1], rowf, axis=AX.X)
                vector.scalar_tensor_tensor(sqj4[:], rowf, 1.0, rowf, ALU.bypass,
                                            ALU.mult, accum_out=st4B[:, pi:pi + 1]).then_inc(s4_vd, 1)
                vector.wait_ge(s4_ad, ab + 1)
                vector.scalar_tensor_tensor(ln4st[:, 2 * pi:2 * pi + 1], st4B[:, pi:pi + 1], 1.0 / D,
                                            st4V[:, pi:pi + 1], ALU.mult, ALU.subtract).then_inc(s4_vd, 1)
                if typ == 2:
                    vector.wait_ge(s4_ad, ab + 2)
                    vector.wait_ge(s4_x, 16 * (2 * G4 + mloc + 1))
                    vector.tensor_tensor(x2_4.ap()[:, mloc * D:(mloc + 1) * D], xld4[:],
                                         rowf, ALU.add).then_inc(s4_vd, 1)
                elif typ == 4:
                    vector.wait_ge(s4_ad, ab + 2)
                    if 2 * G4 + mloc >= 1:
                        vector.wait_ge(s_out, 16 * (2 * G4 + mloc))
                    vector.tensor_tensor(outrow[:], x2_4.ap()[:, mloc * D:(mloc + 1) * D],
                                         rowf, ALU.add).then_inc(s4_vd, 1)

            for G4 in range(NG4):
                for nn in range(16):
                    for mloc in range(2):
                        oi = 32 * G4 + 2 * nn + mloc
                        vector.wait_ge(s4_mo, oi + 1 - 32 * G4 + 32 * G4)
                        vector.tensor_copy(proj4.ap()[:, mloc * D + nn * 256: mloc * D + nn * 256 + 256],
                                           pac4[oi % 4][:, 0:256]).then_inc(s4_bo, 1)
                for mloc in range(2):
                    ln_chain_dve(2, G4, mloc)
                for mloc in range(2):
                    ln_chain_dve(3, G4, mloc)
                for mloc in range(2):
                    im = 2 * G4 + mloc
                    vector.wait_ge(s4_mr, im + 1)
                    if im >= 1:
                        vector.wait_ge(s4_ar, im)
                    vector.tensor_copy(r8[:, 0:8], prt4[:, 0:8]).then_inc(s4_br, 1)
                    vector.wait_ge(s4_br, im + 1)
                    ins = vector.reduce_max(nm8[:, 0:1], r8[:, 0:8], axis=AX.X, negate=True)
                    ins.then_inc(s4_vr, 1)
                    vector.wait_ge(s4_ar, im + 1)
                    ins = vector.reduce_max(m1c[:, 0:1], e8[:, 0:8], axis=AX.X)
                    vs4.edge(ins)
                    ins = vector.tensor_scalar(t8[:, 0:8], e8[:, 0:8], m1c[:, 0:1], None, ALU.is_ge)
                    vs4.edge(ins)
                    ins = vector.scalar_tensor_tensor(t8[:, 0:8], t8[:, 0:8], -1e30, e8[:, 0:8],
                                                      ALU.mult, ALU.add)
                    vs4.edge(ins)
                    ins = vector.reduce_max(m2c[:, 0:1], t8[:, 0:8], axis=AX.X)
                    vs4.edge(ins)
                    ins = vector.tensor_scalar(g8[:, 0:8], e8[:, 0:8], m2c[:, 0:1], None, ALU.is_ge)
                    vs4.edge(ins)
                    ins = vector.tensor_tensor(g8[:, 0:8], g8[:, 0:8], e8[:, 0:8], ALU.mult)
                    vs4.edge(ins)
                    ins = vector.tensor_tensor(m1c[:, 0:1], m1c[:, 0:1], m2c[:, 0:1], ALU.add)
                    vs4.edge(ins)
                    ins = vector.reciprocal(rc8[:, 0:1], m1c[:, 0:1])
                    vs4.edge(ins)
                    ins = vector.tensor_scalar(comb[:, 0:8], g8[:, 0:8], rc8[:, 0:1], 2.0,
                                               ALU.mult, ALU.mult)
                    vs4.edge(ins)
                    vector.wait_ge(s4_mh, im + 1)
                    vector.tensor_copy(h4f[:, 0:128], ph4[:, 0:128]).then_inc(s4_bh, 1)
                    vector.wait_ge(s4_bh, im + 1)
                    if im >= 1:
                        pg, pm = divmod(im - 1, 2)
                        vector.wait_ge(s4_tx, tx_ord(pg, "hw", pm) + 1)
                    for e in range(8):
                        ins = vector.tensor_single_scalar(hw4[:, e * 16:(e + 1) * 16],
                                                          h4f[:, e * 16:(e + 1) * 16],
                                                          comb[:, e:e + 1], ALU.mult)
                    ins.then_inc(s4_hw, 1)
                for nn in range(16):
                    for mloc in range(2):
                        oi = 32 * G4 + 2 * nn + mloc
                        vector.wait_ge(s4_mf, oi + 1)
                        ci = CH4.index((2, G4, mloc))
                        vb, _ = ch4_base(ci)
                        vector.wait_ge(s4_vd, vb + 3)
                        vector.tensor_copy(proj4.ap()[:, mloc * D + nn * 256: mloc * D + nn * 256 + 256],
                                           pac4[oi % 4][:, 0:256]).then_inc(s4_bf, 1)
                for mloc in range(2):
                    ln_chain_dve(4, G4, mloc)

        @block.scalar
        def _(scalar):
            as4 = SelfSem(scalar, s_as4)

            def ln_chain_act(typ, G4, mloc):
                ci = CH4.index((typ, G4, mloc))
                vb, ab = ch4_base(ci)
                if typ == 2:
                    rowf = proj4.ap()[:, mloc * D:(mloc + 1) * D]
                elif typ == 3:
                    rowf = x2_4.ap()[:, mloc * D:(mloc + 1) * D]
                else:
                    rowf = proj4.ap()[:, mloc * D:(mloc + 1) * D]
                pi = ci % 2
                scalar.wait_ge(s4_vd, vb + 1)
                ins = scalar.mul(st4M[:, pi:pi + 1], st4A[:, pi:pi + 1], 1.0 / D)
                as4.edge(ins)
                scalar.activation(st4V[:, pi:pi + 1], st4M[:, pi:pi + 1], AF.Square).then_inc(s4_ad, 1)
                scalar.wait_ge(s4_vd, vb + 2)
                ins = scalar.activation(ln4st[:, 2 * pi + 1:2 * pi + 2], ln4st[:, 2 * pi:2 * pi + 1], AF.Ln, bias=eps_sb[:, 0:1])
                as4.edge(ins)
                ins = scalar.activation(ln4st[:, 2 * pi + 1:2 * pi + 2], ln4st[:, 2 * pi + 1:2 * pi + 2], AF.Exp, scale=-0.5)
                as4.edge(ins)
                ins = scalar.activation(ln4st[:, 2 * pi:2 * pi + 1], st4M[:, pi:pi + 1], AF.Identity, scale=ln4st[:, 2 * pi + 1:2 * pi + 2])
                as4.edge(ins)
                ins = scalar.mul(ln4st[:, 2 * pi:2 * pi + 1], ln4st[:, 2 * pi:2 * pi + 1], -1.0)
                as4.edge(ins)
                if typ == 3:
                    pci = ci - 1
                    if CH4[pci][0] == 3:
                        scalar.wait_ge(s4_tx, tx_ord(G4, "x3", mloc - 1, 7) + 1)
                    elif G4 >= 1:
                        scalar.wait_ge(s4_tx, tx_ord(G4 - 1, "x3", 1, 7) + 1)
                    scalar.activation(x3h4[:], rowf, AF.Identity,
                                      scale=ln4st[:, 2 * pi + 1:2 * pi + 2], bias=ln4st[:, 2 * pi:2 * pi + 1]).then_inc(s4_ad, 1)
                else:
                    scalar.activation(rowf, rowf, AF.Identity,
                                      scale=ln4st[:, 2 * pi + 1:2 * pi + 2], bias=ln4st[:, 2 * pi:2 * pi + 1]).then_inc(s4_ad, 1)

            for G4 in range(NG4):
                for mloc in range(2):
                    ln_chain_act(2, G4, mloc)
                for mloc in range(2):
                    ln_chain_act(3, G4, mloc)
                    for grp in range(8):
                        gi = tx_ord(G4, "x3", mloc, grp)
                        scalar.wait_ge(s4_tx, gi + 1)
                        k0 = grp * 4
                        dst = bigT.ap().rearrange("p (k j) -> p k j", k=NK, j=256)[
                            :, k0:k0 + 4, mloc * 128:(mloc + 1) * 128]
                        srcp = ptr4[gi % 2][:, 0:512].rearrange("p (k j) -> p k j", k=4, j=128)
                        scalar.copy(dst, srcp).then_inc(s4_te, 1)
                for mloc in range(2):
                    im = 2 * G4 + mloc
                    scalar.wait_ge(s4_vr, im + 1)
                    scalar.activation(e8[:, 0:8], r8[:, 0:8], AF.Exp,
                                      bias=nm8[:, 0:1]).then_inc(s4_ar, 1)
                    gi = tx_ord(G4, "hw", mloc)
                    scalar.wait_ge(s4_tx, gi + 1)
                    if G4 >= 1:
                        scalar.wait_ge(s4_mf, 32 * G4)
                    scalar.copy(hwT4[:, mloc * 128:(mloc + 1) * 128],
                                ptr4[gi % 2][:, 0:128]).then_inc(s4_te, 1)
                for mloc in range(2):
                    ln_chain_act(4, G4, mloc)

    for g in reversed(ctx):
        g.__exit__(None, None, None)
    return nc, dbg



def prep_inputs(inputs):
    wq = np.ascontiguousarray(inputs["L0_attn_Wqkv"].T)
    tiles = wq.reshape(NK, 128, 24, 512).transpose(2, 0, 1, 3)
    order = []
    for kind, a in SCHED:
        base = {"q": 0, "k": 8, "v": 16}[kind]
        order.append(base + a)
    wqkv_t = np.ascontiguousarray(tiles[order].transpose(0, 2, 1, 3))

    lnp_arr = np.zeros((128, 4 * 32), np.float32)
    lnp_arr[:, 0:32] = inputs["L0_attn_q_ln_w"].reshape(32, 128).T * SCALE
    lnp_arr[:, 32:64] = inputs["L0_attn_q_ln_b"].reshape(32, 128).T * SCALE
    lnp_arr[:, 64:96] = inputs["L0_attn_k_ln_w"].reshape(32, 128).T
    lnp_arr[:, 96:128] = inputs["L0_attn_k_ln_b"].reshape(32, 128).T

    ident = np.eye(128, dtype=ml_dtypes.bfloat16)
    identf = np.eye(128, dtype=np.float32)

    wout_t = np.ascontiguousarray(
        inputs["L0_attn_Wout"].T.reshape(NK, 128, 16, 256).transpose(2, 1, 0, 3))
    wbase_t = np.ascontiguousarray(
        inputs["L0_ffn_W_base"].T.reshape(NK, 128, 16, 256).transpose(2, 1, 0, 3))
    wa_t = np.ascontiguousarray(
        np.asarray(inputs["L0_ffn_A"]).reshape(128, D).T.reshape(NK, 128, 128).transpose(1, 0, 2))
    wb_t = np.ascontiguousarray(
        np.asarray(inputs["L0_ffn_B"]).transpose(0, 2, 1).reshape(128, D).reshape(128, 16, 256))
    wr_t = np.ascontiguousarray(
        np.asarray(inputs["L0_ffn_W_router"]).T.reshape(NK, 128, 8).transpose(1, 0, 2))
    bb = np.ascontiguousarray(np.asarray(inputs["L0_ffn_b_base"]).reshape(1, D))
    return dict(wqkv=wqkv_t, lnp=lnp_arr, ident=ident, identf=identf,
                wout=wout_t, wbase=wbase_t, wa=wa_t, wb=wb_t, wr=wr_t, bb=bb)


def core_x(inputs, c):
    x = np.asarray(inputs["L0_attn_hidden_states"])
    seq, par = c // 2, c % 2
    return np.ascontiguousarray(x[seq].reshape(16, 128, D)[par::2].reshape(RTOK, D))


def core_dtab(c):
    p = c % 2
    out = np.full((NDTAB, 128, 128), -1e33, np.float32)
    jj = np.arange(128)[None, :]
    ii = np.arange(128)[:, None]
    idx = 0
    for s in range(8):
        Q = 2 * s + p
        for t in range(LS[s]):
            Dv = (t * 128 + jj) - (Q * 128 + ii)
            out[idx] = np.where(Dv <= 0, Dv.astype(np.float32), np.float32(-1e33))
            idx += 1
    assert idx == NDTAB
    return out



_CACHE = {}


def kernel(**inputs):
    inputs = {k: np.asarray(v) for k, v in inputs.items()}
    if "nc" not in _CACHE:
        nc, _ = build(debug=False)
        _CACHE["nc"] = nc
    nc = _CACHE["nc"]
    shared = prep_inputs(inputs)
    in_maps = []
    for c in range(NCORE):
        m = dict(shared)
        m["x"] = core_x(inputs, c)
        m["dtab"] = core_dtab(c)
        in_maps.append(m)
    res = run_bass_kernel_spmd(nc, in_maps, core_ids=list(range(NCORE)))
    out = np.empty((B, S, D), np.float32)
    view = out.reshape(B, 16, 128, D)
    for c in range(NCORE):
        seq, par = c // 2, c % 2
        view[seq, par::2] = np.asarray(res.results[c]["out"]).reshape(NM, 128, D)
    return out


# version 05
# speedup vs baseline: 1.1200x; 1.0848x over previous
import numpy as np
import ml_dtypes
import concourse.bass as bass
import concourse.mybir as mybir
from concourse.bass_utils import run_bass_kernel_spmd

F32 = mybir.dt.float32
BF16 = mybir.dt.bfloat16
AF = mybir.ActivationFunctionType
ALU = mybir.AluOpType
AX = mybir.AxisListType

B, S, D, H, DH = 4, 2048, 4096, 32, 128
NCORE = 8
RTOK = 1024
NM = 8
NK = 32
CLIP = 8.0
EPS = 1e-5
SCALE = DH ** -0.5
PAIRS = [[2 * i, 2 * i + 1] for i in range(4)]

SCHED = []
for g in range(4):
    SCHED += [("k", 2 * g), ("k", 2 * g + 1), ("v", 2 * g), ("v", 2 * g + 1)]
SCHED += [("q", a) for a in range(8)]
QK_ORD, V_ORD, QK_SCHED, V_SCHED = {}, {}, [], []
for _i, (_kind, _a) in enumerate(SCHED):
    if _kind == "v":
        V_ORD[_i] = len(V_SCHED); V_SCHED.append(_i)
    else:
        QK_ORD[_i] = len(QK_SCHED); QK_SCHED.append(_i)

LS = [2 * s + 2 for s in range(8)]
TSTART = [sum(LS[:s]) for s in range(8)]
NDTAB = sum(LS)
NCH = 24 * NM
P0_GRPS = NM * 8
P1_GRPS = len(QK_SCHED) * NM


import math as _math
CHW = []
for _s in range(8):
    w = LS[_s] * 128
    CHW.append([min(512, w - c * 512) for c in range((w + 511) // 512)])
preC = [0]
for _s in range(8):
    preC.append(preC[-1] + len(CHW[_s]))
CPH = preC[-1]


def GROUPS(s):
    return [(t0, min(4, LS[s] - t0)) for t0 in range(0, LS[s], 4)]


GPH = sum(len(GROUPS(_s)) for _s in range(8))
preG = [0]
for _s in range(8):
    preG.append(preG[-1] + len(GROUPS(_s)))


def GG_END(sg):
    h, s = divmod(sg, 8)
    return GPH * h + preG[s + 1]


SLOPES = [2.0 ** (-0.25 * (hh + 1)) for hh in range(H)]
MMB = 12


class SelfSem:
    def __init__(self, eng, semh):
        self.eng, self.sem, self.n = eng, semh, 0

    def edge(self, instr):
        instr.then_inc(self.sem, 1)
        self.n += 1
        self.eng.wait_ge(self.sem, self.n)


def build(debug=True):
    nc = bass.Bass("TRN2", target_bir_lowering=False, num_devices=NCORE)

    x_in = nc.dram_tensor("x", [RTOK, D], F32, kind="ExternalInput").ap()
    wqkv = nc.dram_tensor("wqkv", [24, 128, NK, 512], F32, kind="ExternalInput").ap()
    lnp = nc.dram_tensor("lnp", [128, 4 * 32], F32, kind="ExternalInput").ap()
    ident_in = nc.dram_tensor("ident", [128, 128], BF16, kind="ExternalInput").ap()
    identf_in = nc.dram_tensor("identf", [128, 128], F32, kind="ExternalInput").ap()
    out = nc.dram_tensor("out", [RTOK, D], F32, kind="ExternalOutput").ap()

    cin = [nc.dram_tensor(f"cin{g}", [2048, 1024], BF16, kind="Internal").ap() for g in range(4)]
    cout = [nc.dram_tensor(f"cout{g}", [4096, 1024], BF16, kind="Internal").ap() for g in range(4)]
    sin = nc.dram_tensor("sin", [16, 128], F32, kind="Internal").ap()
    sout = nc.dram_tensor("sout", [32, 128], F32, kind="Internal").ap()
    qT_d = nc.dram_tensor("qT_d", [D, RTOK], BF16, kind="Internal").ap()
    qs_d = nc.dram_tensor("qs_d", [32, 128], F32, kind="Internal").ap()
    aT_d = nc.dram_tensor("aT_d", [D, RTOK], BF16, kind="Internal").ap()
    zb_d = nc.dram_tensor("zb_d", [8, 128], BF16, kind="Internal").ap()
    dtab_in = nc.dram_tensor("dtab", [NDTAB, 128, 128], F32, kind="ExternalInput").ap()
    wout_in = nc.dram_tensor("wout", [16, 128, NK, 256], F32, kind="ExternalInput").ap()
    wbase_in = nc.dram_tensor("wbase", [16, 128, NK, 256], F32, kind="ExternalInput").ap()
    wa_in = nc.dram_tensor("wa", [128, NK, 128], F32, kind="ExternalInput").ap()
    wb_in = nc.dram_tensor("wb", [128, 16, 256], F32, kind="ExternalInput").ap()
    wr_in = nc.dram_tensor("wr", [128, NK, 8], F32, kind="ExternalInput").ap()
    bb_in = nc.dram_tensor("bb", [1, D], F32, kind="ExternalInput").ap()

    dbg = {}
    if debug:
        dbg["d_qT"] = nc.dram_tensor("d_qT", [D, RTOK], BF16, kind="ExternalOutput").ap()
        dbg["d_kv"] = nc.dram_tensor("d_kv", [4, 4096, 1024], BF16, kind="ExternalOutput").ap()
        dbg["d_st"] = nc.dram_tensor("d_st", [32, 128], F32, kind="ExternalOutput").ap()
        dbg["d_xT"] = nc.dram_tensor("d_xT", [128, NK * RTOK], BF16, kind="ExternalOutput").ap()
        dbg["d_aT"] = nc.dram_tensor("d_aT", [D, RTOK], BF16, kind="ExternalOutput").ap()

    ctx = []

    def sb(name, shape, dtype):
        g = nc.sbuf_tensor(name, shape, dtype); t = g.__enter__(); ctx.append(g); return t

    def psum(name, shape, dtype):
        g = nc.psum_tensor(name, shape, dtype); t = g.__enter__(); ctx.append(g); return t

    def sem(name):
        g = nc.semaphore(name); s = g.__enter__(); ctx.append(g); return s

    id_bf = sb("id_bf", [128, 128], BF16)
    id_f = sb("id_f", [128, 128], F32)
    lnp_sb = sb("lnp_sb", [128, 4 * 32], F32)
    eps_sb = sb("eps_sb", [128, 1], F32)

    ctx_p1_base = len(ctx)
    xhatT = sb("xhatT", [128, NK * RTOK], BF16)
    wbuf = [sb(f"wbuf{j}", [128, NK * 512], BF16) for j in range(2)]
    neg8 = sb("neg8", [128, 512], F32)
    xld = [sb("xld0", [128, D], F32)] * 2
    xhat = [sb("xhat0", [128, D], BF16)] * 2
    sqj = sb("sqj", [128, D], BF16)
    qring = [sb(f"qr{j}", [128, 512], BF16) for j in range(4)]
    tring = [sb(f"tr{j}", [128, 4096], BF16) for j in range(2)]
    vring = [sb(f"vr{j}", [128, 4096], BF16) for j in range(2)]
    psums = sb("psums", [128, 128], F32)
    psumsq = sb("psumsq", [128, 128], F32)
    stA = sb("stA", [128, 8], F32)
    stB = sb("stB", [128, 8], F32)
    stM = sb("stM", [128, 8], F32)
    stV = sb("stV", [128, 8], F32)
    allst = sb("allst", [128, 32], F32)
    ln1st = sb("ln1st", [128, 2], F32)
    st32 = sb("st32", [32, 128], F32)

    pacc = [psum(f"pacc{j}", [128, 512], F32) for j in range(3)]
    ptr = [psum(f"ptr{j}", [128, 1024], BF16) for j in range(2)]
    pst = psum("pst", [128, 128], F32)
    ctx_p1_end = len(ctx)

    s_x = sem("s_x")
    s_x2 = sem("s_x2")
    s_w = [sem("s_w0"), sem("s_w1")]
    s_mm = sem("s_mm")
    s_bk = sem("s_bk")
    s_tp = sem("s_tp")
    s_ev = sem("s_ev")
    s_tsp = [sem("s_tsp0"), sem("s_tsp1")]
    s_vsp = [sem("s_vsp0"), sem("s_vsp1")]
    s_ssp = sem("s_ssp")
    s_ccg = [sem(f"s_ccg{g}") for g in range(4)]
    s_ccs = sem("s_ccs")
    s_vd = sem("s_vd")
    s_ad = sem("s_ad")
    s_st = sem("s_st")
    s_dbg = sem("s_dbg")
    s_vself = sem("s_vself")
    s_aself = sem("s_aself")

    STATS_TP = P0_GRPS + P1_GRPS

    with nc.Block() as block:

        @block.sync
        def _(sync):
            sync.dma_start(out=id_bf[:], in_=ident_in[:, :]).then_inc(s_x, 16)
            sync.dma_start(out=id_f[:], in_=identf_in[:, :]).then_inc(s_x, 16)
            sync.dma_start(out=lnp_sb[:], in_=lnp[:, :]).then_inc(s_x, 16)
            for m in range(NM):
                if m >= 1:
                    sync.wait_ge(s_ad, 2 * (m - 1) + 2)
                sync.dma_start(out=xld[m % 2][:], in_=x_in[m * 128:(m + 1) * 128, :]).then_inc(s_x2, 16)
            for i, (kind, a) in enumerate(SCHED):
                if kind == "v":
                    sync.wait_ge(s_bk, i * NM + NM)
                    par = V_ORD[i] % 2
                    g = a // 2
                    hl0 = (a % 2) * 4
                    flat = cin[g].rearrange("r c -> (r c)")
                    vsrc = vring[par].ap().rearrange("p (m hl d) -> p m hl d", m=NM, hl=4, d=128)
                    for hl in range(4):
                        base = 1024 * 1024 + (hl0 + hl) * 131072
                        dst3 = flat[base: base + 131072].rearrange("(m p d) -> p m d", m=NM, p=128, d=128)
                        sync.dma_start(out=dst3, in_=vsrc[:, :, hl, :]).then_inc(s_vsp[V_ORD[i] % 2], 16)
                else:
                    sync.wait_ge(s_ev, P0_GRPS + NM * (QK_ORD[i] + 1))
                    par = QK_ORD[i] % 2
                    if kind == "k":
                        g = a // 2
                        hl0 = (a % 2) * 4
                        dst = cin[g][hl0 * 128:(hl0 + 4) * 128, :]
                    else:
                        dst = qT_d[a * 512:(a + 1) * 512, :]
                    dst3 = dst.rearrange("(hl d) t -> d hl t", hl=4, d=128)
                    src3 = tring[par].ap().rearrange("p (hl t) -> p hl t", hl=4, t=1024)
                    sync.dma_start(out=dst3, in_=src3).then_inc(s_tsp[QK_ORD[i] % 2], 16)
            sync.wait_ge(s_vd, 23)
            sync.dma_start(out=qs_d[:, :], in_=st32[:, :]).then_inc(s_ssp, 16)
            sync.dma_start(out=sin[:, :], in_=st32[16:32, :]).then_inc(s_ssp, 16)

            if debug:
                for g in range(4):
                    sync.wait_ge(s_ccg[g], 1)
                sync.wait_ge(s_ccs, 1)
                sync.wait_ge(s_tsp[0], 16 * 8)
                sync.wait_ge(s_tsp[1], 16 * 8)
                sync.dma_start(out=dbg["d_qT"][:, :], in_=qT_d[:, :]).then_inc(s_dbg, 16)
                for g in range(4):
                    sync.dma_start(out=dbg["d_kv"][g], in_=cout[g][:, :]).then_inc(s_dbg, 16)
                sync.dma_start(out=dbg["d_st"][:, :], in_=sout[:, :]).then_inc(s_dbg, 16)
                sync.dma_start(out=dbg["d_xT"][:, :], in_=xhatT[:]).then_inc(s_dbg, 16)
                sync.wait_ge(s_dbg, 16 * 7)

        @block.gpsimd
        def _(gpsimd):
            for nn in range(24):
                if nn >= 2:
                    gpsimd.wait_ge(s_mm, (nn - 1) * NM)
                gpsimd.dma_start(
                    out=wbuf[nn % 2].ap().rearrange("p (k n) -> p k n", k=NK, n=512),
                    in_=wqkv[nn],
                ).then_inc(s_w[nn % 2], 16)
            for g in range(4):
                gpsimd.wait_ge(s_tsp[0], 16 * (g + 1))
                gpsimd.wait_ge(s_tsp[1], 16 * (g + 1))
                gpsimd.wait_ge(s_vsp[0], 64 * (g + 1))
                gpsimd.wait_ge(s_vsp[1], 64 * (g + 1))
                gpsimd.collective_compute(
                    "AllGather", ALU.bypass, replica_groups=PAIRS,
                    ins=[cin[g][:, :]], outs=[cout[g][:, :]],
                ).then_inc(s_ccg[g], 1)
            gpsimd.wait_ge(s_ssp, 32)
            gpsimd.collective_compute(
                "AllGather", ALU.bypass, replica_groups=PAIRS,
                ins=[sin[:, :]], outs=[sout[:, :]],
            ).then_inc(s_ccs, 1)

        @block.tensor
        def _(tensor):
            tensor.wait_ge(s_x, 48)
            for m in range(NM):
                tensor.wait_ge(s_ad, 2 * m + 2)
                for kg in range(8):
                    gi = m * 8 + kg
                    if gi >= 2:
                        tensor.wait_ge(s_ev, gi - 1)
                    tb = ptr[gi % 2]
                    for j in range(4):
                        k = kg * 4 + j
                        ins = tensor.transpose(tb[:, j * 128:(j + 1) * 128],
                                               xhat[m % 2][:, k * 128:(k + 1) * 128],
                                               id_bf[:, :])
                    ins.then_inc(s_tp, 1)

            def emit_qk_transposes(j):
                jnn, jm = divmod(j, NM)
                if SCHED[jnn][0] == "v":
                    return
                gidx = P0_GRPS + QK_ORD[jnn] * NM + jm
                tensor.wait_ge(s_bk, j + 1)
                if gidx >= 2:
                    tensor.wait_ge(s_ev, gidx - 1)
                tb = ptr[gidx % 2]
                for t in range(4):
                    ins = tensor.transpose(tb[:, t * 128:(t + 1) * 128],
                                           qring[j % 4][:, t * 128:(t + 1) * 128],
                                           id_bf[:, :])
                ins.then_inc(s_tp, 1)

            for i in range(NCH):
                nn, m = divmod(i, NM)
                tensor.wait_ge(s_w[nn % 2], 16 * (nn // 2 + 1))
                tensor.wait_ge(s_ev, 8 * (m + 1))
                if i >= 3:
                    tensor.wait_ge(s_bk, i - 2)
                bk = pacc[i % 3]
                for k in range(NK):
                    ins = tensor.matmul(bk[:, :],
                                        xhatT[:, k * RTOK + m * 128: k * RTOK + m * 128 + 128],
                                        wbuf[nn % 2][:, k * 512:(k + 1) * 512],
                                        start=(k == 0), stop=(k == NK - 1))
                ins.then_inc(s_mm, 1)
                if i >= 1:
                    emit_qk_transposes(i - 1)
            emit_qk_transposes(NCH - 1)

            tensor.wait_ge(s_ad, 22)
            tensor.transpose(pst[0:32, :], allst[:, 0:32], id_f[:, :]).then_inc(s_tp, 1)

        @block.vector
        def _(vector):
            vs = SelfSem(vector, s_vself)
            ins = vector.memset(neg8[:], -CLIP)
            vs.edge(ins)
            ins = vector.memset(eps_sb[:], EPS)
            vs.edge(ins)
            for m in range(NM):
                vector.wait_ge(s_x2, 16 * (m + 1))
                xm = xld[m % 2]
                vector.reduce_sum(stA[:, m:m + 1], xm[:], axis=AX.X)
                vector.scalar_tensor_tensor(sqj[:], xm[:], 1.0, xm[:], ALU.bypass,
                                            ALU.mult, accum_out=stB[:, m:m + 1]).then_inc(s_vd, 1)
                vector.wait_ge(s_ad, 2 * m + 1)
                vector.scalar_tensor_tensor(ln1st[:, 0:1], stB[:, m:m + 1], 1.0 / D,
                                            stV[:, m:m + 1], ALU.mult, ALU.subtract).then_inc(s_vd, 1)
            ins = vector.nop()
            vs.edge(ins)
            for i in range(NCH):
                nn, m = divmod(i, NM)
                kind, a = SCHED[nn]
                vector.wait_ge(s_mm, i + 1)
                if kind == "v":
                    par = V_ORD[nn] % 2
                    if V_ORD[nn] >= 2 and m == 0:
                        vector.wait_ge(s_vsp[V_ORD[nn] % 2], 64 * (V_ORD[nn] // 2))
                    vector.scalar_tensor_tensor(vring[par][:, m * 512:(m + 1) * 512],
                                                pacc[i % 3][:, :], CLIP, neg8[:],
                                                ALU.min, ALU.max).then_inc(s_bk, 1)
                else:
                    if i >= 4:
                        j = i - 4
                        jnn = j // NM
                        if SCHED[jnn][0] != "v":
                            vector.wait_ge(s_tp, P0_GRPS + QK_ORD[jnn] * NM + (j % NM) + 1)
                    col = (0 if kind == "q" else 64) + a * 8 + m
                    vector.scalar_tensor_tensor(qring[i % 4][:, :], pacc[i % 3][:, :],
                                                CLIP, neg8[:], ALU.min, ALU.max,
                                                accum_out=psums[:, col:col + 1]).then_inc(s_bk, 1)
                j = i - 1
                if j >= 0 and SCHED[j // NM][0] != "v":
                    jnn, jm = divmod(j, NM)
                    jcol = (0 if SCHED[jnn][0] == "q" else 64) + SCHED[jnn][1] * 8 + jm
                    vector.wait_ge(s_bk, j + 1)
                    vector.scalar_tensor_tensor(sqj[:, (jm % 8) * 512:(jm % 8) * 512 + 512], qring[j % 4][:, :], 1.0,
                                                qring[j % 4][:, :], ALU.bypass, ALU.mult,
                                                accum_out=psumsq[:, jcol:jcol + 1])
            j = NCH - 1
            if SCHED[j // NM][0] != "v":
                jnn, jm = divmod(j, NM)
                jcol = (0 if SCHED[jnn][0] == "q" else 64) + SCHED[jnn][1] * 8 + jm
                vector.wait_ge(s_bk, j + 1)
                ins = vector.scalar_tensor_tensor(sqj[:, (jm % 8) * 512:(jm % 8) * 512 + 512], qring[j % 4][:, :], 1.0,
                                                  qring[j % 4][:, :], ALU.bypass, ALU.mult,
                                                  accum_out=psumsq[:, jcol:jcol + 1])
            else:
                ins = vector.nop()
            vs.edge(ins)

            for qi, off in ((0, 0), (1, 64)):
                if qi == 1:
                    vector.wait_ge(s_ad, 16 + 3)
                for m in range(NM):
                    vector.reduce_sum(stA[:, m:m + 1], psums[:, off + m: off + 64: 8], axis=AX.X)
                    ins = vector.reduce_sum(stB[:, m:m + 1], psumsq[:, off + m: off + 64: 8], axis=AX.X)
                ins.then_inc(s_vd, 1)
                vector.wait_ge(s_ad, 16 + 3 * qi + 1)
                vector.scalar_tensor_tensor(stV[:, 0:8], stB[:, 0:8], 1.0 / D,
                                            stV[:, 0:8], ALU.mult, ALU.subtract).then_inc(s_vd, 1)
                vector.wait_ge(s_ad, 16 + 3 * qi + 2)
                rc = slice(0, 8) if qi == 0 else slice(16, 24)
                vector.tensor_tensor(stB[:, 0:8], stM[:, 0:8], allst[:, rc],
                                     ALU.mult).then_inc(s_vd, 1)
            vector.wait_ge(s_tp, STATS_TP + 1)
            vector.tensor_copy(st32[:, :], pst[0:32, :]).then_inc(s_vd, 23 - 22)

        @block.scalar
        def _(scalar):
            asf = SelfSem(scalar, s_aself)
            for m in range(NM):
                scalar.wait_ge(s_vd, 2 * m + 1)
                ins = scalar.mul(stM[:, m:m + 1], stA[:, m:m + 1], 1.0 / D)
                asf.edge(ins)
                scalar.activation(stV[:, m:m + 1], stM[:, m:m + 1], AF.Square).then_inc(s_ad, 1)
                scalar.wait_ge(s_vd, 2 * m + 2)
                ins = scalar.activation(ln1st[:, 1:2], ln1st[:, 0:1], AF.Ln, bias=eps_sb[:, 0:1])
                asf.edge(ins)
                ins = scalar.activation(ln1st[:, 1:2], ln1st[:, 1:2], AF.Exp, scale=-0.5)
                asf.edge(ins)
                ins = scalar.activation(ln1st[:, 0:1], stM[:, m:m + 1], AF.Identity, scale=ln1st[:, 1:2])
                asf.edge(ins)
                ins = scalar.mul(ln1st[:, 0:1], ln1st[:, 0:1], -1.0)
                asf.edge(ins)
                if m >= 1:
                    for kg in range(8):
                        gi = (m - 1) * 8 + kg
                        scalar.wait_ge(s_tp, gi + 1)
                        k0 = kg * 4
                        dst = xhatT.ap().rearrange("p (k t) -> p k t", k=NK, t=RTOK)[
                            :, k0:k0 + 4, (m - 1) * 128:m * 128]
                        src = ptr[gi % 2][:, 0:512].rearrange("p (k t) -> p k t", k=4, t=128)
                        scalar.copy(dst, src).then_inc(s_ev, 1)
                    scalar.wait_ge(s_tp, 8 * m)
                scalar.activation(xhat[m % 2][:], xld[m % 2][:], AF.Identity,
                                  scale=ln1st[:, 1:2], bias=ln1st[:, 0:1]).then_inc(s_ad, 1)
                if m == NM - 1:
                    for kg in range(8):
                        gi = m * 8 + kg
                        scalar.wait_ge(s_tp, gi + 1)
                        k0 = kg * 4
                        dst = xhatT.ap().rearrange("p (k t) -> p k t", k=NK, t=RTOK)[
                            :, k0:k0 + 4, m * 128:(m + 1) * 128]
                        src = ptr[gi % 2][:, 0:512].rearrange("p (k t) -> p k t", k=4, t=128)
                        scalar.copy(dst, src).then_inc(s_ev, 1)
            for i in range(NCH):
                nn, m = divmod(i, NM)
                kind, a = SCHED[nn]
                if kind == "v":
                    continue
                qko = QK_ORD[nn]
                gidx = P0_GRPS + qko * NM + m
                scalar.wait_ge(s_tp, gidx + 1)
                par = qko % 2
                if qko >= 2 and m == 0:
                    scalar.wait_ge(s_tsp[qko % 2], 16 * (qko // 2))
                hl_all = tring[par].ap().rearrange("p (hl t) -> p hl t", hl=4, t=1024)
                dst = hl_all[:, :, m * 128:(m + 1) * 128]
                src = ptr[gidx % 2][:, 0:512].rearrange("p (hl t) -> p hl t", hl=4, t=128)
                scalar.copy(dst, src).then_inc(s_ev, 1)
            for qi in range(2):
                scalar.wait_ge(s_vd, 16 + 3 * qi + 1)
                ins = scalar.mul(stM[:, 0:8], stA[:, 0:8], 1.0 / D)
                asf.edge(ins)
                scalar.activation(stV[:, 0:8], stM[:, 0:8], AF.Square).then_inc(s_ad, 1)
                scalar.wait_ge(s_vd, 16 + 3 * qi + 2)
                rc = slice(0, 8) if qi == 0 else slice(16, 24)
                nbc = slice(8, 16) if qi == 0 else slice(24, 32)
                ins = scalar.activation(allst[:, rc], stV[:, 0:8], AF.Ln, bias=eps_sb[:, 0:1])
                asf.edge(ins)
                scalar.activation(allst[:, rc], allst[:, rc], AF.Exp,
                                  scale=-0.5).then_inc(s_ad, 1)
                scalar.wait_ge(s_vd, 16 + 3 * qi + 3)
                scalar.mul(allst[:, nbc], stB[:, 0:8], -1.0).then_inc(s_ad, 1)


        for g_ in reversed(ctx[ctx_p1_base:ctx_p1_end]):
            g_.__exit__(None, None, None)
        del ctx[ctx_p1_base:ctx_p1_end]

        ctx_p3_base = len(ctx)
        dtab_sb = sb("dtab_sb", [128, NDTAB * 128], F32)
        rrep_k = sb("rrep_k", [128, 2048], F32)
        nbrep_k = sb("nbrep_k", [128, 2048], F32)
        rrep_q = sb("rrep_q", [128, 1024], F32)
        nbrep_q = sb("nbrep_q", [128, 1024], F32)
        krow = sb("krow", [1, 2048], F32)
        knbrow = sb("knbrow", [1, 2048], F32)
        qrow = sb("qrow", [1, 1024], F32)
        qnbrow = sb("qnbrow", [1, 1024], F32)
        ones_f = sb("ones_f", [1, 128], F32)
        ones_b = sb("ones_b", [1, 128], BF16)
        qraw = [sb(f"qraw{j}", [128, 1024], BF16) for j in range(2)]
        kraw = [sb(f"kraw{j}", [128, 2048], BF16) for j in range(2)]
        v_sb = [sb(f"v_sb{j}", [128, 2048], BF16) for j in range(2)]
        scr = [sb(f"scr{j}", [128, 3072], F32) for j in range(2)]
        qhat = [sb(f"qhat{j}", [128, 1024], BF16) for j in range(2)]
        khat = [sb(f"khat{j}", [128, 2048], BF16) for j in range(2)]
        scores = [sb(f"scores{j}", [128, 2048], F32) for j in range(2)]
        probs = [sb(f"probs{j}", [128, 2048], BF16) for j in range(3)]
        probsT = [sb(f"probsT{j}", [128, 2048], BF16) for j in range(3)]
        negmax = [sb(f"negmax{j}", [128, 8], F32) for j in range(2)]
        Zcols = [sb(f"Zcols{j}", [128, 8], F32) for j in range(2)]
        recipZ = sb("recipZ", [128, 8], F32)
        recipZb = sb("recipZb", [128, 8], BF16)
        zb_sb = sb("zb_sb", [8, 128], BF16)
        zrow1 = [sb(f"zrow1{j}", [1, 1024], BF16) for j in range(2)]
        attnraw = [sb(f"attnraw{j}", [128, 1024], F32) for j in range(2)]
        reciprep = [sb(f"reciprep{j}", [128, 1024], F32) for j in range(2)]
        attnT_sb = [sb(f"attnT{j}", [128, 1024], BF16) for j in range(2)]

        psc = [psum(f"psc{j}", [128, 512], F32) for j in range(3)]
        ptp = [psum(f"ptp{j}", [128, 1024], BF16) for j in range(2)]
        av_ps = psum("av_ps", [128, 1024], F32)
        rp_ps = psum("rp_ps", [128, 512], F32)
        ctx_p3_end = len(ctx)

        s3_pl = sem("s3_pl")
        s3_ld = [sem("s3_ld0"), sem("s3_ld1")]
        s3_mm3 = sem("s3_mm3")
        s3_bk3 = sem("s3_bk3")
        s3_tp3 = sem("s3_tp3")
        s3_te = sem("s3_te")
        s3_teA = sem("s3_teA")
        s3_ex = sem("s3_ex")
        s3_mx = sem("s3_mx")
        s3_avs = sem("s3_avs")
        s3_ave = sem("s3_ave")
        s3_vd3 = sem("s3_vd3")
        s3_pd3 = sem("s3_pd3")
        s3_ak3 = sem("s3_ak3")
        s3_zr = sem("s3_zr")
        s3_zc = sem("s3_zc")
        s3_zt = sem("s3_zt")
        s3_ze = sem("s3_ze")
        s3_zd = sem("s3_zd")
        s3_zm = sem("s3_zm")
        s3_rp = sem("s3_rp")
        s3_nrm = sem("s3_nrm")
        s3_sp3 = [sem("s3_sp3a"), sem("s3_sp3b")]
        s_vs3 = sem("s_vs3")
        s_as3 = sem("s_as3")
        s_ps3 = sem("s_ps3")

        @block.sync
        def _(sync):
            sync.wait_ge(s_mm, NCH)
            sync.wait_ge(s_vd, 23)
            sync.wait_ge(s_ad, 22)
            sync.dma_start(out=dtab_sb.ap().rearrange("p (i j) -> p i j", i=NDTAB, j=128),
                           in_=dtab_in.rearrange("i p j -> p i j")).then_inc(s3_pl, 16)
            sync.wait_ge(s_ccs, 1)
            srows = sout.rearrange("(r t u) p -> t u r p", r=2, t=2, u=8)
            for ri in range(2):
                sync.dma_start(out=krow.ap().rearrange("o (u r p) -> o u r p", u=8, r=2, p=128)[:, :, ri, :],
                               in_=srows[0:1, :, ri, :]).then_inc(s3_pl, 16)
                sync.dma_start(out=knbrow.ap().rearrange("o (u r p) -> o u r p", u=8, r=2, p=128)[:, :, ri, :],
                               in_=srows[1:2, :, ri, :]).then_inc(s3_pl, 16)
            qsf = qs_d.rearrange("r p -> (r p)")
            sync.dma_start(out=qrow[0:1, :], in_=qsf[0:1024].rearrange("(o c) -> o c", o=1)).then_inc(s3_pl, 16)
            sync.dma_start(out=qnbrow[0:1, :], in_=qsf[1024:2048].rearrange("(o c) -> o c", o=1)).then_inc(s3_pl, 16)
            sync.wait_ge(s_tsp[0], 16 * 8)
            sync.wait_ge(s_tsp[1], 16 * 8)
            for h in range(H):
                g = h // 8
                hl = h % 8
                if h >= 2:
                    sync.wait_ge(s3_vd3, h - 1)
                    sync.wait_ge(s3_pd3, 2 * (h - 2) + 2)
                    sync.wait_ge(s3_avs, 8 * (h - 1))
                sync.wait_ge(s_ccg[g], 1)
                sync.dma_start(out=qraw[h % 2][:], in_=qT_d[h * 128:(h + 1) * 128, :]).then_inc(s3_ld[h % 2], 16)
                kdst = kraw[h % 2].ap().rearrange("p (u r j) -> p u r j", u=8, r=2, j=128)
                vdst = v_sb[h % 2].ap().rearrange("p (u r d) -> p u r d", u=8, r=2, d=128)
                cfl = cout[g].rearrange("r c -> (r c)")
                for r in range(2):
                    ksrc = cout[g][r * 2048 + hl * 128: r * 2048 + (hl + 1) * 128, :]
                    sync.dma_start(out=kdst[:, :, r, :],
                                   in_=ksrc.rearrange("d (u j) -> d u j", u=8, j=128)).then_inc(s3_ld[h % 2], 16)
                    vbase = r * 2097152 + 1048576 + hl * 131072
                    vsrc = cfl[vbase: vbase + 131072].rearrange("(u p d) -> p u d", u=8, p=128, d=128)
                    sync.dma_start(out=vdst[:, :, r, :], in_=vsrc).then_inc(s3_ld[h % 2], 16)
                if h >= 1:
                    j = h - 1
                    sync.wait_ge(s3_ze, j + 1)
                    if j >= 1:
                        sync.wait_ge(s3_zd, 32 * j)
                    sync.dma_start(out=zb_d[:, :], in_=zb_sb[:, :]).then_inc(s3_zd, 16)
                    sync.wait_ge(s3_zd, 32 * j + 16)
                    if j >= 2:
                        sync.wait_ge(s3_zm, 2 * (j - 1))
                    sync.dma_start(out=zrow1[j % 2][0:1, :],
                                   in_=zb_d.rearrange("r p -> (r p)").rearrange("(o c) -> o c", o=1)).then_inc(s3_zd, 16)
                if h >= 2:
                    j = h - 2
                    sync.wait_ge(s3_nrm, j + 1)
                    sync.dma_start(out=aT_d[j * 128:(j + 1) * 128, :], in_=attnT_sb[j % 2][:]).then_inc(s3_sp3[j % 2], 16)
            j = H - 1
            sync.wait_ge(s3_ze, j + 1)
            sync.wait_ge(s3_zd, 32 * j)
            sync.dma_start(out=zb_d[:, :], in_=zb_sb[:, :]).then_inc(s3_zd, 16)
            sync.wait_ge(s3_zd, 32 * j + 16)
            sync.wait_ge(s3_zm, 2 * (j - 1))
            sync.dma_start(out=zrow1[j % 2][0:1, :],
                           in_=zb_d.rearrange("r p -> (r p)").rearrange("(o c) -> o c", o=1)).then_inc(s3_zd, 16)
            for j in (H - 2, H - 1):
                sync.wait_ge(s3_nrm, j + 1)
                sync.dma_start(out=aT_d[j * 128:(j + 1) * 128, :], in_=attnT_sb[j % 2][:]).then_inc(s3_sp3[j % 2], 16)
            if debug:
                sync.wait_ge(s3_sp3[0], 16 * (H // 2))
                sync.wait_ge(s3_sp3[1], 16 * (H // 2))
                sync.dma_start(out=dbg["d_aT"][:, :], in_=aT_d[:, :]).then_inc(s_dbg, 16)
                sync.wait_ge(s_dbg, 16 * 8)

        @block.tensor
        def _(tensor):
            tensor.wait_ge(s3_pl, 112)
            tensor.wait_ge(s_vd, 24)
            mmi = 0
            for row_sb, width in ((krow, 2048), (knbrow, 2048), (qrow, 1024), (qnbrow, 1024)):
                for c in range(width // 512):
                    if mmi >= 3:
                        tensor.wait_ge(s3_bk3, mmi - 2)
                    tensor.matmul(psc[mmi % 3][:, 0:512], ones_f[0:1, :],
                                  row_sb[0:1, c * 512:(c + 1) * 512],
                                  start=True, stop=True).then_inc(s3_mm3, 1)
                    mmi += 1
            assert mmi == MMB

            for h in range(H):
                if h >= 1:
                    jz = h - 1
                    tensor.wait_ge(s3_zc, jz + 1)
                    tensor.wait_ge(s3_te, 10 * h)
                    tensor.wait_ge(s3_ze, jz)
                    tensor.transpose(ptp[0][0:8, 0:128], recipZb[:, 0:8], id_bf[:, :]).then_inc(s3_zt, 1)
                tensor.wait_ge(s3_ak3, 2 * h + 2)
                if h >= 1:
                    tensor.wait_ge(s3_ave, h)

                def emit_scores(s):
                    sg = 8 * h + s
                    off = 0
                    for c, w in enumerate(CHW[s]):
                        pidx = MMB + CPH * h + preC[s] + c
                        if pidx >= 3:
                            tensor.wait_ge(s3_bk3, pidx - 2)
                        tensor.matmul(psc[pidx % 3][:, 0:w],
                                      qhat[h % 2][:, s * 128:(s + 1) * 128],
                                      khat[h % 2][:, off:off + w],
                                      start=True, stop=True).then_inc(s3_mm3, 1)
                        off += w

                def emit_tpav(s):
                    sg = 8 * h + s
                    tensor.wait_ge(s3_ex, sg + 1)
                    if s == 0:
                        tensor.wait_ge(s3_ze, h)
                    for gi, (t0, nt) in enumerate(GROUPS(s)):
                        gg = GPH * h + preG[s] + gi
                        if gg >= 2:
                            if gg % 2 == 0:
                                tensor.wait_ge(s3_te, gg // 2)
                            else:
                                tensor.wait_ge(s3_teA, (gg - 1) // 2)
                        tb = ptp[gg % 2]
                        for t in range(t0, t0 + nt):
                            ins = tensor.transpose(tb[:, (t - t0) * 128:(t - t0 + 1) * 128],
                                                   probs[sg % 3][:, t * 128:(t + 1) * 128],
                                                   id_bf[:, :])
                        ins.then_inc(s3_tp3, 1)
                        if gg % 2 == 0:
                            tensor.wait_ge(s3_te, gg // 2 + 1)
                        else:
                            tensor.wait_ge(s3_teA, (gg + 1) // 2)
                        for t in range(t0, t0 + nt):
                            ins = tensor.matmul(av_ps[:, s * 128:(s + 1) * 128],
                                                v_sb[h % 2][:, t * 128:(t + 1) * 128],
                                                probsT[sg % 3][:, t * 128:(t + 1) * 128],
                                                start=(s % 4 == 0 and t == 0),
                                                stop=(s % 4 == 3 and t == LS[s] - 1))
                        if t0 + nt == LS[s]:
                            ins.then_inc(s3_avs, 1)

                emit_scores(0)
                emit_scores(1)
                if h >= 2:
                    jz = h - 2
                    tensor.wait_ge(s3_zd, 32 * jz + 32)
                    for half in range(2):
                        tensor.wait_ge(s3_rp, 2 * jz + half)
                        for s8 in range(4 * half, 4 * half + 4):
                            ins = tensor.matmul(rp_ps[:, (s8 % 4) * 128:(s8 % 4 + 1) * 128], ones_b[0:1, :],
                                                zrow1[jz % 2][0:1, s8 * 128:(s8 + 1) * 128],
                                                start=(s8 % 4 == 0), stop=(s8 % 4 == 3))
                        ins.then_inc(s3_zm, 1)
                for s in range(8):
                    if s + 2 < 8:
                        emit_scores(s + 2)
                    emit_tpav(s)
            jz = H - 1
            tensor.wait_ge(s3_zc, jz + 1)
            tensor.wait_ge(s3_te, 10 * H)
            tensor.wait_ge(s3_ze, jz)
            tensor.transpose(ptp[0][0:8, 0:128], recipZb[:, 0:8], id_bf[:, :]).then_inc(s3_zt, 1)
            for jz in (H - 2, H - 1):
                tensor.wait_ge(s3_zd, 32 * jz + 32)
                for half in range(2):
                    tensor.wait_ge(s3_rp, 2 * jz + half)
                    for s8 in range(4 * half, 4 * half + 4):
                        ins = tensor.matmul(rp_ps[:, (s8 % 4) * 128:(s8 % 4 + 1) * 128], ones_b[0:1, :],
                                            zrow1[jz % 2][0:1, s8 * 128:(s8 + 1) * 128],
                                            start=(s8 % 4 == 0), stop=(s8 % 4 == 3))
                    ins.then_inc(s3_zm, 1)

        @block.vector
        def _(vector):
            vs3 = SelfSem(vector, s_vs3)
            vector.memset(ones_f[:, :], 1.0)
            vector.memset(ones_b[:, :], 1.0).then_inc(s_vd, 1)
            dsts = ([(rrep_k, c) for c in range(4)] + [(nbrep_k, c) for c in range(4)]
                    + [(rrep_q, c) for c in range(2)] + [(nbrep_q, c) for c in range(2)])
            for mmi, (dst, c) in enumerate(dsts):
                vector.wait_ge(s3_mm3, mmi + 1)
                vector.tensor_copy(dst.ap()[:, c * 512:(c + 1) * 512], psc[mmi % 3][:, 0:512]).then_inc(s3_bk3, 1)

            def emit_tt(hh):
                if hh == 0:
                    vector.wait_ge(s3_bk3, MMB)
                vector.wait_ge(s3_ld[hh % 2], 80 * (hh // 2 + 1))
                if hh >= 2:
                    vector.wait_ge(s3_ak3, 2 * (hh - 2) + 1)
                vector.tensor_tensor(scr[hh % 2][:, 0:1024], qraw[hh % 2][:], rrep_q[:],
                                     ALU.mult).then_inc(s3_vd3, 1)

            emit_tt(0)
            for h in range(H):
                if h >= 1:
                    j = h - 1
                    vector.wait_ge(s3_avs, 8 * (j + 1))
                    if j >= 2:
                        vector.wait_ge(s3_nrm, j - 1)
                    vector.tensor_copy(attnraw[j % 2][:], av_ps[:, 0:1024]).then_inc(s3_ave, 1)
                    vector.wait_ge(s3_zt, j + 1)
                    if j >= 1:
                        vector.wait_ge(s3_zd, 32 * (j - 1) + 16)
                    vector.tensor_copy(zb_sb[0:8, :], ptp[0][0:8, 0:128]).then_inc(s3_ze, 1)
                def emit_chunks_max(s):
                    sg = 8 * h + s
                    if sg >= 2:
                        vector.wait_ge(s3_ex, sg - 1)
                    off = 0
                    for c, w in enumerate(CHW[s]):
                        pidx = MMB + CPH * h + preC[s] + c
                        vector.wait_ge(s3_mm3, pidx + 1)
                        vector.scalar_tensor_tensor(scores[sg % 2][:, off:off + w],
                                                    dtab_sb[:, TSTART[s] * 128 + off: TSTART[s] * 128 + off + w],
                                                    SLOPES[h], psc[pidx % 3][:, 0:w],
                                                    ALU.mult, ALU.add).then_inc(s3_bk3, 1)
                        off += w
                    if h >= 2:
                        vector.wait_ge(s3_ex, 8 * (h - 2) + s + 1)
                    vector.wait_ge(s3_bk3, MMB + CPH * h + preC[s + 1])
                    vector.reduce_max(negmax[h % 2][:, s:s + 1], scores[sg % 2][:, 0:LS[s] * 128],
                                      axis=AX.X, negate=True).then_inc(s3_mx, 1)

                def emit_psT_even(s):
                    sg = 8 * h + s
                    for gi, (t0, nt) in enumerate(GROUPS(s)):
                        gg = GPH * h + preG[s] + gi
                        if gg % 2 != 0:
                            continue
                        vector.wait_ge(s3_tp3, gg + 1)
                        if sg >= 3:
                            vector.wait_ge(s3_avs, sg - 2)
                        vector.tensor_copy(probsT[sg % 3][:, t0 * 128:(t0 + nt) * 128],
                                           ptp[0][:, 0:nt * 128]).then_inc(s3_te, 1)

                emit_chunks_max(0)
                emit_chunks_max(1)
                def emit_rp(half):
                    jz = h - 2
                    vector.wait_ge(s3_zm, 2 * jz + half + 1)
                    if jz >= 2 and half == 0:
                        vector.wait_ge(s3_nrm, jz - 1)
                    vector.tensor_copy(reciprep[jz % 2][:, half * 512:(half + 1) * 512],
                                       rp_ps[:, 0:512]).then_inc(s3_rp, 1)

                for s in range(8):
                    if s == 0 and h >= 2:
                        emit_rp(0)
                    if s + 2 < 8:
                        emit_chunks_max(s + 2)
                    emit_psT_even(s)
                    if s == 0 and h >= 2:
                        emit_rp(1)
                vector.wait_ge(s3_ex, 8 * h + 8)
                if h >= 1:
                    vector.wait_ge(s3_zc, h)
                vector.reciprocal(recipZ[:, 0:8], Zcols[h % 2][:, 0:8]).then_inc(s3_zr, 1)
                if h + 1 < H:
                    emit_tt(h + 1)
            j = H - 1
            vector.wait_ge(s3_avs, 8 * (j + 1))
            vector.wait_ge(s3_nrm, j - 1)
            vector.tensor_copy(attnraw[j % 2][:], av_ps[:, 0:1024]).then_inc(s3_ave, 1)
            vector.wait_ge(s3_zt, j + 1)
            vector.wait_ge(s3_zd, 32 * (j - 1) + 16)
            vector.tensor_copy(zb_sb[0:8, :], ptp[0][0:8, 0:128]).then_inc(s3_ze, 1)
            for jz in (H - 2, H - 1):
                for half in range(2):
                    vector.wait_ge(s3_zm, 2 * jz + half + 1)
                    if half == 0:
                        vector.wait_ge(s3_nrm, jz - 1)
                    vector.tensor_copy(reciprep[jz % 2][:, half * 512:(half + 1) * 512],
                                       rp_ps[:, 0:512]).then_inc(s3_rp, 1)

        @block.gpsimd
        def _(gpsimd):
            for h in range(H):
                gpsimd.wait_ge(s3_vd3, h + 1)
                gpsimd.tensor_tensor(scr[h % 2][:, 0:1024], scr[h % 2][:, 0:1024],
                                     nbrep_q[:], ALU.add).then_inc(s3_pd3, 1)
                gpsimd.wait_ge(s3_ld[h % 2], 80 * (h // 2 + 1))
                gpsimd.wait_ge(s3_bk3, 8)
                if h >= 2:
                    gpsimd.wait_ge(s3_ak3, 2 * (h - 2) + 2)
                gpsimd.tensor_tensor(scr[h % 2][:, 1024:3072], kraw[h % 2][:], rrep_k[:],
                                     ALU.mult).then_inc(s_ps3, 1)
                gpsimd.wait_ge(s_ps3, h + 1)
                gpsimd.tensor_tensor(scr[h % 2][:, 1024:3072], scr[h % 2][:, 1024:3072],
                                     nbrep_k[:], ALU.add).then_inc(s3_pd3, 1)
                if h >= 2:
                    j = h - 2
                    gpsimd.wait_ge(s3_rp, 2 * (j + 1))
                    gpsimd.wait_ge(s3_ave, j + 1)
                    if j >= 2:
                        gpsimd.wait_ge(s3_sp3[j % 2], 16 * (j // 2))
                    gpsimd.tensor_tensor(attnT_sb[j % 2][:], attnraw[j % 2][:],
                                         reciprep[j % 2][:], ALU.mult).then_inc(s3_nrm, 1)
            for j in (H - 2, H - 1):
                gpsimd.wait_ge(s3_rp, 2 * (j + 1))
                gpsimd.wait_ge(s3_ave, j + 1)
                gpsimd.wait_ge(s3_sp3[j % 2], 16 * (j // 2))
                gpsimd.tensor_tensor(attnT_sb[j % 2][:], attnraw[j % 2][:],
                                     reciprep[j % 2][:], ALU.mult).then_inc(s3_nrm, 1)

        @block.scalar
        def _(scalar):
            as3 = SelfSem(scalar, s_as3)
            for h in range(H):
                scalar.wait_ge(s3_pd3, 2 * h + 1)
                if h >= 2:
                    scalar.wait_ge(s3_mm3, MMB + CPH * (h - 1))
                scalar.activation(qhat[h % 2][:], scr[h % 2][:, 0:1024], AF.Identity,
                                  scale=lnp_sb[:, h:h + 1],
                                  bias=lnp_sb[:, 32 + h:33 + h]).then_inc(s3_ak3, 1)
                scalar.wait_ge(s3_pd3, 2 * h + 2)
                scalar.activation(khat[h % 2][:], scr[h % 2][:, 1024:3072], AF.Identity,
                                  scale=lnp_sb[:, 64 + h:65 + h],
                                  bias=lnp_sb[:, 96 + h:97 + h]).then_inc(s3_ak3, 1)
                def emit_psT_odd(s):
                    sg = 8 * h + s
                    for gi, (t0, nt) in enumerate(GROUPS(s)):
                        gg = GPH * h + preG[s] + gi
                        if gg % 2 != 1:
                            continue
                        scalar.wait_ge(s3_tp3, gg + 1)
                        if sg >= 3:
                            scalar.wait_ge(s3_avs, sg - 2)
                        scalar.copy(probsT[sg % 3][:, t0 * 128:(t0 + nt) * 128],
                                    ptp[1][:, 0:nt * 128]).then_inc(s3_teA, 1)

                for s in range(8):
                    sg = 8 * h + s
                    if s >= 1:
                        emit_psT_odd(s - 1)
                    scalar.wait_ge(s3_mx, sg + 1)
                    if sg >= 3:
                        scalar.wait_ge(s3_tp3, GG_END(sg - 3))
                    if s == 0 and h >= 1:
                        scalar.wait_ge(s3_zr, h)
                    scalar.activation(probs[sg % 3][:, 0:LS[s] * 128],
                                      scores[sg % 2][:, 0:LS[s] * 128], AF.Exp,
                                      bias=negmax[h % 2][:, s:s + 1],
                                      accum_out=Zcols[h % 2][:, s:s + 1]).then_inc(s3_ex, 1)
                emit_psT_odd(7)
                scalar.wait_ge(s3_zr, h + 1)
                if h >= 1:
                    scalar.wait_ge(s3_zt, h)
                scalar.copy(recipZb[:, 0:8], recipZ[:, 0:8]).then_inc(s3_zc, 1)


        for g_ in reversed(ctx[ctx_p3_base:ctx_p3_end]):
            g_.__exit__(None, None, None)
        del ctx[ctx_p3_base:ctx_p3_end]

        NG4 = 2
        NM4 = 4
        proj_bf = sb("proj_bf", [128, NM4 * D], BF16)
        x2_4 = sb("x2_4", [128, NM4 * D], F32)
        bigT = sb("bigT", [128, NK * 512], BF16)
        w4buf = [sb(f"w4buf{j}", [128, NK * 256], BF16) for j in range(2)]
        xld4 = sb("xld4", [128, 2048], F32)
        x3h4 = sb("x3h4", [128, D], BF16)
        sqj4 = x3h4
        at_sb = sb("at_sb", [128, NK * 128], BF16)
        bt_sb = sb("bt_sb", [128, 16 * 256], BF16)
        rt_sb = sb("rt_sb", [128, NK * 8], BF16)
        brow = sb("brow", [1, D], BF16)
        ones4 = sb("ones4", [1, 128], BF16)
        hw4 = sb("hw4", [128, 128], BF16)
        hwT4 = sb("hwT4", [128, NM4 * 128], BF16)
        h4f = sb("h4f", [128, 128], F32)
        j256 = sb("j256", [128, 512], BF16)
        pps = sb("pps", [128, 128], F32)
        r8 = sb("r8", [128, 8], F32)
        e8 = sb("e8", [128, 8], F32)
        nm8 = sb("nm8", [128, 1], F32)
        t8 = sb("t8", [128, 8], F32)
        g8 = sb("g8", [128, 8], F32)
        m1c = sb("m1c", [128, 1], F32)
        m2c = sb("m2c", [128, 1], F32)
        rc8 = sb("rc8", [128, 1], F32)
        comb = sb("comb", [128, 8], F32)
        st4A = sb("st4A", [128, 2], F32)
        st4B = sb("st4B", [128, 2], F32)
        st4M = sb("st4M", [128, 2], F32)
        st4V = sb("st4V", [128, 2], F32)
        ln4st = sb("ln4st", [128, 4], F32)

        pac4 = [psum(f"pac4{j}", [128, 512], F32) for j in range(4)]
        ptr4 = [psum(f"ptr4{j}", [128, 1024], BF16) for j in range(2)]
        prt4 = psum("prt4", [128, 512], F32)
        ph4 = psum("ph4", [128, 512], F32)

        s4_w = [sem("s4_w0"), sem("s4_w1")]
        s4_wp = sem("s4_wp")
        s4_at = sem("s4_at")
        s4_x = sem("s4_x")
        s4_mo = sem("s4_mo")
        s4_bo = sem("s4_bo")
        s4_mf = sem("s4_mf")
        s4_bf = sem("s4_bf")
        s4_mr = sem("s4_mr")
        s4_br = sem("s4_br")
        s4_mh = sem("s4_mh")
        s4_bh = sem("s4_bh")
        s4_tx = sem("s4_tx")
        s4_te = sem("s4_te")
        s4_vd = sem("s4_vd")
        s4_ad = sem("s4_ad")
        s4_vr = sem("s4_vr")
        s4_ar = sem("s4_ar")
        s4_hw = sem("s4_hw")
        s_out = [sem(f"s_out{j}") for j in range(NM4)]
        s_vs4 = sem("s_vs4")
        s_as4 = sem("s_as4")

        CH4 = []
        for G4 in range(NG4):
            for typ in (2, 3, 4):
                for mloc in range(NM4):
                    CH4.append((typ, G4, mloc))

        VD_INC = {2: 4, 3: 2, 4: 3}

        def ch4_base(idx):
            vd = sum(VD_INC[CH4[i][0]] for i in range(idx))
            ad = 2 * idx
            return vd, ad

        def tx_ord(G4, kind, mloc, grp=0):
            base = 36 * G4
            if kind == "x3":
                return base + mloc * 8 + grp
            return base + 32 + mloc

        @block.sync
        def _(sync):
            sync.wait_ge(s3_zm, 2 * H)
            sync.wait_ge(s3_rp, 2 * H)
            sync.wait_ge(s3_zc, H)
            sync.wait_ge(s3_zd, 32 * H)
            sync.wait_ge(s3_nrm, H)
            sync.wait_ge(s3_sp3[0], 16 * (H // 2))
            sync.wait_ge(s3_sp3[1], 16 * (H // 2))
            for G4 in range(NG4):
                if G4 >= 1:
                    sync.wait_ge(s4_mf, 64 * G4)
                sync.dma_start(
                    out=bigT.ap().rearrange("p (k j) -> p k j", k=NK, j=512),
                    in_=aT_d.rearrange("(k d) t -> d k t", k=NK, d=128)[:, :, G4 * 512:(G4 + 1) * 512],
                ).then_inc(s4_at, 16)
                for mloc in range(NM4):
                    m = NM4 * G4 + mloc
                    ci = CH4.index((2, G4, mloc))
                    vb, _ = ch4_base(ci)
                    for hh in range(2):
                        if m + hh >= 1:
                            sync.wait_ge(s4_vd, vb + 2 + hh)
                        sync.dma_start(out=xld4[:],
                                       in_=x_in[m * 128:(m + 1) * 128, hh * 2048:(hh + 1) * 2048]).then_inc(s4_x, 16)
                for mloc in range(NM4):
                    m = NM4 * G4 + mloc
                    ci = CH4.index((4, G4, mloc))
                    vb, ab = ch4_base(ci)
                    sync.wait_ge(s4_vd, vb + 3)
                    sync.dma_start(out=out[m * 128:(m + 1) * 128, :],
                                   in_=x2_4.ap()[:, mloc * D:(mloc + 1) * D]).then_inc(s_out[mloc], 16)
            for mloc in range(NM4):
                sync.wait_ge(s_out[mloc], 16 * NG4)

        @block.gpsimd
        def _(gpsimd):
            gpsimd.wait_ge(s3_nrm, H)
            gpsimd.wait_ge(s3_sp3[0], 16 * (H // 2))
            gpsimd.wait_ge(s3_sp3[1], 16 * (H // 2))
            gpsimd.wait_ge(s3_zm, 2 * H)
            gpsimd.wait_ge(s3_rp, 2 * H)
            gpsimd.wait_ge(s3_zc, H)
            gpsimd.wait_ge(s3_zd, 32 * H)
            gpsimd.dma_start(out=at_sb.ap().rearrange("p (k n) -> p k n", k=NK, n=128),
                             in_=wa_in).then_inc(s4_wp, 16)
            gpsimd.dma_start(out=bt_sb.ap().rearrange("p (k n) -> p k n", k=16, n=256),
                             in_=wb_in).then_inc(s4_wp, 16)
            gpsimd.dma_start(out=rt_sb.ap().rearrange("p (k n) -> p k n", k=NK, n=8),
                             in_=wr_in).then_inc(s4_wp, 16)
            gpsimd.dma_start(out=brow[:, :], in_=bb_in[:, :]).then_inc(s4_wp, 16)
            for G4 in range(NG4):
                for c4 in range(32):
                    widx = 32 * G4 + c4
                    src = wout_in[c4] if c4 < 16 else wbase_in[c4 - 16]
                    if widx >= 2:
                        prev = widx - 2
                        pG, pc = divmod(prev, 32)
                        if pc < 16:
                            gpsimd.wait_ge(s4_mo, 64 * pG + 4 * (pc + 1))
                        else:
                            gpsimd.wait_ge(s4_mf, 64 * pG + 4 * (pc - 16 + 1))
                    gpsimd.dma_start(
                        out=w4buf[widx % 2].ap().rearrange("p (k n) -> p k n", k=NK, n=256),
                        in_=src,
                    ).then_inc(s4_w[widx % 2], 16)

        @block.tensor
        def _(tensor):
            tensor.wait_ge(s3_nrm, H)
            tensor.wait_ge(s3_rp, 2 * H)
            tensor.wait_ge(s3_zc, H)
            tensor.wait_ge(s3_zm, 2 * H)
            tensor.wait_ge(s3_zd, 32 * H)
            tensor.wait_ge(s3_sp3[0], 16 * 16)
            tensor.wait_ge(s3_sp3[1], 16 * 16)
            tensor.wait_ge(s4_wp, 64)
            for G4 in range(NG4):
                tensor.wait_ge(s4_at, 16 * (G4 + 1))
                for nn in range(16):
                    widx = 32 * G4 + nn
                    tensor.wait_ge(s4_w[widx % 2], 16 * (widx // 2 + 1))
                    for mloc in range(NM4):
                        oi = 64 * G4 + 4 * nn + mloc
                        if oi >= 4:
                            tensor.wait_ge(s4_bo, oi - 3)
                        bk = pac4[oi % 4]
                        for k in range(NK):
                            ins = tensor.matmul(bk[:, 0:256],
                                                bigT[:, k * 512 + mloc * 128: k * 512 + mloc * 128 + 128],
                                                w4buf[widx % 2][:, k * 256:(k + 1) * 256],
                                                start=(k == 0), stop=(k == NK - 1))
                        ins.then_inc(s4_mo, 1)
                for mloc in range(NM4):
                    ci = CH4.index((3, G4, mloc))
                    vb, ab = ch4_base(ci)
                    tensor.wait_ge(s4_ad, ab + 2)
                    for grp in range(8):
                        gi = tx_ord(G4, "x3", mloc, grp)
                        if gi >= 2:
                            tensor.wait_ge(s4_te, gi - 1)
                        tb = ptr4[gi % 2]
                        for j in range(4):
                            k = grp * 4 + j
                            ins = tensor.transpose(tb[:, j * 128:(j + 1) * 128],
                                                   x3h4[:, k * 128:(k + 1) * 128],
                                                   id_bf[:, :])
                        ins.then_inc(s4_tx, 1)
                for mloc in range(NM4):
                    im = NM4 * G4 + mloc
                    if im >= 1:
                        tensor.wait_ge(s4_br, im)
                    tensor.wait_ge(s4_te, tx_ord(G4, "x3", mloc, 7) + 1)
                    for k in range(NK):
                        ins = tensor.matmul(prt4[:, 0:8],
                                            bigT[:, k * 512 + mloc * 128: k * 512 + mloc * 128 + 128],
                                            rt_sb[:, k * 8:(k + 1) * 8],
                                            start=(k == 0), stop=(k == NK - 1))
                    ins.then_inc(s4_mr, 1)
                    if im >= 1:
                        tensor.wait_ge(s4_bh, im)
                    for k in range(NK):
                        ins = tensor.matmul(ph4[:, 0:128],
                                            bigT[:, k * 512 + mloc * 128: k * 512 + mloc * 128 + 128],
                                            at_sb[:, k * 128:(k + 1) * 128],
                                            start=(k == 0), stop=(k == NK - 1))
                    ins.then_inc(s4_mh, 1)
                    gi = tx_ord(G4, "hw", mloc)
                    tensor.wait_ge(s4_hw, im + 1)
                    tensor.wait_ge(s4_te, gi - 1)
                    tensor.transpose(ptr4[gi % 2][:, 0:128], hw4[:, 0:128],
                                     id_bf[:, :]).then_inc(s4_tx, 1)
                for nn in range(16):
                    widx = 32 * G4 + 16 + nn
                    tensor.wait_ge(s4_w[widx % 2], 16 * (widx // 2 + 1))
                    for mloc in range(NM4):
                        oi = 64 * G4 + 4 * nn + mloc
                        if oi >= 4:
                            tensor.wait_ge(s4_bf, oi - 3)
                        tensor.wait_ge(s4_te, tx_ord(G4, "hw", mloc) + 1)
                        bk = pac4[oi % 4]
                        for k in range(NK):
                            tensor.matmul(bk[:, 0:256],
                                          bigT[:, k * 512 + mloc * 128: k * 512 + mloc * 128 + 128],
                                          w4buf[widx % 2][:, k * 256:(k + 1) * 256],
                                          start=(k == 0), stop=False)
                        tensor.matmul(bk[:, 0:256],
                                      hwT4[:, mloc * 128:(mloc + 1) * 128],
                                      bt_sb[:, nn * 256:(nn + 1) * 256],
                                      start=False, stop=False)
                        ins = tensor.matmul(bk[:, 0:256], ones4[0:1, :],
                                            brow[0:1, nn * 256:(nn + 1) * 256],
                                            start=False, stop=True)
                        ins.then_inc(s4_mf, 1)

        @block.vector
        def _(vector):
            vs4 = SelfSem(vector, s_vs4)
            vector.wait_ge(s3_nrm, H)
            vector.wait_ge(s3_rp, 2 * H)
            vector.wait_ge(s3_zc, H)
            vector.wait_ge(s3_zm, 2 * H)
            vector.wait_ge(s3_zd, 32 * H)
            vector.wait_ge(s3_sp3[0], 16 * 16)
            vector.wait_ge(s3_sp3[1], 16 * 16)
            ins = vector.memset(ones4[:, :], 1.0)
            vs4.edge(ins)

            def evacs(G4, which):
                sem_m = s4_mo if which == "o" else s4_mf
                sem_b = s4_bo if which == "o" else s4_bf
                pend = []
                for nn in range(16):
                    for mloc in range(NM4):
                        oi = 64 * G4 + 4 * nn + mloc
                        vector.wait_ge(sem_m, oi - 64 * G4 + 64 * G4 + 1)
                        if which == "f":
                            ci2 = CH4.index((2, G4, mloc))
                            vb2, _ = ch4_base(ci2)
                            vector.wait_ge(s4_vd, vb2 + 4)
                        col = mloc * 16 + nn
                        dst = proj_bf.ap()[:, mloc * D + nn * 256: mloc * D + nn * 256 + 256]
                        vector.tensor_scalar(dst, pac4[oi % 4][:, 0:256], 1.0, None,
                                             ALU.mult, ALU.add,
                                             accum_out=pps[:, col:col + 1]).then_inc(sem_b, 1)
                        pend.append((oi, dst, col))
                        if len(pend) >= 2:
                            poi, pdst, pcol = pend.pop(0)
                            vector.wait_ge(sem_b, poi - 64 * G4 + 64 * G4 + 1)
                            vector.scalar_tensor_tensor(j256[:, (poi % 2) * 256:(poi % 2) * 256 + 256],
                                                        pdst, 1.0, pdst, ALU.bypass, ALU.mult,
                                                        accum_out=pps[:, 64 + pcol:65 + pcol])
                for poi, pdst, pcol in pend:
                    vector.wait_ge(sem_b, poi - 64 * G4 + 64 * G4 + 1)
                    vector.scalar_tensor_tensor(j256[:, (poi % 2) * 256:(poi % 2) * 256 + 256],
                                                pdst, 1.0, pdst, ALU.bypass, ALU.mult,
                                                accum_out=pps[:, 64 + pcol:65 + pcol])
                ins = vector.nop()
                vs4.edge(ins)

            def ln_chain_dve(typ, G4, mloc):
                ci = CH4.index((typ, G4, mloc))
                vb, ab = ch4_base(ci)
                pi = ci % 2
                if typ == 3:
                    rowf = x2_4.ap()[:, mloc * D:(mloc + 1) * D]
                    vb2, _ = ch4_base(CH4.index((2, G4, mloc)))
                    vector.wait_ge(s4_vd, vb2 + 4)
                    if mloc >= 1:
                        vector.wait_ge(s4_tx, tx_ord(G4, "x3", mloc - 1, 7) + 1)
                    elif G4 >= 1:
                        vector.wait_ge(s4_tx, tx_ord(G4 - 1, "x3", NM4 - 1, 7) + 1)
                    vector.reduce_sum(st4A[:, pi:pi + 1], rowf, axis=AX.X)
                    vector.scalar_tensor_tensor(sqj4[:], rowf, 1.0, rowf, ALU.bypass,
                                                ALU.mult, accum_out=st4B[:, pi:pi + 1]).then_inc(s4_vd, 1)
                else:
                    vector.reduce_sum(st4A[:, pi:pi + 1], pps[:, mloc * 16: mloc * 16 + 16], axis=AX.X)
                    vector.reduce_sum(st4B[:, pi:pi + 1], pps[:, 64 + mloc * 16: 64 + mloc * 16 + 16],
                                      axis=AX.X).then_inc(s4_vd, 1)
                vector.wait_ge(s4_ad, ab + 1)
                vector.scalar_tensor_tensor(ln4st[:, 2 * pi:2 * pi + 1], st4B[:, pi:pi + 1], 1.0 / D,
                                            st4V[:, pi:pi + 1], ALU.mult, ALU.subtract).then_inc(s4_vd, 1)
                if typ == 2:
                    vector.wait_ge(s4_ad, ab + 2)
                    m = NM4 * G4 + mloc
                    for hh in range(2):
                        vector.wait_ge(s4_x, 32 * m + 16 * (hh + 1))
                        rowx = x2_4.ap()[:, mloc * D + hh * 2048: mloc * D + (hh + 1) * 2048]
                        vector.tensor_tensor(rowx, rowx, xld4[:], ALU.add).then_inc(s4_vd, 1)
                elif typ == 4:
                    vector.wait_ge(s4_ad, ab + 2)
                    rowx = x2_4.ap()[:, mloc * D:(mloc + 1) * D]
                    vector.tensor_tensor(rowx, rowx,
                                         proj_bf.ap()[:, mloc * D:(mloc + 1) * D], ALU.add).then_inc(s4_vd, 1)

            for G4 in range(NG4):
                if G4 >= 1:
                    vb_last, _ = ch4_base(CH4.index((4, G4 - 1, NM4 - 1)))
                    vector.wait_ge(s4_vd, vb_last + 3)
                evacs(G4, "o")
                for mloc in range(NM4):
                    ln_chain_dve(2, G4, mloc)
                for mloc in range(NM4):
                    ln_chain_dve(3, G4, mloc)
                for mloc in range(NM4):
                    im = NM4 * G4 + mloc
                    vector.wait_ge(s4_mr, im + 1)
                    if im >= 1:
                        vector.wait_ge(s4_ar, im)
                    vector.tensor_copy(r8[:, 0:8], prt4[:, 0:8]).then_inc(s4_br, 1)
                    vector.wait_ge(s4_br, im + 1)
                    ins = vector.reduce_max(nm8[:, 0:1], r8[:, 0:8], axis=AX.X, negate=True)
                    ins.then_inc(s4_vr, 1)
                    vector.wait_ge(s4_ar, im + 1)
                    ins = vector.reduce_max(m1c[:, 0:1], e8[:, 0:8], axis=AX.X)
                    vs4.edge(ins)
                    ins = vector.tensor_scalar(t8[:, 0:8], e8[:, 0:8], m1c[:, 0:1], None, ALU.is_ge)
                    vs4.edge(ins)
                    ins = vector.scalar_tensor_tensor(t8[:, 0:8], t8[:, 0:8], -1e30, e8[:, 0:8],
                                                      ALU.mult, ALU.add)
                    vs4.edge(ins)
                    ins = vector.reduce_max(m2c[:, 0:1], t8[:, 0:8], axis=AX.X)
                    vs4.edge(ins)
                    ins = vector.tensor_scalar(g8[:, 0:8], e8[:, 0:8], m2c[:, 0:1], None, ALU.is_ge)
                    vs4.edge(ins)
                    ins = vector.tensor_tensor(g8[:, 0:8], g8[:, 0:8], e8[:, 0:8], ALU.mult)
                    vs4.edge(ins)
                    ins = vector.tensor_tensor(m1c[:, 0:1], m1c[:, 0:1], m2c[:, 0:1], ALU.add)
                    vs4.edge(ins)
                    ins = vector.reciprocal(rc8[:, 0:1], m1c[:, 0:1])
                    vs4.edge(ins)
                    ins = vector.tensor_scalar(comb[:, 0:8], g8[:, 0:8], rc8[:, 0:1], 2.0,
                                               ALU.mult, ALU.mult)
                    vs4.edge(ins)
                    vector.wait_ge(s4_mh, im + 1)
                    vector.tensor_copy(h4f[:, 0:128], ph4[:, 0:128]).then_inc(s4_bh, 1)
                    vector.wait_ge(s4_bh, im + 1)
                    if im >= 1:
                        pg, pm = divmod(im - 1, NM4)
                        vector.wait_ge(s4_tx, tx_ord(pg, "hw", pm) + 1)
                    for e in range(8):
                        ins = vector.tensor_single_scalar(hw4[:, e * 16:(e + 1) * 16],
                                                          h4f[:, e * 16:(e + 1) * 16],
                                                          comb[:, e:e + 1], ALU.mult)
                    ins.then_inc(s4_hw, 1)
                evacs(G4, "f")
                for mloc in range(NM4):
                    ln_chain_dve(4, G4, mloc)

        @block.scalar
        def _(scalar):
            as4 = SelfSem(scalar, s_as4)
            scalar.wait_ge(s3_nrm, H)
            scalar.wait_ge(s3_rp, 2 * H)
            scalar.wait_ge(s3_zc, H)
            scalar.wait_ge(s3_zm, 2 * H)
            scalar.wait_ge(s3_zd, 32 * H)
            scalar.wait_ge(s3_sp3[0], 16 * 16)
            scalar.wait_ge(s3_sp3[1], 16 * 16)

            def ln_chain_act(typ, G4, mloc):
                ci = CH4.index((typ, G4, mloc))
                vb, ab = ch4_base(ci)
                pi = ci % 2
                scalar.wait_ge(s4_vd, vb + 1)
                ins = scalar.mul(st4M[:, pi:pi + 1], st4A[:, pi:pi + 1], 1.0 / D)
                as4.edge(ins)
                scalar.activation(st4V[:, pi:pi + 1], st4M[:, pi:pi + 1], AF.Square).then_inc(s4_ad, 1)
                scalar.wait_ge(s4_vd, vb + 2)
                ins = scalar.activation(ln4st[:, 2 * pi + 1:2 * pi + 2], ln4st[:, 2 * pi:2 * pi + 1],
                                        AF.Ln, bias=eps_sb[:, 0:1])
                as4.edge(ins)
                ins = scalar.activation(ln4st[:, 2 * pi + 1:2 * pi + 2], ln4st[:, 2 * pi + 1:2 * pi + 2],
                                        AF.Exp, scale=-0.5)
                as4.edge(ins)
                ins = scalar.activation(ln4st[:, 2 * pi:2 * pi + 1], st4M[:, pi:pi + 1],
                                        AF.Identity, scale=ln4st[:, 2 * pi + 1:2 * pi + 2])
                as4.edge(ins)
                ins = scalar.mul(ln4st[:, 2 * pi:2 * pi + 1], ln4st[:, 2 * pi:2 * pi + 1], -1.0)
                as4.edge(ins)
                sc = ln4st[:, 2 * pi + 1:2 * pi + 2]
                bi = ln4st[:, 2 * pi:2 * pi + 1]
                if typ == 2:
                    if G4 >= 1:
                        scalar.wait_ge(s_out[mloc], 16 * G4)
                    scalar.activation(x2_4.ap()[:, mloc * D:(mloc + 1) * D],
                                      proj_bf.ap()[:, mloc * D:(mloc + 1) * D],
                                      AF.Identity, scale=sc, bias=bi).then_inc(s4_ad, 1)
                elif typ == 3:
                    pci = ci - 1
                    if CH4[pci][0] == 3:
                        scalar.wait_ge(s4_tx, tx_ord(G4, "x3", mloc - 1, 7) + 1)
                    elif G4 >= 1:
                        scalar.wait_ge(s4_tx, tx_ord(G4 - 1, "x3", NM4 - 1, 7) + 1)
                    scalar.activation(x3h4[:], x2_4.ap()[:, mloc * D:(mloc + 1) * D],
                                      AF.Identity, scale=sc, bias=bi).then_inc(s4_ad, 1)
                else:
                    scalar.activation(proj_bf.ap()[:, mloc * D:(mloc + 1) * D],
                                      proj_bf.ap()[:, mloc * D:(mloc + 1) * D],
                                      AF.Identity, scale=sc, bias=bi).then_inc(s4_ad, 1)

            for G4 in range(NG4):
                for mloc in range(NM4):
                    ln_chain_act(2, G4, mloc)
                for mloc in range(NM4):
                    ln_chain_act(3, G4, mloc)
                    for grp in range(8):
                        gi = tx_ord(G4, "x3", mloc, grp)
                        scalar.wait_ge(s4_tx, gi + 1)
                        k0 = grp * 4
                        dst = bigT.ap().rearrange("p (k j) -> p k j", k=NK, j=512)[
                            :, k0:k0 + 4, mloc * 128:(mloc + 1) * 128]
                        srcp = ptr4[gi % 2][:, 0:512].rearrange("p (k j) -> p k j", k=4, j=128)
                        scalar.copy(dst, srcp).then_inc(s4_te, 1)
                for mloc in range(NM4):
                    im = NM4 * G4 + mloc
                    scalar.wait_ge(s4_vr, im + 1)
                    scalar.activation(e8[:, 0:8], r8[:, 0:8], AF.Exp,
                                      bias=nm8[:, 0:1]).then_inc(s4_ar, 1)
                    gi = tx_ord(G4, "hw", mloc)
                    scalar.wait_ge(s4_tx, gi + 1)
                    if G4 >= 1:
                        scalar.wait_ge(s4_mf, 64 * G4)
                    scalar.copy(hwT4[:, mloc * 128:(mloc + 1) * 128],
                                ptr4[gi % 2][:, 0:128]).then_inc(s4_te, 1)
                for mloc in range(NM4):
                    ln_chain_act(4, G4, mloc)

    for g in reversed(ctx):
        g.__exit__(None, None, None)
    return nc, dbg



def prep_inputs(inputs):
    wq = np.ascontiguousarray(inputs["L0_attn_Wqkv"].T)
    tiles = wq.reshape(NK, 128, 24, 512).transpose(2, 0, 1, 3)
    order = []
    for kind, a in SCHED:
        base = {"q": 0, "k": 8, "v": 16}[kind]
        order.append(base + a)
    wqkv_t = np.ascontiguousarray(tiles[order].transpose(0, 2, 1, 3))

    lnp_arr = np.zeros((128, 4 * 32), np.float32)
    lnp_arr[:, 0:32] = inputs["L0_attn_q_ln_w"].reshape(32, 128).T * SCALE
    lnp_arr[:, 32:64] = inputs["L0_attn_q_ln_b"].reshape(32, 128).T * SCALE
    lnp_arr[:, 64:96] = inputs["L0_attn_k_ln_w"].reshape(32, 128).T
    lnp_arr[:, 96:128] = inputs["L0_attn_k_ln_b"].reshape(32, 128).T

    ident = np.eye(128, dtype=ml_dtypes.bfloat16)
    identf = np.eye(128, dtype=np.float32)

    wout_t = np.ascontiguousarray(
        inputs["L0_attn_Wout"].T.reshape(NK, 128, 16, 256).transpose(2, 1, 0, 3))
    wbase_t = np.ascontiguousarray(
        inputs["L0_ffn_W_base"].T.reshape(NK, 128, 16, 256).transpose(2, 1, 0, 3))
    wa_t = np.ascontiguousarray(
        np.asarray(inputs["L0_ffn_A"]).reshape(128, D).T.reshape(NK, 128, 128).transpose(1, 0, 2))
    wb_t = np.ascontiguousarray(
        np.asarray(inputs["L0_ffn_B"]).transpose(0, 2, 1).reshape(128, D).reshape(128, 16, 256))
    wr_t = np.ascontiguousarray(
        np.asarray(inputs["L0_ffn_W_router"]).T.reshape(NK, 128, 8).transpose(1, 0, 2))
    bb = np.ascontiguousarray(np.asarray(inputs["L0_ffn_b_base"]).reshape(1, D))
    return dict(wqkv=wqkv_t, lnp=lnp_arr, ident=ident, identf=identf,
                wout=wout_t, wbase=wbase_t, wa=wa_t, wb=wb_t, wr=wr_t, bb=bb)


def core_x(inputs, c):
    x = np.asarray(inputs["L0_attn_hidden_states"])
    seq, par = c // 2, c % 2
    return np.ascontiguousarray(x[seq].reshape(16, 128, D)[par::2].reshape(RTOK, D))


def core_dtab(c):
    p = c % 2
    out = np.full((NDTAB, 128, 128), -1e33, np.float32)
    jj = np.arange(128)[None, :]
    ii = np.arange(128)[:, None]
    idx = 0
    for s in range(8):
        Q = 2 * s + p
        for t in range(LS[s]):
            Dv = (t * 128 + jj) - (Q * 128 + ii)
            out[idx] = np.where(Dv <= 0, Dv.astype(np.float32), np.float32(-1e33))
            idx += 1
    assert idx == NDTAB
    return out



_CACHE = {}


def kernel(**inputs):
    inputs = {k: np.asarray(v) for k, v in inputs.items()}
    if "nc" not in _CACHE:
        nc, _ = build(debug=False)
        _CACHE["nc"] = nc
    nc = _CACHE["nc"]
    shared = prep_inputs(inputs)
    in_maps = []
    for c in range(NCORE):
        m = dict(shared)
        m["x"] = core_x(inputs, c)
        m["dtab"] = core_dtab(c)
        in_maps.append(m)
    res = run_bass_kernel_spmd(nc, in_maps, core_ids=list(range(NCORE)))
    out = np.empty((B, S, D), np.float32)
    view = out.reshape(B, 16, 128, D)
    for c in range(NCORE):
        seq, par = c // 2, c % 2
        view[seq, par::2] = np.asarray(res.results[c]["out"]).reshape(NM, 128, D)
    return out


# version 07 (final)
# speedup vs baseline: 1.2039x; 1.0750x over previous
import numpy as np
import ml_dtypes
import concourse.bass as bass
import concourse.mybir as mybir
from concourse.bass_utils import run_bass_kernel_spmd

F32 = mybir.dt.float32
BF16 = mybir.dt.bfloat16
AF = mybir.ActivationFunctionType
ALU = mybir.AluOpType
AX = mybir.AxisListType

B, S, D, H, DH = 4, 2048, 4096, 32, 128
NCORE = 8
RTOK = 1024
NM = 8
NK = 32
CLIP = 8.0
EPS = 1e-5
SCALE = DH ** -0.5
PAIRS = [[2 * i, 2 * i + 1] for i in range(4)]

SCHED = []
for g in range(4):
    SCHED += [("k", 2 * g), ("k", 2 * g + 1), ("v", 2 * g), ("v", 2 * g + 1)]
SCHED += [("q", a) for a in range(8)]
QK_ORD, V_ORD, QK_SCHED, V_SCHED = {}, {}, [], []
for _i, (_kind, _a) in enumerate(SCHED):
    if _kind == "v":
        V_ORD[_i] = len(V_SCHED); V_SCHED.append(_i)
    else:
        QK_ORD[_i] = len(QK_SCHED); QK_SCHED.append(_i)

LS = [2 * s + 2 for s in range(8)]
TSTART = [sum(LS[:s]) for s in range(8)]
NDTAB = sum(LS)
NCH = 24 * NM
P0_GRPS = NM * 8
P1_GRPS = len(QK_SCHED) * NM


import math as _math
CHW = []
for _s in range(8):
    w = LS[_s] * 128
    CHW.append([min(512, w - c * 512) for c in range((w + 511) // 512)])
preC = [0]
for _s in range(8):
    preC.append(preC[-1] + len(CHW[_s]))
CPH = preC[-1]


def GROUPS(s):
    return [(t0, min(8, LS[s] - t0)) for t0 in range(0, LS[s], 8)]


GPH = sum(len(GROUPS(_s)) for _s in range(8))
preG = [0]
for _s in range(8):
    preG.append(preG[-1] + len(GROUPS(_s)))


def GG_END(sg):
    h, s = divmod(sg, 8)
    return GPH * h + preG[s + 1]


SLOPES = [2.0 ** (-0.25 * (hh + 1)) for hh in range(H)]
MMB = 12


class SelfSem:
    def __init__(self, eng, semh):
        self.eng, self.sem, self.n = eng, semh, 0

    def edge(self, instr):
        instr.then_inc(self.sem, 1)
        self.n += 1
        self.eng.wait_ge(self.sem, self.n)


def build(debug=True):
    nc = bass.Bass("TRN2", target_bir_lowering=False, num_devices=NCORE)

    x_in = nc.dram_tensor("x", [RTOK, D], F32, kind="ExternalInput").ap()
    wqkv = nc.dram_tensor("wqkv", [24, 128, NK, 512], F32, kind="ExternalInput").ap()
    lnp = nc.dram_tensor("lnp", [128, 4 * 32], F32, kind="ExternalInput").ap()
    ident_in = nc.dram_tensor("ident", [128, 128], BF16, kind="ExternalInput").ap()
    identf_in = nc.dram_tensor("identf", [128, 128], F32, kind="ExternalInput").ap()
    out = nc.dram_tensor("out", [RTOK, D], F32, kind="ExternalOutput").ap()

    cin = [nc.dram_tensor(f"cin{g}", [2048, 1024], BF16, kind="Internal").ap() for g in range(4)]
    cout = [nc.dram_tensor(f"cout{g}", [4096, 1024], BF16, kind="Internal").ap() for g in range(4)]
    sin = nc.dram_tensor("sin", [16, 128], F32, kind="Internal").ap()
    sout = nc.dram_tensor("sout", [32, 128], F32, kind="Internal").ap()
    qT_d = nc.dram_tensor("qT_d", [D, RTOK], BF16, kind="Internal").ap()
    qs_d = nc.dram_tensor("qs_d", [32, 128], F32, kind="Internal").ap()
    aT_d = nc.dram_tensor("aT_d", [D, RTOK], BF16, kind="Internal").ap()
    zb_d = nc.dram_tensor("zb_d", [8, 128], BF16, kind="Internal").ap()
    dtab_in = nc.dram_tensor("dtab", [NDTAB, 128, 128], F32, kind="ExternalInput").ap()
    wout_in = nc.dram_tensor("wout", [16, 128, NK, 256], F32, kind="ExternalInput").ap()
    wbase_in = nc.dram_tensor("wbase", [16, 128, NK, 256], F32, kind="ExternalInput").ap()
    wa_in = nc.dram_tensor("wa", [128, NK, 128], F32, kind="ExternalInput").ap()
    wb_in = nc.dram_tensor("wb", [128, 16, 256], F32, kind="ExternalInput").ap()
    wr_in = nc.dram_tensor("wr", [128, NK, 8], F32, kind="ExternalInput").ap()
    bb_in = nc.dram_tensor("bb", [1, D], F32, kind="ExternalInput").ap()

    dbg = {}
    if debug:
        dbg["d_qT"] = nc.dram_tensor("d_qT", [D, RTOK], BF16, kind="ExternalOutput").ap()
        dbg["d_kv"] = nc.dram_tensor("d_kv", [4, 4096, 1024], BF16, kind="ExternalOutput").ap()
        dbg["d_st"] = nc.dram_tensor("d_st", [32, 128], F32, kind="ExternalOutput").ap()
        dbg["d_xT"] = nc.dram_tensor("d_xT", [128, NK * RTOK], BF16, kind="ExternalOutput").ap()
        dbg["d_aT"] = nc.dram_tensor("d_aT", [D, RTOK], BF16, kind="ExternalOutput").ap()

    ctx = []

    def sb(name, shape, dtype):
        g = nc.sbuf_tensor(name, shape, dtype); t = g.__enter__(); ctx.append(g); return t

    def psum(name, shape, dtype):
        g = nc.psum_tensor(name, shape, dtype); t = g.__enter__(); ctx.append(g); return t

    def sem(name):
        g = nc.semaphore(name); s = g.__enter__(); ctx.append(g); return s

    id_bf = sb("id_bf", [128, 128], BF16)
    id_f = sb("id_f", [128, 128], F32)
    lnp_sb = sb("lnp_sb", [128, 4 * 32], F32)
    eps_sb = sb("eps_sb", [128, 1], F32)

    ctx_p1_base = len(ctx)
    xhatT = sb("xhatT", [128, NK * RTOK], BF16)
    wbuf = [sb(f"wbuf{j}", [128, NK * 512], BF16) for j in range(2)]
    neg8 = sb("neg8", [128, 512], F32)
    xld = [sb("xld0", [128, D], F32)] * 2
    xhat = [sb("xhat0", [128, D], BF16)] * 2
    sqj = sb("sqj", [128, D], BF16)
    qring = [sb(f"qr{j}", [128, 512], BF16) for j in range(4)]
    tring = [sb(f"tr{j}", [128, 4096], BF16) for j in range(2)]
    vring = [sb(f"vr{j}", [128, 4096], BF16) for j in range(2)]
    psums = sb("psums", [128, 128], F32)
    psumsq = sb("psumsq", [128, 128], F32)
    stA = sb("stA", [128, 8], F32)
    stB = sb("stB", [128, 8], F32)
    stM = sb("stM", [128, 8], F32)
    stV = sb("stV", [128, 8], F32)
    allst = sb("allst", [128, 32], F32)
    ln1st = sb("ln1st", [128, 2], F32)
    st32 = sb("st32", [32, 128], F32)

    pacc = [psum(f"pacc{j}", [128, 512], F32) for j in range(3)]
    ptr = [psum(f"ptr{j}", [128, 1024], BF16) for j in range(2)]
    pst = psum("pst", [128, 128], F32)
    ctx_p1_end = len(ctx)

    s_x = sem("s_x")
    s_x2 = sem("s_x2")
    s_w = [sem("s_w0"), sem("s_w1")]
    s_mm = sem("s_mm")
    s_bk = sem("s_bk")
    s_tp = sem("s_tp")
    s_ev = sem("s_ev")
    s_tsp = [sem("s_tsp0"), sem("s_tsp1")]
    s_vsp = [sem("s_vsp0"), sem("s_vsp1")]
    s_ssp = sem("s_ssp")
    s_ccg = [sem(f"s_ccg{g}") for g in range(4)]
    s_ccs = sem("s_ccs")
    s_vd = sem("s_vd")
    s_ad = sem("s_ad")
    s_st = sem("s_st")
    s_dbg = sem("s_dbg")
    s_vself = sem("s_vself")
    s_aself = sem("s_aself")

    STATS_TP = P0_GRPS + P1_GRPS

    with nc.Block() as block:

        @block.sync
        def _(sync):
            sync.dma_start(out=id_bf[:], in_=ident_in[:, :]).then_inc(s_x, 16)
            sync.dma_start(out=id_f[:], in_=identf_in[:, :]).then_inc(s_x, 16)
            sync.dma_start(out=lnp_sb[:], in_=lnp[:, :]).then_inc(s_x, 16)
            for m in range(NM):
                if m >= 1:
                    sync.wait_ge(s_ad, 2 * (m - 1) + 2)
                sync.dma_start(out=xld[m % 2][:], in_=x_in[m * 128:(m + 1) * 128, :]).then_inc(s_x2, 16)
            for i, (kind, a) in enumerate(SCHED):
                if kind == "v":
                    sync.wait_ge(s_bk, i * NM + NM)
                    par = V_ORD[i] % 2
                    g = a // 2
                    hl0 = (a % 2) * 4
                    flat = cin[g].rearrange("r c -> (r c)")
                    vsrc = vring[par].ap().rearrange("p (m hl d) -> p m hl d", m=NM, hl=4, d=128)
                    for hl in range(4):
                        base = 1024 * 1024 + (hl0 + hl) * 131072
                        dst3 = flat[base: base + 131072].rearrange("(m p d) -> p m d", m=NM, p=128, d=128)
                        sync.dma_start(out=dst3, in_=vsrc[:, :, hl, :]).then_inc(s_vsp[V_ORD[i] % 2], 16)
                else:
                    sync.wait_ge(s_ev, P0_GRPS + NM * (QK_ORD[i] + 1))
                    par = QK_ORD[i] % 2
                    if kind == "k":
                        g = a // 2
                        hl0 = (a % 2) * 4
                        dst = cin[g][hl0 * 128:(hl0 + 4) * 128, :]
                    else:
                        dst = qT_d[a * 512:(a + 1) * 512, :]
                    dst3 = dst.rearrange("(hl d) t -> d hl t", hl=4, d=128)
                    src3 = tring[par].ap().rearrange("p (hl t) -> p hl t", hl=4, t=1024)
                    sync.dma_start(out=dst3, in_=src3).then_inc(s_tsp[QK_ORD[i] % 2], 16)
            sync.wait_ge(s_vd, 23)
            sync.dma_start(out=qs_d[:, :], in_=st32[:, :]).then_inc(s_ssp, 16)
            sync.dma_start(out=sin[:, :], in_=st32[16:32, :]).then_inc(s_ssp, 16)

            if debug:
                for g in range(4):
                    sync.wait_ge(s_ccg[g], 1)
                sync.wait_ge(s_ccs, 1)
                sync.wait_ge(s_tsp[0], 16 * 8)
                sync.wait_ge(s_tsp[1], 16 * 8)
                sync.dma_start(out=dbg["d_qT"][:, :], in_=qT_d[:, :]).then_inc(s_dbg, 16)
                for g in range(4):
                    sync.dma_start(out=dbg["d_kv"][g], in_=cout[g][:, :]).then_inc(s_dbg, 16)
                sync.dma_start(out=dbg["d_st"][:, :], in_=sout[:, :]).then_inc(s_dbg, 16)
                sync.dma_start(out=dbg["d_xT"][:, :], in_=xhatT[:]).then_inc(s_dbg, 16)
                sync.wait_ge(s_dbg, 16 * 7)

        @block.gpsimd
        def _(gpsimd):
            for nn in range(24):
                if nn >= 2:
                    gpsimd.wait_ge(s_mm, (nn - 1) * NM)
                gpsimd.dma_start(
                    out=wbuf[nn % 2].ap().rearrange("p (k n) -> p k n", k=NK, n=512),
                    in_=wqkv[nn],
                ).then_inc(s_w[nn % 2], 16)
            for g in range(4):
                gpsimd.wait_ge(s_tsp[0], 16 * (g + 1))
                gpsimd.wait_ge(s_tsp[1], 16 * (g + 1))
                gpsimd.wait_ge(s_vsp[0], 64 * (g + 1))
                gpsimd.wait_ge(s_vsp[1], 64 * (g + 1))
                gpsimd.collective_compute(
                    "AllGather", ALU.bypass, replica_groups=PAIRS,
                    ins=[cin[g][:, :]], outs=[cout[g][:, :]],
                ).then_inc(s_ccg[g], 1)
            gpsimd.wait_ge(s_ssp, 32)
            gpsimd.collective_compute(
                "AllGather", ALU.bypass, replica_groups=PAIRS,
                ins=[sin[:, :]], outs=[sout[:, :]],
            ).then_inc(s_ccs, 1)

        @block.tensor
        def _(tensor):
            tensor.wait_ge(s_x, 48)
            for m in range(NM):
                tensor.wait_ge(s_ad, 2 * m + 2)
                for kg in range(8):
                    gi = m * 8 + kg
                    if gi >= 2:
                        tensor.wait_ge(s_ev, gi - 1)
                    tb = ptr[gi % 2]
                    for j in range(4):
                        k = kg * 4 + j
                        ins = tensor.transpose(tb[:, j * 128:(j + 1) * 128],
                                               xhat[m % 2][:, k * 128:(k + 1) * 128],
                                               id_bf[:, :])
                    ins.then_inc(s_tp, 1)

            def emit_qk_transposes(j):
                jnn, jm = divmod(j, NM)
                if SCHED[jnn][0] == "v":
                    return
                gidx = P0_GRPS + QK_ORD[jnn] * NM + jm
                tensor.wait_ge(s_bk, j + 1)
                if gidx >= 2:
                    tensor.wait_ge(s_ev, gidx - 1)
                tb = ptr[gidx % 2]
                for t in range(4):
                    ins = tensor.transpose(tb[:, t * 128:(t + 1) * 128],
                                           qring[j % 4][:, t * 128:(t + 1) * 128],
                                           id_bf[:, :])
                ins.then_inc(s_tp, 1)

            for i in range(NCH):
                nn, m = divmod(i, NM)
                tensor.wait_ge(s_w[nn % 2], 16 * (nn // 2 + 1))
                tensor.wait_ge(s_ev, 8 * (m + 1))
                if i >= 3:
                    tensor.wait_ge(s_bk, i - 2)
                bk = pacc[i % 3]
                for k in range(NK):
                    ins = tensor.matmul(bk[:, :],
                                        xhatT[:, k * RTOK + m * 128: k * RTOK + m * 128 + 128],
                                        wbuf[nn % 2][:, k * 512:(k + 1) * 512],
                                        start=(k == 0), stop=(k == NK - 1))
                ins.then_inc(s_mm, 1)
                if i >= 1:
                    emit_qk_transposes(i - 1)
            emit_qk_transposes(NCH - 1)

            tensor.wait_ge(s_ad, 22)
            tensor.transpose(pst[0:32, :], allst[:, 0:32], id_f[:, :]).then_inc(s_tp, 1)

        @block.vector
        def _(vector):
            vs = SelfSem(vector, s_vself)
            ins = vector.memset(neg8[:], -CLIP)
            vs.edge(ins)
            ins = vector.memset(eps_sb[:], EPS)
            vs.edge(ins)
            for m in range(NM):
                vector.wait_ge(s_x2, 16 * (m + 1))
                xm = xld[m % 2]
                vector.reduce_sum(stA[:, m:m + 1], xm[:], axis=AX.X)
                vector.scalar_tensor_tensor(sqj[:], xm[:], 1.0, xm[:], ALU.bypass,
                                            ALU.mult, accum_out=stB[:, m:m + 1]).then_inc(s_vd, 1)
                vector.wait_ge(s_ad, 2 * m + 1)
                vector.scalar_tensor_tensor(ln1st[:, 0:1], stB[:, m:m + 1], 1.0 / D,
                                            stV[:, m:m + 1], ALU.mult, ALU.subtract).then_inc(s_vd, 1)
            ins = vector.nop()
            vs.edge(ins)
            for i in range(NCH):
                nn, m = divmod(i, NM)
                kind, a = SCHED[nn]
                vector.wait_ge(s_mm, i + 1)
                if kind == "v":
                    par = V_ORD[nn] % 2
                    if V_ORD[nn] >= 2 and m == 0:
                        vector.wait_ge(s_vsp[V_ORD[nn] % 2], 64 * (V_ORD[nn] // 2))
                    vector.scalar_tensor_tensor(vring[par][:, m * 512:(m + 1) * 512],
                                                pacc[i % 3][:, :], CLIP, neg8[:],
                                                ALU.min, ALU.max).then_inc(s_bk, 1)
                else:
                    if i >= 4:
                        j = i - 4
                        jnn = j // NM
                        if SCHED[jnn][0] != "v":
                            vector.wait_ge(s_tp, P0_GRPS + QK_ORD[jnn] * NM + (j % NM) + 1)
                    col = (0 if kind == "q" else 64) + a * 8 + m
                    vector.scalar_tensor_tensor(qring[i % 4][:, :], pacc[i % 3][:, :],
                                                CLIP, neg8[:], ALU.min, ALU.max,
                                                accum_out=psums[:, col:col + 1]).then_inc(s_bk, 1)
                j = i - 1
                if j >= 0 and SCHED[j // NM][0] != "v":
                    jnn, jm = divmod(j, NM)
                    jcol = (0 if SCHED[jnn][0] == "q" else 64) + SCHED[jnn][1] * 8 + jm
                    vector.wait_ge(s_bk, j + 1)
                    vector.scalar_tensor_tensor(sqj[:, (jm % 8) * 512:(jm % 8) * 512 + 512], qring[j % 4][:, :], 1.0,
                                                qring[j % 4][:, :], ALU.bypass, ALU.mult,
                                                accum_out=psumsq[:, jcol:jcol + 1])
            j = NCH - 1
            if SCHED[j // NM][0] != "v":
                jnn, jm = divmod(j, NM)
                jcol = (0 if SCHED[jnn][0] == "q" else 64) + SCHED[jnn][1] * 8 + jm
                vector.wait_ge(s_bk, j + 1)
                ins = vector.scalar_tensor_tensor(sqj[:, (jm % 8) * 512:(jm % 8) * 512 + 512], qring[j % 4][:, :], 1.0,
                                                  qring[j % 4][:, :], ALU.bypass, ALU.mult,
                                                  accum_out=psumsq[:, jcol:jcol + 1])
            else:
                ins = vector.nop()
            vs.edge(ins)

            for qi, off in ((0, 0), (1, 64)):
                if qi == 1:
                    vector.wait_ge(s_ad, 16 + 3)
                for m in range(NM):
                    vector.reduce_sum(stA[:, m:m + 1], psums[:, off + m: off + 64: 8], axis=AX.X)
                    ins = vector.reduce_sum(stB[:, m:m + 1], psumsq[:, off + m: off + 64: 8], axis=AX.X)
                ins.then_inc(s_vd, 1)
                vector.wait_ge(s_ad, 16 + 3 * qi + 1)
                vector.scalar_tensor_tensor(stV[:, 0:8], stB[:, 0:8], 1.0 / D,
                                            stV[:, 0:8], ALU.mult, ALU.subtract).then_inc(s_vd, 1)
                vector.wait_ge(s_ad, 16 + 3 * qi + 2)
                rc = slice(0, 8) if qi == 0 else slice(16, 24)
                vector.tensor_tensor(stB[:, 0:8], stM[:, 0:8], allst[:, rc],
                                     ALU.mult).then_inc(s_vd, 1)
            vector.wait_ge(s_tp, STATS_TP + 1)
            vector.tensor_copy(st32[:, :], pst[0:32, :]).then_inc(s_vd, 23 - 22)

        @block.scalar
        def _(scalar):
            asf = SelfSem(scalar, s_aself)
            for m in range(NM):
                scalar.wait_ge(s_vd, 2 * m + 1)
                ins = scalar.mul(stM[:, m:m + 1], stA[:, m:m + 1], 1.0 / D)
                asf.edge(ins)
                scalar.activation(stV[:, m:m + 1], stM[:, m:m + 1], AF.Square).then_inc(s_ad, 1)
                scalar.wait_ge(s_vd, 2 * m + 2)
                ins = scalar.activation(ln1st[:, 1:2], ln1st[:, 0:1], AF.Ln, bias=eps_sb[:, 0:1])
                asf.edge(ins)
                ins = scalar.activation(ln1st[:, 1:2], ln1st[:, 1:2], AF.Exp, scale=-0.5)
                asf.edge(ins)
                ins = scalar.activation(ln1st[:, 0:1], stM[:, m:m + 1], AF.Identity, scale=ln1st[:, 1:2])
                asf.edge(ins)
                ins = scalar.mul(ln1st[:, 0:1], ln1st[:, 0:1], -1.0)
                asf.edge(ins)
                if m >= 1:
                    for kg in range(8):
                        gi = (m - 1) * 8 + kg
                        scalar.wait_ge(s_tp, gi + 1)
                        k0 = kg * 4
                        dst = xhatT.ap().rearrange("p (k t) -> p k t", k=NK, t=RTOK)[
                            :, k0:k0 + 4, (m - 1) * 128:m * 128]
                        src = ptr[gi % 2][:, 0:512].rearrange("p (k t) -> p k t", k=4, t=128)
                        scalar.copy(dst, src).then_inc(s_ev, 1)
                    scalar.wait_ge(s_tp, 8 * m)
                scalar.activation(xhat[m % 2][:], xld[m % 2][:], AF.Identity,
                                  scale=ln1st[:, 1:2], bias=ln1st[:, 0:1]).then_inc(s_ad, 1)
                if m == NM - 1:
                    for kg in range(8):
                        gi = m * 8 + kg
                        scalar.wait_ge(s_tp, gi + 1)
                        k0 = kg * 4
                        dst = xhatT.ap().rearrange("p (k t) -> p k t", k=NK, t=RTOK)[
                            :, k0:k0 + 4, m * 128:(m + 1) * 128]
                        src = ptr[gi % 2][:, 0:512].rearrange("p (k t) -> p k t", k=4, t=128)
                        scalar.copy(dst, src).then_inc(s_ev, 1)
            for i in range(NCH):
                nn, m = divmod(i, NM)
                kind, a = SCHED[nn]
                if kind == "v":
                    continue
                qko = QK_ORD[nn]
                gidx = P0_GRPS + qko * NM + m
                scalar.wait_ge(s_tp, gidx + 1)
                par = qko % 2
                if qko >= 2 and m == 0:
                    scalar.wait_ge(s_tsp[qko % 2], 16 * (qko // 2))
                hl_all = tring[par].ap().rearrange("p (hl t) -> p hl t", hl=4, t=1024)
                dst = hl_all[:, :, m * 128:(m + 1) * 128]
                src = ptr[gidx % 2][:, 0:512].rearrange("p (hl t) -> p hl t", hl=4, t=128)
                scalar.copy(dst, src).then_inc(s_ev, 1)
            for qi in range(2):
                scalar.wait_ge(s_vd, 16 + 3 * qi + 1)
                ins = scalar.mul(stM[:, 0:8], stA[:, 0:8], 1.0 / D)
                asf.edge(ins)
                scalar.activation(stV[:, 0:8], stM[:, 0:8], AF.Square).then_inc(s_ad, 1)
                scalar.wait_ge(s_vd, 16 + 3 * qi + 2)
                rc = slice(0, 8) if qi == 0 else slice(16, 24)
                nbc = slice(8, 16) if qi == 0 else slice(24, 32)
                ins = scalar.activation(allst[:, rc], stV[:, 0:8], AF.Ln, bias=eps_sb[:, 0:1])
                asf.edge(ins)
                scalar.activation(allst[:, rc], allst[:, rc], AF.Exp,
                                  scale=-0.5).then_inc(s_ad, 1)
                scalar.wait_ge(s_vd, 16 + 3 * qi + 3)
                scalar.mul(allst[:, nbc], stB[:, 0:8], -1.0).then_inc(s_ad, 1)


        for g_ in reversed(ctx[ctx_p1_base:ctx_p1_end]):
            g_.__exit__(None, None, None)
        del ctx[ctx_p1_base:ctx_p1_end]

        ctx_p3_base = len(ctx)
        dtab_sb = sb("dtab_sb", [128, NDTAB * 128], F32)
        rrep_k = sb("rrep_k", [128, 2048], F32)
        nbrep_k = sb("nbrep_k", [128, 2048], F32)
        rrep_q = sb("rrep_q", [128, 1024], F32)
        nbrep_q = sb("nbrep_q", [128, 1024], F32)
        krow = sb("krow", [1, 2048], F32)
        knbrow = sb("knbrow", [1, 2048], F32)
        qrow = sb("qrow", [1, 1024], F32)
        qnbrow = sb("qnbrow", [1, 1024], F32)
        ones_f = sb("ones_f", [1, 128], F32)
        ones_b = sb("ones_b", [1, 128], BF16)
        qraw = [sb(f"qraw{j}", [128, 1024], BF16) for j in range(2)]
        kraw = [sb(f"kraw{j}", [128, 2048], BF16) for j in range(2)]
        v_sb = [sb(f"v_sb{j}", [128, 2048], BF16) for j in range(2)]
        scr = [sb(f"scr{j}", [128, 3072], F32) for j in range(2)]
        qhat = [sb(f"qhat{j}", [128, 1024], BF16) for j in range(2)]
        khat = [sb(f"khat{j}", [128, 2048], BF16) for j in range(2)]
        scores = [sb(f"scores{j}", [128, 2048], F32) for j in range(2)]
        probs = [sb(f"probs{j}", [128, 2048], BF16) for j in range(3)]
        probsT = [sb(f"probsT{j}", [128, 2048], BF16) for j in range(3)]
        negmax = [sb(f"negmax{j}", [128, 8], F32) for j in range(2)]
        Zcols = [sb(f"Zcols{j}", [128, 8], F32) for j in range(2)]
        recipZ = sb("recipZ", [128, 8], F32)
        recipZb = sb("recipZb", [128, 8], BF16)
        zb_sb = sb("zb_sb", [8, 128], BF16)
        zrow1 = [sb(f"zrow1{j}", [1, 1024], BF16) for j in range(2)]
        attnraw = [sb(f"attnraw{j}", [128, 1024], F32) for j in range(2)]
        reciprep = [sb(f"reciprep{j}", [128, 1024], F32) for j in range(2)]
        attnT_sb = [sb(f"attnT{j}", [128, 1024], BF16) for j in range(2)]

        psc = [psum(f"psc{j}", [128, 512], F32) for j in range(3)]
        ptp = [psum(f"ptp{j}", [128, 1024], BF16) for j in range(2)]
        av_ps = psum("av_ps", [128, 1024], F32)
        rp_ps = psum("rp_ps", [128, 512], F32)
        ctx_p3_end = len(ctx)

        s3_pl = sem("s3_pl")
        s3_ld = [sem("s3_ld0"), sem("s3_ld1")]
        s3_mm3 = sem("s3_mm3")
        s3_bk3 = sem("s3_bk3")
        s3_tp3 = sem("s3_tp3")
        s3_te = sem("s3_te")
        s3_teA = sem("s3_teA")
        s3_ex = sem("s3_ex")
        s3_mx = sem("s3_mx")
        s3_avs = sem("s3_avs")
        s3_ave = sem("s3_ave")
        s3_vd3 = sem("s3_vd3")
        s3_pd3 = sem("s3_pd3")
        s3_ak3 = sem("s3_ak3")
        s3_zr = sem("s3_zr")
        s3_zc = sem("s3_zc")
        s3_zt = sem("s3_zt")
        s3_ze = sem("s3_ze")
        s3_zd = sem("s3_zd")
        s3_zm = sem("s3_zm")
        s3_rp = sem("s3_rp")
        s3_nrm = sem("s3_nrm")
        s3_sp3 = [sem("s3_sp3a"), sem("s3_sp3b")]
        s_vs3 = sem("s_vs3")
        s_as3 = sem("s_as3")
        s_ps3 = sem("s_ps3")

        @block.sync
        def _(sync):
            sync.wait_ge(s_mm, NCH)
            sync.wait_ge(s_vd, 23)
            sync.wait_ge(s_ad, 22)
            sync.dma_start(out=dtab_sb.ap().rearrange("p (i j) -> p i j", i=NDTAB, j=128),
                           in_=dtab_in.rearrange("i p j -> p i j")).then_inc(s3_pl, 16)
            sync.wait_ge(s_ccs, 1)
            srows = sout.rearrange("(r t u) p -> t u r p", r=2, t=2, u=8)
            for ri in range(2):
                sync.dma_start(out=krow.ap().rearrange("o (u r p) -> o u r p", u=8, r=2, p=128)[:, :, ri, :],
                               in_=srows[0:1, :, ri, :]).then_inc(s3_pl, 16)
                sync.dma_start(out=knbrow.ap().rearrange("o (u r p) -> o u r p", u=8, r=2, p=128)[:, :, ri, :],
                               in_=srows[1:2, :, ri, :]).then_inc(s3_pl, 16)
            qsf = qs_d.rearrange("r p -> (r p)")
            sync.dma_start(out=qrow[0:1, :], in_=qsf[0:1024].rearrange("(o c) -> o c", o=1)).then_inc(s3_pl, 16)
            sync.dma_start(out=qnbrow[0:1, :], in_=qsf[1024:2048].rearrange("(o c) -> o c", o=1)).then_inc(s3_pl, 16)
            sync.wait_ge(s_tsp[0], 16 * 8)
            sync.wait_ge(s_tsp[1], 16 * 8)
            for h in range(H):
                g = h // 8
                hl = h % 8
                if h >= 2:
                    sync.wait_ge(s3_vd3, h - 1)
                    sync.wait_ge(s3_pd3, 2 * (h - 2) + 2)
                    sync.wait_ge(s3_avs, 8 * (h - 1))
                sync.wait_ge(s_ccg[g], 1)
                sync.dma_start(out=qraw[h % 2][:], in_=qT_d[h * 128:(h + 1) * 128, :]).then_inc(s3_ld[h % 2], 16)
                kdst = kraw[h % 2].ap().rearrange("p (u r j) -> p u r j", u=8, r=2, j=128)
                vdst = v_sb[h % 2].ap().rearrange("p (u r d) -> p u r d", u=8, r=2, d=128)
                cfl = cout[g].rearrange("r c -> (r c)")
                for r in range(2):
                    ksrc = cout[g][r * 2048 + hl * 128: r * 2048 + (hl + 1) * 128, :]
                    sync.dma_start(out=kdst[:, :, r, :],
                                   in_=ksrc.rearrange("d (u j) -> d u j", u=8, j=128)).then_inc(s3_ld[h % 2], 16)
                    vbase = r * 2097152 + 1048576 + hl * 131072
                    vsrc = cfl[vbase: vbase + 131072].rearrange("(u p d) -> p u d", u=8, p=128, d=128)
                    sync.dma_start(out=vdst[:, :, r, :], in_=vsrc).then_inc(s3_ld[h % 2], 16)
                if h >= 1:
                    j = h - 1
                    sync.wait_ge(s3_ze, j + 1)
                    if j >= 1:
                        sync.wait_ge(s3_zd, 32 * j)
                    sync.dma_start(out=zb_d[:, :], in_=zb_sb[:, :]).then_inc(s3_zd, 16)
                    sync.wait_ge(s3_zd, 32 * j + 16)
                    if j >= 2:
                        sync.wait_ge(s3_zm, 2 * (j - 1))
                    sync.dma_start(out=zrow1[j % 2][0:1, :],
                                   in_=zb_d.rearrange("r p -> (r p)").rearrange("(o c) -> o c", o=1)).then_inc(s3_zd, 16)
                if h >= 2:
                    j = h - 2
                    sync.wait_ge(s3_nrm, j + 1)
                    sync.dma_start(out=aT_d[j * 128:(j + 1) * 128, :], in_=attnT_sb[j % 2][:]).then_inc(s3_sp3[j % 2], 16)
            j = H - 1
            sync.wait_ge(s3_ze, j + 1)
            sync.wait_ge(s3_zd, 32 * j)
            sync.dma_start(out=zb_d[:, :], in_=zb_sb[:, :]).then_inc(s3_zd, 16)
            sync.wait_ge(s3_zd, 32 * j + 16)
            sync.wait_ge(s3_zm, 2 * (j - 1))
            sync.dma_start(out=zrow1[j % 2][0:1, :],
                           in_=zb_d.rearrange("r p -> (r p)").rearrange("(o c) -> o c", o=1)).then_inc(s3_zd, 16)
            for j in (H - 2, H - 1):
                sync.wait_ge(s3_nrm, j + 1)
                sync.dma_start(out=aT_d[j * 128:(j + 1) * 128, :], in_=attnT_sb[j % 2][:]).then_inc(s3_sp3[j % 2], 16)
            if debug:
                sync.wait_ge(s3_sp3[0], 16 * (H // 2))
                sync.wait_ge(s3_sp3[1], 16 * (H // 2))
                sync.dma_start(out=dbg["d_aT"][:, :], in_=aT_d[:, :]).then_inc(s_dbg, 16)
                sync.wait_ge(s_dbg, 16 * 8)

        @block.tensor
        def _(tensor):
            tensor.wait_ge(s3_pl, 112)
            tensor.wait_ge(s_vd, 24)
            mmi = 0
            for row_sb, width in ((krow, 2048), (knbrow, 2048), (qrow, 1024), (qnbrow, 1024)):
                for c in range(width // 512):
                    if mmi >= 3:
                        tensor.wait_ge(s3_bk3, mmi - 2)
                    tensor.matmul(psc[mmi % 3][:, 0:512], ones_f[0:1, :],
                                  row_sb[0:1, c * 512:(c + 1) * 512],
                                  start=True, stop=True).then_inc(s3_mm3, 1)
                    mmi += 1
            assert mmi == MMB

            for h in range(H):
                if h >= 1:
                    jz = h - 1
                    tensor.wait_ge(s3_zc, jz + 1)
                    tensor.wait_ge(s3_te, 6 * h)
                    tensor.wait_ge(s3_ze, jz)
                    tensor.transpose(ptp[0][0:8, 0:128], recipZb[:, 0:8], id_bf[:, :]).then_inc(s3_zt, 1)
                tensor.wait_ge(s3_ak3, 2 * h + 2)
                if h >= 1:
                    tensor.wait_ge(s3_ave, h)

                def emit_scores(s):
                    sg = 8 * h + s
                    off = 0
                    for c, w in enumerate(CHW[s]):
                        pidx = MMB + CPH * h + preC[s] + c
                        if pidx >= 3:
                            tensor.wait_ge(s3_bk3, pidx - 2)
                        tensor.matmul(psc[pidx % 3][:, 0:w],
                                      qhat[h % 2][:, s * 128:(s + 1) * 128],
                                      khat[h % 2][:, off:off + w],
                                      start=True, stop=True).then_inc(s3_mm3, 1)
                        off += w

                pend = []

                def emit_AV(item):
                    s, t0, nt, gg = item
                    sg = 8 * h + s
                    if gg % 2 == 0:
                        tensor.wait_ge(s3_te, gg // 2 + 1)
                    else:
                        tensor.wait_ge(s3_teA, (gg + 1) // 2)
                    for t in range(t0, t0 + nt):
                        ins = tensor.matmul(av_ps[:, s * 128:(s + 1) * 128],
                                            v_sb[h % 2][:, t * 128:(t + 1) * 128],
                                            probsT[sg % 3][:, t * 128:(t + 1) * 128],
                                            start=(s % 4 == 0 and t == 0),
                                            stop=(s % 4 == 3 and t == LS[s] - 1))
                    if t0 + nt == LS[s]:
                        ins.then_inc(s3_avs, 1)

                def emit_tpav(s):
                    sg = 8 * h + s
                    tensor.wait_ge(s3_ex, sg + 1)
                    if s == 0:
                        tensor.wait_ge(s3_ze, h)
                    for gi, (t0, nt) in enumerate(GROUPS(s)):
                        gg = GPH * h + preG[s] + gi
                        if gg >= 2:
                            if gg % 2 == 0:
                                tensor.wait_ge(s3_te, gg // 2)
                            else:
                                tensor.wait_ge(s3_teA, (gg - 1) // 2)
                        tb = ptp[gg % 2]
                        for t in range(t0, t0 + nt):
                            ins = tensor.transpose(tb[:, (t - t0) * 128:(t - t0 + 1) * 128],
                                                   probs[sg % 3][:, t * 128:(t + 1) * 128],
                                                   id_bf[:, :])
                        ins.then_inc(s3_tp3, 1)
                        pend.append((s, t0, nt, gg))
                        if len(pend) > 1:
                            emit_AV(pend.pop(0))

                emit_scores(0)
                emit_scores(1)
                if h >= 2:
                    jz = h - 2
                    tensor.wait_ge(s3_zd, 32 * jz + 32)
                    for half in range(2):
                        tensor.wait_ge(s3_rp, 2 * jz + half)
                        for s8 in range(4 * half, 4 * half + 4):
                            ins = tensor.matmul(rp_ps[:, (s8 % 4) * 128:(s8 % 4 + 1) * 128], ones_b[0:1, :],
                                                zrow1[jz % 2][0:1, s8 * 128:(s8 + 1) * 128],
                                                start=(s8 % 4 == 0), stop=(s8 % 4 == 3))
                        ins.then_inc(s3_zm, 1)
                for s in range(8):
                    if s + 2 < 8:
                        emit_scores(s + 2)
                    emit_tpav(s)
                while pend:
                    emit_AV(pend.pop(0))
            jz = H - 1
            tensor.wait_ge(s3_zc, jz + 1)
            tensor.wait_ge(s3_te, 6 * H)
            tensor.wait_ge(s3_ze, jz)
            tensor.transpose(ptp[0][0:8, 0:128], recipZb[:, 0:8], id_bf[:, :]).then_inc(s3_zt, 1)
            for jz in (H - 2, H - 1):
                tensor.wait_ge(s3_zd, 32 * jz + 32)
                for half in range(2):
                    tensor.wait_ge(s3_rp, 2 * jz + half)
                    for s8 in range(4 * half, 4 * half + 4):
                        ins = tensor.matmul(rp_ps[:, (s8 % 4) * 128:(s8 % 4 + 1) * 128], ones_b[0:1, :],
                                            zrow1[jz % 2][0:1, s8 * 128:(s8 + 1) * 128],
                                            start=(s8 % 4 == 0), stop=(s8 % 4 == 3))
                    ins.then_inc(s3_zm, 1)

        @block.vector
        def _(vector):
            vs3 = SelfSem(vector, s_vs3)
            vector.memset(ones_f[:, :], 1.0)
            vector.memset(ones_b[:, :], 1.0).then_inc(s_vd, 1)
            dsts = ([(rrep_k, c) for c in range(4)] + [(nbrep_k, c) for c in range(4)]
                    + [(rrep_q, c) for c in range(2)] + [(nbrep_q, c) for c in range(2)])
            for mmi, (dst, c) in enumerate(dsts):
                vector.wait_ge(s3_mm3, mmi + 1)
                vector.tensor_copy(dst.ap()[:, c * 512:(c + 1) * 512], psc[mmi % 3][:, 0:512]).then_inc(s3_bk3, 1)

            def emit_tt(hh):
                if hh == 0:
                    vector.wait_ge(s3_bk3, MMB)
                vector.wait_ge(s3_ld[hh % 2], 80 * (hh // 2 + 1))
                if hh >= 2:
                    vector.wait_ge(s3_ak3, 2 * (hh - 2) + 1)
                vector.tensor_tensor(scr[hh % 2][:, 0:1024], qraw[hh % 2][:], rrep_q[:],
                                     ALU.mult).then_inc(s3_vd3, 1)

            emit_tt(0)
            for h in range(H):
                if h >= 1:
                    j = h - 1
                    vector.wait_ge(s3_avs, 8 * (j + 1))
                    if j >= 2:
                        vector.wait_ge(s3_nrm, j - 1)
                    vector.tensor_copy(attnraw[j % 2][:], av_ps[:, 0:1024]).then_inc(s3_ave, 1)
                    vector.wait_ge(s3_zt, j + 1)
                    if j >= 1:
                        vector.wait_ge(s3_zd, 32 * (j - 1) + 16)
                    vector.tensor_copy(zb_sb[0:8, :], ptp[0][0:8, 0:128]).then_inc(s3_ze, 1)
                def emit_chunks_max(s):
                    sg = 8 * h + s
                    if sg >= 2:
                        vector.wait_ge(s3_ex, sg - 1)
                    off = 0
                    for c, w in enumerate(CHW[s]):
                        pidx = MMB + CPH * h + preC[s] + c
                        vector.wait_ge(s3_mm3, pidx + 1)
                        vector.scalar_tensor_tensor(scores[sg % 2][:, off:off + w],
                                                    dtab_sb[:, TSTART[s] * 128 + off: TSTART[s] * 128 + off + w],
                                                    SLOPES[h], psc[pidx % 3][:, 0:w],
                                                    ALU.mult, ALU.add).then_inc(s3_bk3, 1)
                        off += w
                    if h >= 2:
                        vector.wait_ge(s3_ex, 8 * (h - 2) + s + 1)
                    vector.wait_ge(s3_bk3, MMB + CPH * h + preC[s + 1])
                    vector.reduce_max(negmax[h % 2][:, s:s + 1], scores[sg % 2][:, 0:LS[s] * 128],
                                      axis=AX.X, negate=True).then_inc(s3_mx, 1)

                def emit_psT_even(s):
                    sg = 8 * h + s
                    for gi, (t0, nt) in enumerate(GROUPS(s)):
                        gg = GPH * h + preG[s] + gi
                        if gg % 2 != 0:
                            continue
                        vector.wait_ge(s3_tp3, gg + 1)
                        if sg >= 3:
                            vector.wait_ge(s3_avs, sg - 2)
                        vector.tensor_copy(probsT[sg % 3][:, t0 * 128:(t0 + nt) * 128],
                                           ptp[0][:, 0:nt * 128]).then_inc(s3_te, 1)

                emit_chunks_max(0)
                emit_chunks_max(1)
                def emit_rp(half):
                    jz = h - 2
                    vector.wait_ge(s3_zm, 2 * jz + half + 1)
                    if jz >= 2 and half == 0:
                        vector.wait_ge(s3_nrm, jz - 1)
                    vector.tensor_copy(reciprep[jz % 2][:, half * 512:(half + 1) * 512],
                                       rp_ps[:, 0:512]).then_inc(s3_rp, 1)

                for s in range(8):
                    if s == 0 and h >= 2:
                        emit_rp(0)
                    if s + 2 < 8:
                        emit_chunks_max(s + 2)
                    emit_psT_even(s)
                    if s == 0 and h >= 2:
                        emit_rp(1)
                vector.wait_ge(s3_ex, 8 * h + 8)
                if h >= 1:
                    vector.wait_ge(s3_zc, h)
                vector.reciprocal(recipZ[:, 0:8], Zcols[h % 2][:, 0:8]).then_inc(s3_zr, 1)
                if h + 1 < H:
                    emit_tt(h + 1)
            j = H - 1
            vector.wait_ge(s3_avs, 8 * (j + 1))
            vector.wait_ge(s3_nrm, j - 1)
            vector.tensor_copy(attnraw[j % 2][:], av_ps[:, 0:1024]).then_inc(s3_ave, 1)
            vector.wait_ge(s3_zt, j + 1)
            vector.wait_ge(s3_zd, 32 * (j - 1) + 16)
            vector.tensor_copy(zb_sb[0:8, :], ptp[0][0:8, 0:128]).then_inc(s3_ze, 1)
            for jz in (H - 2, H - 1):
                for half in range(2):
                    vector.wait_ge(s3_zm, 2 * jz + half + 1)
                    if half == 0:
                        vector.wait_ge(s3_nrm, jz - 1)
                    vector.tensor_copy(reciprep[jz % 2][:, half * 512:(half + 1) * 512],
                                       rp_ps[:, 0:512]).then_inc(s3_rp, 1)

        @block.gpsimd
        def _(gpsimd):
            for h in range(H):
                gpsimd.wait_ge(s3_vd3, h + 1)
                gpsimd.tensor_tensor(scr[h % 2][:, 0:1024], scr[h % 2][:, 0:1024],
                                     nbrep_q[:], ALU.add).then_inc(s3_pd3, 1)
                gpsimd.wait_ge(s3_ld[h % 2], 80 * (h // 2 + 1))
                gpsimd.wait_ge(s3_bk3, 8)
                if h >= 2:
                    gpsimd.wait_ge(s3_ak3, 2 * (h - 2) + 2)
                gpsimd.tensor_tensor(scr[h % 2][:, 1024:3072], kraw[h % 2][:], rrep_k[:],
                                     ALU.mult).then_inc(s_ps3, 1)
                gpsimd.wait_ge(s_ps3, h + 1)
                gpsimd.tensor_tensor(scr[h % 2][:, 1024:3072], scr[h % 2][:, 1024:3072],
                                     nbrep_k[:], ALU.add).then_inc(s3_pd3, 1)
                if h >= 2:
                    j = h - 2
                    gpsimd.wait_ge(s3_rp, 2 * (j + 1))
                    gpsimd.wait_ge(s3_ave, j + 1)
                    if j >= 2:
                        gpsimd.wait_ge(s3_sp3[j % 2], 16 * (j // 2))
                    gpsimd.tensor_tensor(attnT_sb[j % 2][:], attnraw[j % 2][:],
                                         reciprep[j % 2][:], ALU.mult).then_inc(s3_nrm, 1)
            for j in (H - 2, H - 1):
                gpsimd.wait_ge(s3_rp, 2 * (j + 1))
                gpsimd.wait_ge(s3_ave, j + 1)
                gpsimd.wait_ge(s3_sp3[j % 2], 16 * (j // 2))
                gpsimd.tensor_tensor(attnT_sb[j % 2][:], attnraw[j % 2][:],
                                     reciprep[j % 2][:], ALU.mult).then_inc(s3_nrm, 1)

        @block.scalar
        def _(scalar):
            as3 = SelfSem(scalar, s_as3)
            for h in range(H):
                scalar.wait_ge(s3_pd3, 2 * h + 1)
                if h >= 2:
                    scalar.wait_ge(s3_mm3, MMB + CPH * (h - 1))
                scalar.activation(qhat[h % 2][:], scr[h % 2][:, 0:1024], AF.Identity,
                                  scale=lnp_sb[:, h:h + 1],
                                  bias=lnp_sb[:, 32 + h:33 + h]).then_inc(s3_ak3, 1)
                scalar.wait_ge(s3_pd3, 2 * h + 2)
                scalar.activation(khat[h % 2][:], scr[h % 2][:, 1024:3072], AF.Identity,
                                  scale=lnp_sb[:, 64 + h:65 + h],
                                  bias=lnp_sb[:, 96 + h:97 + h]).then_inc(s3_ak3, 1)
                def emit_psT_odd(s):
                    sg = 8 * h + s
                    for gi, (t0, nt) in enumerate(GROUPS(s)):
                        gg = GPH * h + preG[s] + gi
                        if gg % 2 != 1:
                            continue
                        scalar.wait_ge(s3_tp3, gg + 1)
                        if sg >= 3:
                            scalar.wait_ge(s3_avs, sg - 2)
                        scalar.copy(probsT[sg % 3][:, t0 * 128:(t0 + nt) * 128],
                                    ptp[1][:, 0:nt * 128]).then_inc(s3_teA, 1)

                for s in range(8):
                    sg = 8 * h + s
                    if s >= 1:
                        emit_psT_odd(s - 1)
                    scalar.wait_ge(s3_mx, sg + 1)
                    if sg >= 3:
                        scalar.wait_ge(s3_tp3, GG_END(sg - 3))
                    if s == 0 and h >= 1:
                        scalar.wait_ge(s3_zr, h)
                    scalar.activation(probs[sg % 3][:, 0:LS[s] * 128],
                                      scores[sg % 2][:, 0:LS[s] * 128], AF.Exp,
                                      bias=negmax[h % 2][:, s:s + 1],
                                      accum_out=Zcols[h % 2][:, s:s + 1]).then_inc(s3_ex, 1)
                emit_psT_odd(7)
                scalar.wait_ge(s3_zr, h + 1)
                if h >= 1:
                    scalar.wait_ge(s3_zt, h)
                scalar.copy(recipZb[:, 0:8], recipZ[:, 0:8]).then_inc(s3_zc, 1)


        for g_ in reversed(ctx[ctx_p3_base:ctx_p3_end]):
            g_.__exit__(None, None, None)
        del ctx[ctx_p3_base:ctx_p3_end]

        NG4 = 2
        NM4 = 4
        proj_bf = sb("proj_bf", [128, NM4 * D], BF16)
        x2_4 = sb("x2_4", [128, NM4 * D], F32)
        bigT = sb("bigT", [128, NK * 512], BF16)
        w4buf = [sb(f"w4buf{j}", [128, NK * 256], BF16) for j in range(2)]
        xld4 = sb("xld4", [128, 2048], F32)
        x3h4 = sb("x3h4", [128, D], BF16)
        sqj4 = x3h4
        at_sb = sb("at_sb", [128, NK * 128], BF16)
        bt_sb = sb("bt_sb", [128, 16 * 256], BF16)
        rt_sb = sb("rt_sb", [128, NK * 8], BF16)
        brow = sb("brow", [1, D], BF16)
        ones4 = sb("ones4", [1, 128], BF16)
        hw4 = sb("hw4", [128, 128], BF16)
        hwT4 = sb("hwT4", [128, NM4 * 128], BF16)
        h4f = sb("h4f", [128, 128], F32)
        j256 = sb("j256", [128, 512], BF16)
        pps = sb("pps", [128, 128], F32)
        r8 = sb("r8", [128, 8], F32)
        e8 = sb("e8", [128, 8], F32)
        nm8 = sb("nm8", [128, 1], F32)
        t8 = sb("t8", [128, 8], F32)
        g8 = sb("g8", [128, 8], F32)
        m1c = sb("m1c", [128, 1], F32)
        m2c = sb("m2c", [128, 1], F32)
        rc8 = sb("rc8", [128, 1], F32)
        comb = sb("comb", [128, 8], F32)
        st4A = sb("st4A", [128, 2], F32)
        st4B = sb("st4B", [128, 2], F32)
        st4M = sb("st4M", [128, 2], F32)
        st4V = sb("st4V", [128, 2], F32)
        ln4st = sb("ln4st", [128, 4], F32)

        pac4 = [psum(f"pac4{j}", [128, 512], F32) for j in range(4)]
        ptr4 = [psum(f"ptr4{j}", [128, 1024], BF16) for j in range(2)]
        prt4 = psum("prt4", [128, 512], F32)
        ph4 = psum("ph4", [128, 512], F32)

        s4_w = [sem("s4_w0"), sem("s4_w1")]
        s4_wp = sem("s4_wp")
        s4_at = sem("s4_at")
        s4_x = sem("s4_x")
        s4_mo = sem("s4_mo")
        s4_bo = sem("s4_bo")
        s4_mf = sem("s4_mf")
        s4_bf = sem("s4_bf")
        s4_mr = sem("s4_mr")
        s4_br = sem("s4_br")
        s4_mh = sem("s4_mh")
        s4_bh = sem("s4_bh")
        s4_tx = sem("s4_tx")
        s4_te = sem("s4_te")
        s4_vd = sem("s4_vd")
        s4_ad = sem("s4_ad")
        s4_vr = sem("s4_vr")
        s4_ar = sem("s4_ar")
        s4_hw = sem("s4_hw")
        s_out = [sem(f"s_out{j}") for j in range(NM4)]
        s_vs4 = sem("s_vs4")
        s_as4 = sem("s_as4")

        CH4 = []
        for G4 in range(NG4):
            for typ in (2, 3, 4):
                for mloc in range(NM4):
                    CH4.append((typ, G4, mloc))

        VD_INC = {2: 4, 3: 2, 4: 3}

        def ch4_base(idx):
            vd = sum(VD_INC[CH4[i][0]] for i in range(idx))
            ad = 2 * idx
            return vd, ad

        def tx_ord(G4, kind, mloc, grp=0):
            base = 36 * G4
            if kind == "x3":
                return base + mloc * 8 + grp
            return base + 32 + mloc

        @block.sync
        def _(sync):
            sync.wait_ge(s3_zm, 2 * H)
            sync.wait_ge(s3_rp, 2 * H)
            sync.wait_ge(s3_zc, H)
            sync.wait_ge(s3_zd, 32 * H)
            sync.wait_ge(s3_nrm, H)
            sync.wait_ge(s3_sp3[0], 16 * (H // 2))
            sync.wait_ge(s3_sp3[1], 16 * (H // 2))
            for G4 in range(NG4):
                if G4 >= 1:
                    sync.wait_ge(s4_mf, 32 * G4)
                sync.dma_start(
                    out=bigT.ap().rearrange("p (k j) -> p k j", k=NK, j=512),
                    in_=aT_d.rearrange("(k d) t -> d k t", k=NK, d=128)[:, :, G4 * 512:(G4 + 1) * 512],
                ).then_inc(s4_at, 16)
                for mloc in range(NM4):
                    m = NM4 * G4 + mloc
                    ci = CH4.index((2, G4, mloc))
                    vb, _ = ch4_base(ci)
                    for hh in range(2):
                        if m + hh >= 1:
                            sync.wait_ge(s4_vd, vb + 2 + hh)
                        sync.dma_start(out=xld4[:],
                                       in_=x_in[m * 128:(m + 1) * 128, hh * 2048:(hh + 1) * 2048]).then_inc(s4_x, 16)
                for mloc in range(NM4):
                    m = NM4 * G4 + mloc
                    ci = CH4.index((4, G4, mloc))
                    vb, ab = ch4_base(ci)
                    sync.wait_ge(s4_vd, vb + 3)
                    sync.dma_start(out=out[m * 128:(m + 1) * 128, :],
                                   in_=x2_4.ap()[:, mloc * D:(mloc + 1) * D]).then_inc(s_out[mloc], 16)
            for mloc in range(NM4):
                sync.wait_ge(s_out[mloc], 16 * NG4)

        @block.gpsimd
        def _(gpsimd):
            gpsimd.wait_ge(s3_nrm, H)
            gpsimd.wait_ge(s3_sp3[0], 16 * (H // 2))
            gpsimd.wait_ge(s3_sp3[1], 16 * (H // 2))
            gpsimd.wait_ge(s3_zm, 2 * H)
            gpsimd.wait_ge(s3_rp, 2 * H)
            gpsimd.wait_ge(s3_zc, H)
            gpsimd.wait_ge(s3_zd, 32 * H)
            gpsimd.dma_start(out=at_sb.ap().rearrange("p (k n) -> p k n", k=NK, n=128),
                             in_=wa_in).then_inc(s4_wp, 16)
            gpsimd.dma_start(out=bt_sb.ap().rearrange("p (k n) -> p k n", k=16, n=256),
                             in_=wb_in).then_inc(s4_wp, 16)
            gpsimd.dma_start(out=rt_sb.ap().rearrange("p (k n) -> p k n", k=NK, n=8),
                             in_=wr_in).then_inc(s4_wp, 16)
            gpsimd.dma_start(out=brow[:, :], in_=bb_in[:, :]).then_inc(s4_wp, 16)
            for G4 in range(NG4):
                for c4 in range(32):
                    widx = 32 * G4 + c4
                    mat = wout_in if c4 < 16 else wbase_in
                    pc2 = c4 if c4 < 16 else c4 - 16
                    nh, hf = pc2 // 2, pc2 % 2
                    if widx >= 2:
                        prev = widx - 2
                        pG, pc = divmod(prev, 32)
                        pcb = pc if pc < 16 else pc - 16
                        need = 32 * pG + 4 * (pcb // 2) + (1 if pcb % 2 == 0 else 4)
                        gpsimd.wait_ge(s4_mo if pc < 16 else s4_mf, need)
                    for ci in range(2):
                        gpsimd.dma_start(
                            out=w4buf[widx % 2].ap().rearrange("p (k n) -> p k n", k=16, n=512)[
                                :, :, ci * 256:(ci + 1) * 256],
                            in_=mat[2 * nh + ci][:, 16 * hf:16 * hf + 16, :],
                        ).then_inc(s4_w[widx % 2], 16)

        @block.tensor
        def _(tensor):
            tensor.wait_ge(s3_nrm, H)
            tensor.wait_ge(s3_rp, 2 * H)
            tensor.wait_ge(s3_zc, H)
            tensor.wait_ge(s3_zm, 2 * H)
            tensor.wait_ge(s3_zd, 32 * H)
            tensor.wait_ge(s3_sp3[0], 16 * 16)
            tensor.wait_ge(s3_sp3[1], 16 * 16)
            tensor.wait_ge(s4_wp, 64)
            for G4 in range(NG4):
                tensor.wait_ge(s4_at, 16 * (G4 + 1))
                for nh in range(8):
                    for hf in range(2):
                        widx = 32 * G4 + 2 * nh + hf
                        tensor.wait_ge(s4_w[widx % 2], 32 * (widx // 2 + 1))
                        for mloc in range(NM4):
                            if hf == 0:
                                if nh >= 1:
                                    tensor.wait_ge(s4_bo, 64 * G4 + 4 * (2 * nh - 1) + mloc + 1)
                                elif G4 >= 1:
                                    tensor.wait_ge(s4_bf, 64 * (G4 - 1) + 60 + mloc + 1)
                            bk = pac4[(4 * nh + mloc) % 4]
                            for k in range(16):
                                kk = 16 * hf + k
                                ins = tensor.matmul(bk[:, 0:512],
                                                    bigT[:, kk * 512 + mloc * 128: kk * 512 + mloc * 128 + 128],
                                                    w4buf[widx % 2][:, k * 512:(k + 1) * 512],
                                                    start=(hf == 0 and k == 0), stop=(hf == 1 and k == 15))
                            if hf == 1:
                                ins.then_inc(s4_mo, 1)
                for mloc in range(NM4):
                    ci = CH4.index((3, G4, mloc))
                    vb, ab = ch4_base(ci)
                    tensor.wait_ge(s4_ad, ab + 2)
                    for grp in range(8):
                        gi = tx_ord(G4, "x3", mloc, grp)
                        if gi >= 2:
                            tensor.wait_ge(s4_te, gi - 1)
                        tb = ptr4[gi % 2]
                        for j in range(4):
                            k = grp * 4 + j
                            ins = tensor.transpose(tb[:, j * 128:(j + 1) * 128],
                                                   x3h4[:, k * 128:(k + 1) * 128],
                                                   id_bf[:, :])
                        ins.then_inc(s4_tx, 1)
                for mloc in range(NM4):
                    im = NM4 * G4 + mloc
                    if im >= 1:
                        tensor.wait_ge(s4_br, im)
                    tensor.wait_ge(s4_te, tx_ord(G4, "x3", mloc, 7) + 1)
                    for k in range(NK):
                        ins = tensor.matmul(prt4[:, 0:8],
                                            bigT[:, k * 512 + mloc * 128: k * 512 + mloc * 128 + 128],
                                            rt_sb[:, k * 8:(k + 1) * 8],
                                            start=(k == 0), stop=(k == NK - 1))
                    ins.then_inc(s4_mr, 1)
                    if im >= 1:
                        tensor.wait_ge(s4_bh, im)
                    for k in range(NK):
                        ins = tensor.matmul(ph4[:, 0:128],
                                            bigT[:, k * 512 + mloc * 128: k * 512 + mloc * 128 + 128],
                                            at_sb[:, k * 128:(k + 1) * 128],
                                            start=(k == 0), stop=(k == NK - 1))
                    ins.then_inc(s4_mh, 1)
                    gi = tx_ord(G4, "hw", mloc)
                    tensor.wait_ge(s4_hw, im + 1)
                    tensor.wait_ge(s4_te, gi - 1)
                    tensor.transpose(ptr4[gi % 2][:, 0:128], hw4[:, 0:128],
                                     id_bf[:, :]).then_inc(s4_tx, 1)
                for nh in range(8):
                    for hf in range(2):
                        widx = 32 * G4 + 16 + 2 * nh + hf
                        tensor.wait_ge(s4_w[widx % 2], 32 * (widx // 2 + 1))
                        for mloc in range(NM4):
                            if hf == 0 and nh >= 1:
                                tensor.wait_ge(s4_bf, 64 * G4 + 4 * (2 * nh - 1) + mloc + 1)
                            tensor.wait_ge(s4_te, tx_ord(G4, "hw", mloc) + 1)
                            bk = pac4[(4 * nh + mloc) % 4]
                            for k in range(16):
                                kk = 16 * hf + k
                                tensor.matmul(bk[:, 0:512],
                                              bigT[:, kk * 512 + mloc * 128: kk * 512 + mloc * 128 + 128],
                                              w4buf[widx % 2][:, k * 512:(k + 1) * 512],
                                              start=(hf == 0 and k == 0), stop=False)
                            if hf == 1:
                                tensor.matmul(bk[:, 0:512],
                                              hwT4[:, mloc * 128:(mloc + 1) * 128],
                                              bt_sb[:, nh * 512:(nh + 1) * 512],
                                              start=False, stop=False)
                                ins = tensor.matmul(bk[:, 0:512], ones4[0:1, :],
                                                    brow[0:1, nh * 512:(nh + 1) * 512],
                                                    start=False, stop=True)
                                ins.then_inc(s4_mf, 1)

        @block.vector
        def _(vector):
            vs4 = SelfSem(vector, s_vs4)
            vector.wait_ge(s3_nrm, H)
            vector.wait_ge(s3_rp, 2 * H)
            vector.wait_ge(s3_zc, H)
            vector.wait_ge(s3_zm, 2 * H)
            vector.wait_ge(s3_zd, 32 * H)
            vector.wait_ge(s3_sp3[0], 16 * 16)
            vector.wait_ge(s3_sp3[1], 16 * 16)
            ins = vector.memset(ones4[:, :], 1.0)
            vs4.edge(ins)

            def evacs(G4, which):
                sem_m = s4_mo if which == "o" else s4_mf
                sem_b = s4_bo if which == "o" else s4_bf
                pend = []
                for nn in range(16):
                    for mloc in range(NM4):
                        oi = 64 * G4 + 4 * nn + mloc
                        vector.wait_ge(sem_m, 32 * G4 + 4 * (nn // 2) + mloc + 1)
                        if which == "f":
                            ci2 = CH4.index((2, G4, mloc))
                            vb2, _ = ch4_base(ci2)
                            vector.wait_ge(s4_vd, vb2 + 4)
                        col = mloc * 16 + nn
                        dst = proj_bf.ap()[:, mloc * D + nn * 256: mloc * D + nn * 256 + 256]
                        bnk = pac4[(4 * (nn // 2) + mloc) % 4][:, (nn % 2) * 256:(nn % 2) * 256 + 256]
                        vector.tensor_scalar(dst, bnk, 1.0, None,
                                             ALU.mult, ALU.add,
                                             accum_out=pps[:, col:col + 1]).then_inc(sem_b, 1)
                        pend.append((oi, dst, col))
                        if len(pend) >= 2:
                            poi, pdst, pcol = pend.pop(0)
                            vector.wait_ge(sem_b, poi - 64 * G4 + 64 * G4 + 1)
                            vector.scalar_tensor_tensor(j256[:, (poi % 2) * 256:(poi % 2) * 256 + 256],
                                                        pdst, 1.0, pdst, ALU.bypass, ALU.mult,
                                                        accum_out=pps[:, 64 + pcol:65 + pcol])
                for poi, pdst, pcol in pend:
                    vector.wait_ge(sem_b, poi - 64 * G4 + 64 * G4 + 1)
                    vector.scalar_tensor_tensor(j256[:, (poi % 2) * 256:(poi % 2) * 256 + 256],
                                                pdst, 1.0, pdst, ALU.bypass, ALU.mult,
                                                accum_out=pps[:, 64 + pcol:65 + pcol])
                ins = vector.nop()
                vs4.edge(ins)

            def ln_chain_dve(typ, G4, mloc):
                ci = CH4.index((typ, G4, mloc))
                vb, ab = ch4_base(ci)
                pi = ci % 2
                if typ == 3:
                    rowf = x2_4.ap()[:, mloc * D:(mloc + 1) * D]
                    vb2, _ = ch4_base(CH4.index((2, G4, mloc)))
                    vector.wait_ge(s4_vd, vb2 + 4)
                    if mloc >= 1:
                        vector.wait_ge(s4_tx, tx_ord(G4, "x3", mloc - 1, 7) + 1)
                    elif G4 >= 1:
                        vector.wait_ge(s4_tx, tx_ord(G4 - 1, "x3", NM4 - 1, 7) + 1)
                    vector.reduce_sum(st4A[:, pi:pi + 1], rowf, axis=AX.X)
                    vector.scalar_tensor_tensor(sqj4[:], rowf, 1.0, rowf, ALU.bypass,
                                                ALU.mult, accum_out=st4B[:, pi:pi + 1]).then_inc(s4_vd, 1)
                else:
                    vector.reduce_sum(st4A[:, pi:pi + 1], pps[:, mloc * 16: mloc * 16 + 16], axis=AX.X)
                    vector.reduce_sum(st4B[:, pi:pi + 1], pps[:, 64 + mloc * 16: 64 + mloc * 16 + 16],
                                      axis=AX.X).then_inc(s4_vd, 1)
                vector.wait_ge(s4_ad, ab + 1)
                vector.scalar_tensor_tensor(ln4st[:, 2 * pi:2 * pi + 1], st4B[:, pi:pi + 1], 1.0 / D,
                                            st4V[:, pi:pi + 1], ALU.mult, ALU.subtract).then_inc(s4_vd, 1)
                if typ == 2:
                    vector.wait_ge(s4_ad, ab + 2)
                    m = NM4 * G4 + mloc
                    for hh in range(2):
                        vector.wait_ge(s4_x, 32 * m + 16 * (hh + 1))
                        rowx = x2_4.ap()[:, mloc * D + hh * 2048: mloc * D + (hh + 1) * 2048]
                        vector.tensor_tensor(rowx, rowx, xld4[:], ALU.add).then_inc(s4_vd, 1)
                elif typ == 4:
                    vector.wait_ge(s4_ad, ab + 2)
                    rowx = x2_4.ap()[:, mloc * D:(mloc + 1) * D]
                    vector.tensor_tensor(rowx, rowx,
                                         proj_bf.ap()[:, mloc * D:(mloc + 1) * D], ALU.add).then_inc(s4_vd, 1)

            for G4 in range(NG4):
                if G4 >= 1:
                    vb_last, _ = ch4_base(CH4.index((4, G4 - 1, NM4 - 1)))
                    vector.wait_ge(s4_vd, vb_last + 3)
                evacs(G4, "o")
                for mloc in range(NM4):
                    ln_chain_dve(2, G4, mloc)
                for mloc in range(NM4):
                    ln_chain_dve(3, G4, mloc)
                for mloc in range(NM4):
                    im = NM4 * G4 + mloc
                    vector.wait_ge(s4_mr, im + 1)
                    if im >= 1:
                        vector.wait_ge(s4_ar, im)
                    vector.tensor_copy(r8[:, 0:8], prt4[:, 0:8]).then_inc(s4_br, 1)
                    vector.wait_ge(s4_br, im + 1)
                    ins = vector.reduce_max(nm8[:, 0:1], r8[:, 0:8], axis=AX.X, negate=True)
                    ins.then_inc(s4_vr, 1)
                    vector.wait_ge(s4_ar, im + 1)
                    ins = vector.reduce_max(m1c[:, 0:1], e8[:, 0:8], axis=AX.X)
                    vs4.edge(ins)
                    ins = vector.tensor_scalar(t8[:, 0:8], e8[:, 0:8], m1c[:, 0:1], None, ALU.is_ge)
                    vs4.edge(ins)
                    ins = vector.scalar_tensor_tensor(t8[:, 0:8], t8[:, 0:8], -1e30, e8[:, 0:8],
                                                      ALU.mult, ALU.add)
                    vs4.edge(ins)
                    ins = vector.reduce_max(m2c[:, 0:1], t8[:, 0:8], axis=AX.X)
                    vs4.edge(ins)
                    ins = vector.tensor_scalar(g8[:, 0:8], e8[:, 0:8], m2c[:, 0:1], None, ALU.is_ge)
                    vs4.edge(ins)
                    ins = vector.tensor_tensor(g8[:, 0:8], g8[:, 0:8], e8[:, 0:8], ALU.mult)
                    vs4.edge(ins)
                    ins = vector.tensor_tensor(m1c[:, 0:1], m1c[:, 0:1], m2c[:, 0:1], ALU.add)
                    vs4.edge(ins)
                    ins = vector.reciprocal(rc8[:, 0:1], m1c[:, 0:1])
                    vs4.edge(ins)
                    ins = vector.tensor_scalar(comb[:, 0:8], g8[:, 0:8], rc8[:, 0:1], 2.0,
                                               ALU.mult, ALU.mult)
                    vs4.edge(ins)
                    vector.wait_ge(s4_mh, im + 1)
                    vector.tensor_copy(h4f[:, 0:128], ph4[:, 0:128]).then_inc(s4_bh, 1)
                    vector.wait_ge(s4_bh, im + 1)
                    if im >= 1:
                        pg, pm = divmod(im - 1, NM4)
                        vector.wait_ge(s4_tx, tx_ord(pg, "hw", pm) + 1)
                    for e in range(8):
                        ins = vector.tensor_single_scalar(hw4[:, e * 16:(e + 1) * 16],
                                                          h4f[:, e * 16:(e + 1) * 16],
                                                          comb[:, e:e + 1], ALU.mult)
                    ins.then_inc(s4_hw, 1)
                evacs(G4, "f")
                for mloc in range(NM4):
                    ln_chain_dve(4, G4, mloc)

        @block.scalar
        def _(scalar):
            as4 = SelfSem(scalar, s_as4)
            scalar.wait_ge(s3_nrm, H)
            scalar.wait_ge(s3_rp, 2 * H)
            scalar.wait_ge(s3_zc, H)
            scalar.wait_ge(s3_zm, 2 * H)
            scalar.wait_ge(s3_zd, 32 * H)
            scalar.wait_ge(s3_sp3[0], 16 * 16)
            scalar.wait_ge(s3_sp3[1], 16 * 16)

            def ln_chain_act(typ, G4, mloc):
                ci = CH4.index((typ, G4, mloc))
                vb, ab = ch4_base(ci)
                pi = ci % 2
                scalar.wait_ge(s4_vd, vb + 1)
                ins = scalar.mul(st4M[:, pi:pi + 1], st4A[:, pi:pi + 1], 1.0 / D)
                as4.edge(ins)
                scalar.activation(st4V[:, pi:pi + 1], st4M[:, pi:pi + 1], AF.Square).then_inc(s4_ad, 1)
                scalar.wait_ge(s4_vd, vb + 2)
                ins = scalar.activation(ln4st[:, 2 * pi + 1:2 * pi + 2], ln4st[:, 2 * pi:2 * pi + 1],
                                        AF.Ln, bias=eps_sb[:, 0:1])
                as4.edge(ins)
                ins = scalar.activation(ln4st[:, 2 * pi + 1:2 * pi + 2], ln4st[:, 2 * pi + 1:2 * pi + 2],
                                        AF.Exp, scale=-0.5)
                as4.edge(ins)
                ins = scalar.activation(ln4st[:, 2 * pi:2 * pi + 1], st4M[:, pi:pi + 1],
                                        AF.Identity, scale=ln4st[:, 2 * pi + 1:2 * pi + 2])
                as4.edge(ins)
                ins = scalar.mul(ln4st[:, 2 * pi:2 * pi + 1], ln4st[:, 2 * pi:2 * pi + 1], -1.0)
                as4.edge(ins)
                sc = ln4st[:, 2 * pi + 1:2 * pi + 2]
                bi = ln4st[:, 2 * pi:2 * pi + 1]
                if typ == 2:
                    if G4 >= 1:
                        scalar.wait_ge(s_out[mloc], 16 * G4)
                    scalar.activation(x2_4.ap()[:, mloc * D:(mloc + 1) * D],
                                      proj_bf.ap()[:, mloc * D:(mloc + 1) * D],
                                      AF.Identity, scale=sc, bias=bi).then_inc(s4_ad, 1)
                elif typ == 3:
                    pci = ci - 1
                    if CH4[pci][0] == 3:
                        scalar.wait_ge(s4_tx, tx_ord(G4, "x3", mloc - 1, 7) + 1)
                    elif G4 >= 1:
                        scalar.wait_ge(s4_tx, tx_ord(G4 - 1, "x3", NM4 - 1, 7) + 1)
                    scalar.activation(x3h4[:], x2_4.ap()[:, mloc * D:(mloc + 1) * D],
                                      AF.Identity, scale=sc, bias=bi).then_inc(s4_ad, 1)
                else:
                    scalar.activation(proj_bf.ap()[:, mloc * D:(mloc + 1) * D],
                                      proj_bf.ap()[:, mloc * D:(mloc + 1) * D],
                                      AF.Identity, scale=sc, bias=bi).then_inc(s4_ad, 1)

            for G4 in range(NG4):
                for mloc in range(NM4):
                    ln_chain_act(2, G4, mloc)
                for mloc in range(NM4):
                    ln_chain_act(3, G4, mloc)
                    for grp in range(8):
                        gi = tx_ord(G4, "x3", mloc, grp)
                        scalar.wait_ge(s4_tx, gi + 1)
                        k0 = grp * 4
                        dst = bigT.ap().rearrange("p (k j) -> p k j", k=NK, j=512)[
                            :, k0:k0 + 4, mloc * 128:(mloc + 1) * 128]
                        srcp = ptr4[gi % 2][:, 0:512].rearrange("p (k j) -> p k j", k=4, j=128)
                        scalar.copy(dst, srcp).then_inc(s4_te, 1)
                for mloc in range(NM4):
                    im = NM4 * G4 + mloc
                    scalar.wait_ge(s4_vr, im + 1)
                    scalar.activation(e8[:, 0:8], r8[:, 0:8], AF.Exp,
                                      bias=nm8[:, 0:1]).then_inc(s4_ar, 1)
                    gi = tx_ord(G4, "hw", mloc)
                    scalar.wait_ge(s4_tx, gi + 1)
                    if G4 >= 1:
                        scalar.wait_ge(s4_mf, 32 * G4)
                    scalar.copy(hwT4[:, mloc * 128:(mloc + 1) * 128],
                                ptr4[gi % 2][:, 0:128]).then_inc(s4_te, 1)
                for mloc in range(NM4):
                    ln_chain_act(4, G4, mloc)

    for g in reversed(ctx):
        g.__exit__(None, None, None)
    return nc, dbg



def prep_inputs(inputs):
    wq = np.ascontiguousarray(inputs["L0_attn_Wqkv"].T)
    tiles = wq.reshape(NK, 128, 24, 512).transpose(2, 0, 1, 3)
    order = []
    for kind, a in SCHED:
        base = {"q": 0, "k": 8, "v": 16}[kind]
        order.append(base + a)
    wqkv_t = np.ascontiguousarray(tiles[order].transpose(0, 2, 1, 3))

    lnp_arr = np.zeros((128, 4 * 32), np.float32)
    lnp_arr[:, 0:32] = inputs["L0_attn_q_ln_w"].reshape(32, 128).T * SCALE
    lnp_arr[:, 32:64] = inputs["L0_attn_q_ln_b"].reshape(32, 128).T * SCALE
    lnp_arr[:, 64:96] = inputs["L0_attn_k_ln_w"].reshape(32, 128).T
    lnp_arr[:, 96:128] = inputs["L0_attn_k_ln_b"].reshape(32, 128).T

    ident = np.eye(128, dtype=ml_dtypes.bfloat16)
    identf = np.eye(128, dtype=np.float32)

    wout_t = np.ascontiguousarray(
        inputs["L0_attn_Wout"].T.reshape(NK, 128, 16, 256).transpose(2, 1, 0, 3))
    wbase_t = np.ascontiguousarray(
        inputs["L0_ffn_W_base"].T.reshape(NK, 128, 16, 256).transpose(2, 1, 0, 3))
    wa_t = np.ascontiguousarray(
        np.asarray(inputs["L0_ffn_A"]).reshape(128, D).T.reshape(NK, 128, 128).transpose(1, 0, 2))
    wb_t = np.ascontiguousarray(
        np.asarray(inputs["L0_ffn_B"]).transpose(0, 2, 1).reshape(128, D).reshape(128, 16, 256))
    wr_t = np.ascontiguousarray(
        np.asarray(inputs["L0_ffn_W_router"]).T.reshape(NK, 128, 8).transpose(1, 0, 2))
    bb = np.ascontiguousarray(np.asarray(inputs["L0_ffn_b_base"]).reshape(1, D))
    return dict(wqkv=wqkv_t, lnp=lnp_arr, ident=ident, identf=identf,
                wout=wout_t, wbase=wbase_t, wa=wa_t, wb=wb_t, wr=wr_t, bb=bb)


def core_x(inputs, c):
    x = np.asarray(inputs["L0_attn_hidden_states"])
    seq, par = c // 2, c % 2
    return np.ascontiguousarray(x[seq].reshape(16, 128, D)[par::2].reshape(RTOK, D))


def core_dtab(c):
    p = c % 2
    out = np.full((NDTAB, 128, 128), -1e33, np.float32)
    jj = np.arange(128)[None, :]
    ii = np.arange(128)[:, None]
    idx = 0
    for s in range(8):
        Q = 2 * s + p
        for t in range(LS[s]):
            Dv = (t * 128 + jj) - (Q * 128 + ii)
            out[idx] = np.where(Dv <= 0, Dv.astype(np.float32), np.float32(-1e33))
            idx += 1
    assert idx == NDTAB
    return out



_CACHE = {}


def kernel(**inputs):
    inputs = {k: np.asarray(v) for k, v in inputs.items()}
    if "nc" not in _CACHE:
        nc, _ = build(debug=False)
        _CACHE["nc"] = nc
    nc = _CACHE["nc"]
    shared = prep_inputs(inputs)
    in_maps = []
    for c in range(NCORE):
        m = dict(shared)
        m["x"] = core_x(inputs, c)
        m["dtab"] = core_dtab(c)
        in_maps.append(m)
    res = run_bass_kernel_spmd(nc, in_maps, core_ids=list(range(NCORE)))
    out = np.empty((B, S, D), np.float32)
    view = out.reshape(B, 16, 128, D)
    for c in range(NCORE):
        seq, par = c // 2, c % 2
        view[seq, par::2] = np.asarray(res.results[c]["out"]).reshape(NM, 128, D)
    return out
